# version 03
# baseline (speedup 1.0000x reference)
import contextlib

import numpy as np
import ml_dtypes

import concourse.bass as bass
import concourse.mybir as mybir
from concourse.bass_utils import run_bass_kernel_spmd

F32 = mybir.dt.float32
BF16 = mybir.dt.bfloat16
AF = mybir.ActivationFunctionType
ALU = mybir.AluOpType
AXL = mybir.AxisListType.X

D = 2048
H = 32
M = 4
S = 64
B = 4
L = 2048
NB = L // S
NL = NB * M
TOWN = 1024
NTILE = 16
NOWN = 8
INTER = 8192
BIG = 60.0
DUM = -25.0

bf16 = ml_dtypes.bfloat16

DEBUG_TAPS = ()
STOP_AFTER = None
SKIP_BD = False
P6_NO_LOCAL = False
P6_NO_POOL = False


def _had(n):
    h = np.array([[1.0]])
    while h.shape[0] < n:
        h = np.block([[h, h], [h, -h]])
    return h


def _host_consts(half):
    c = {}
    c["c_ident"] = np.eye(128).astype(np.float32).astype(bf16)
    c["c_identf"] = np.eye(128).astype(np.float32)
    c["c_h128"] = _had(128).astype(np.float32).astype(bf16)
    h64 = _had(64)
    bd = np.zeros((128, 128))
    bd[:64, :64] = h64
    bd[64:, 64:] = h64
    c["c_h64bd"] = bd.astype(np.float32).astype(bf16)
    pos = (np.arange(L) + half * TOWN) % L
    inv = 1.0 / 10000.0 ** (np.arange(0, 64, 2, dtype=np.float64) / 64)
    fr = pos[None, :] * inv[np.arange(128) % 32][:, None]
    c["c_cosq"] = (0.125 * np.cos(fr[:, :TOWN])).astype(np.float32)
    c["c_sinq"] = (0.125 * np.sin(fr[:, :TOWN])).astype(np.float32)
    c["c_cos"] = np.cos(fr).astype(np.float32)
    c["c_sin"] = np.sin(fr).astype(np.float32)
    u2 = np.zeros((67, 128)); w2 = np.zeros((67, 128))
    p = np.arange(128)
    u2[0] = 1.0;        w2[0] = -BIG
    u2[1] = (p < 64);   w2[1] = BIG * (p < 64)
    u2[2] = (p >= 64);  w2[2] = BIG * (p >= 64)
    for r in range(1, 64):
        u2[2 + r] = (p % 64) < r
        w2[2 + r] = -BIG * ((p % 64) == r)
    c["c_lau"] = u2.astype(np.float32).astype(bf16)
    c["c_law"] = w2.astype(np.float32).astype(bf16)
    tblk = (np.arange(NB) + half * 16) % NB
    blk_l = tblk[np.arange(NL) // M]
    ula = np.zeros((32, NL)); wla = np.zeros((32, NL))
    for r in range(32):
        ula[r] = (blk_l < r)
        wla[r] = -8.0 * BIG * (blk_l == r)
    c["c_lat_u"] = ula.astype(np.float32).astype(bf16)
    c["c_lat_w"] = wla.astype(np.float32).astype(bf16)
    blk_q = tblk[np.arange(TOWN) // S]
    ug = np.zeros((33, TOWN)); wg = np.zeros((33, NL + 1))
    for r in range(NB):
        ug[r] = (blk_q == r)
        wg[r, :NL] = -BIG * (blk_l >= r)
    ug[32] = 1.0
    wg[32, NL] = DUM
    c["c_glo_u"] = ug.astype(np.float32).astype(bf16)
    c["c_glo_w"] = wg.astype(np.float32).astype(bf16)
    c["c_ones"] = np.ones((1, 512), dtype=np.float32).astype(bf16)
    bd1 = np.zeros((128, 2)); bd1[:64, 0] = 1.0; bd1[64:, 1] = 1.0
    c["c_bdones"] = bd1.astype(np.float32).astype(bf16)
    rp = np.zeros((2, 128)); rp[0, :64] = 1.0; rp[1, 64:] = 1.0
    c["c_repl2"] = rp.astype(np.float32)
    gr = np.zeros((32, 16 * 128))
    for ch in range(16):
        gr[2 * ch, ch * 128: ch * 128 + 64] = 1.0
        gr[2 * ch + 1, ch * 128 + 64: ch * 128 + 128] = 1.0
    c["c_gaterep"] = gr.astype(np.float32)
    return c


class Build:
    def __init__(self):
        self.nc = bass.Bass("TRN2", target_bir_lowering=False)
        self.st = contextlib.ExitStack()
        self.cnt = {}
        self.sem = {}
        for name in ("s_gld", "s_g2", "s_hld", "s_h2", "s_mm", "s_dv", "s_ac",
                     "s_st", "s_s2"):
            self.sem[name] = self.st.enter_context(self.nc.semaphore(name))
            self.cnt[name] = 0
        self._mem = {}
        self.blk = None

    def sb(self, name, shape, dtype):
        return self.st.enter_context(self.nc.sbuf_tensor(name, shape, dtype))

    def init_arena(self, nbytes):
        self.arena = self.st.enter_context(
            self.nc.sbuf_tensor("arena", [128, nbytes], mybir.dt.uint8))
        self.arena_base = self.nc.lookup_mloc(self.arena).addr
        self.free_list = [(0, nbytes)]
        self.blocks = {}

    def sb_tmp(self, name, shape, dtype):
        per_part = 1
        for s in shape[1:]:
            per_part *= s
        size = (per_part * mybir.dt.size(dtype) + 31) // 32 * 32
        for i, (off, sz) in enumerate(self.free_list):
            if sz >= size:
                if sz == size:
                    self.free_list.pop(i)
                else:
                    self.free_list[i] = (off + size, sz - size)
                break
        else:
            raise MemoryError(f"arena OOM for {name} ({size} B/partition); "
                              f"free={self.free_list}")
        h = self.nc.virtual_sbuf_tensor(name, shape, dtype)
        self.nc._tensor(h.name, h.shape, h.dtype, type="SB")
        mloc = self.nc.lookup_mloc(h)
        mloc.allocated = True
        mloc.addr = self.arena_base + off
        self.blocks[name] = (off, size)
        return h

    def free_tmp(self, *names):
        for name in names:
            off, size = self.blocks.pop(name)
            self.free_list.append((off, size))
        self.free_list.sort()
        out = []
        for off, sz in self.free_list:
            if out and out[-1][0] + out[-1][1] == off:
                out[-1] = (out[-1][0], out[-1][1] + sz)
            else:
                out.append((off, sz))
        self.free_list = [tuple(x) for x in out]

    def psum(self, name, shape):
        return self.st.enter_context(self.nc.psum_tensor(name, shape, F32))

    def inc(self, instr, name, k=1):
        instr.then_inc(self.sem[name], k)
        self.cnt[name] += k
        return self.cnt[name]

    def wait_all(self, eng, names=None):
        for n in (names or list(self.sem)):
            if self.cnt[n]:
                eng.wait_ge(self.sem[n], self.cnt[n])

    def barrier(self):
        blk = self.blk
        snap = dict(self.cnt)
        for engf in (blk.tensor, blk.vector, blk.scalar, blk.sync, blk.gpsimd):
            def f(e, snap=snap):
                for n, v in snap.items():
                    if v:
                        e.wait_ge(self.sem[n], v)
            engf(f)

    def tap(self, name, src_ap, shape, dtype):
        if name not in DEBUG_TAPS:
            return
        t = self.nc.dram_tensor("tap_" + name, shape, dtype,
                                kind="ExternalOutput").ap()
        self.barrier()

        def f(sy):
            self.inc(sy.dma_start(out=t, in_=src_ap), "s_st", 16)
        self.blk.sync(f)
        self.barrier()


def _tp(lhs_base, out_base=0):
    if lhs_base == 0 and out_base == 0:
        return None
    return (lhs_base, out_base)


def build_program():
    b = Build()
    nc = b.nc

    def din(name, shape, dtype=F32):
        return nc.dram_tensor(name, shape, dtype, kind="ExternalInput").ap()

    x_in = din("x", [L, D])
    w_qkv = din("w_qkv", [D, 3 * D])
    pool_q = din("pool_q", [M, D])
    w_lm_qkv = din("w_lm_qkv", [D, 3 * D])
    w_lm_out = din("w_lm_out", [D, D])
    w_l2t_q = din("w_l2t_q", [D, D])
    w_l2t_out = din("w_l2t_out", [D, D])
    w_lm_kv = din("w_lm_kv", [D, 2 * D])
    w_out_i = din("w_out", [D, D])
    w_gate_i = din("w_gate", [D, H])
    b_qkv = din("b_qkv", [3 * D])
    b_lm_qkv = din("b_lm_qkv", [3 * D])
    b_lm_out = din("b_lm_out", [D])
    b_l2t_q = din("b_l2t_q", [D])
    b_l2t_out = din("b_l2t_out", [D])
    b_lm_kv = din("b_lm_kv", [2 * D])
    b_out = din("b_out", [D])
    b_gate = din("b_gate", [H])
    su_ug = din("su_ug", [D])
    su_dp_p = din("su_dp_p", [128, 64])
    ws_up_p = din("ws_up_p", [128, 64])
    ws_gt_p = din("ws_gt_p", [128, 64])
    ws_dp = din("ws_dp", [D])
    wu_in = din("wu", [INTER, D])
    wg_in = din("wg", [INTER, D])
    wd_in = din("wd", [D, INTER])
    c_ident = din("c_ident", [128, 128], BF16)
    c_identf = din("c_identf", [128, 128])
    c_h128 = din("c_h128", [128, 128], BF16)
    c_h64bd = din("c_h64bd", [128, 128], BF16)
    c_cos = din("c_cos", [128, L])
    c_sin = din("c_sin", [128, L])
    c_cosq = din("c_cosq", [128, TOWN])
    c_sinq = din("c_sinq", [128, TOWN])
    c_lau = din("c_lau", [67, 128], BF16)
    c_law = din("c_law", [67, 128], BF16)
    c_lat_u = din("c_lat_u", [32, NL], BF16)
    c_lat_w = din("c_lat_w", [32, NL], BF16)
    c_glo_u = din("c_glo_u", [33, TOWN], BF16)
    c_glo_w = din("c_glo_w", [33, NL + 1], BF16)
    c_ones = din("c_ones", [1, 512], BF16)
    c_bdones = din("c_bdones", [128, 2], BF16)
    c_repl2 = din("c_repl2", [2, 128])
    c_gaterep = din("c_gaterep", [32, 16 * 128])

    y_out = nc.dram_tensor("y", [TOWN, D], F32, kind="ExternalOutput").ap()

    def dscr(name, shape, dtype):
        return nc.dram_tensor(name, shape, dtype, kind="Internal").ap()

    kE_hbm = dscr("kE_hbm", [128, NTILE * 1024], BF16)
    kO_hbm = dscr("kO_hbm", [128, NTILE * 1024], BF16)
    v_hbm = dscr("v_hbm", [128, NTILE * D], BF16)
    s_hbm = dscr("s_hbm", [128, 16 * TOWN], F32)
    g_hbm = dscr("g_hbm", [128, 16 * TOWN], F32)
    wupbf = dscr("wupbf", [INTER, D], BF16)
    wgtbf = dscr("wgtbf", [INTER, D], BF16)
    wdnbf = dscr("wdnbf", [D, INTER], BF16)
    G_hbm = dscr("G_hbm", [INTER, D], BF16)

    identb = b.sb("identb", [128, 128], BF16)
    identf = b.sb("identf", [128, 128], F32)
    h128b = b.sb("h128b", [128, 128], BF16)
    h64bdb = b.sb("h64bdb", [128, 128], BF16)
    lau = b.sb("lau", [67, 128], BF16)
    law = b.sb("law", [67, 128], BF16)
    lat_u = b.sb("lat_u", [32, NL], BF16)
    lat_w = b.sb("lat_w", [32, NL], BF16)
    glo_u = b.sb("glo_u", [33, TOWN], BF16)
    glo_w = b.sb("glo_w", [33, NL + 1], BF16)
    onesb = b.sb("onesb", [1, 512], BF16)
    bdones = b.sb("bdones", [128, 2], BF16)
    repl2 = b.sb("repl2", [2, 128], F32)
    ln_mean = b.sb("ln_mean", [128, NTILE], F32)
    ln_rstd = b.sb("ln_rstd", [128, NTILE], F32)
    su_c = b.sb("su_c", [128, 16], F32)
    sstar = b.sb("sstar", [128, 64], F32)
    sgate = b.sb("sgate", [128, 64], F32)
    wsdp_c = b.sb("wsdp_c", [128, 16], F32)
    pqE = b.sb("pqE", [128, 8 * M], BF16)
    pqO = b.sb("pqO", [128, 8 * M], BF16)
    bias_r = b.sb("bias_r", [1, 3 * D], BF16)
    stats6 = b.sb("stats6", [128, 24], F32)
    mv2 = b.sb("mv2", [128, 8], F32)
    evf = b.sb("evf", [128, 2048], F32)
    ebf = b.sb("ebf", [128, 1024], BF16)
    eb2 = b.sb("eb2", [128, 512], BF16)
    nmax = b.sb("nmax", [128, 8], F32)

    pb = [b.psum(f"pb{i}", [128, 512]) for i in range(8)]
    b.init_arena(174 * 1024)

    sem, cnt = b.sem, b.cnt

    WQ = {"w_qkv": w_qkv, "w_lm_qkv": w_lm_qkv, "w_lm_out": w_lm_out,
          "w_l2t_q": w_l2t_q, "w_l2t_out": w_l2t_out, "w_lm_kv": w_lm_kv,
          "w_out": w_out_i}

    def wsl(buf, kc, c0, w):
        return wslab[:, buf * 4096 + kc * 256 + c0: buf * 4096 + kc * 256 + c0 + w]

    def wsl_eo(buf, kc, eo):
        sl = wslab[:, buf * 4096 + kc * 256: buf * 4096 + (kc + 1) * 256]
        w3 = sl.rearrange("p (slot d par) -> p slot d par", slot=4, par=2)
        return w3[:, :, :, eo]

    def bias_eo(off, eo):
        b3 = bias_r[:, off:off + 256].rearrange(
            "o (slot d par) -> o slot d par", slot=4, par=2)
        return b3[:, :, :, eo]

    def load_wslab(g, w_ap, col0, buf):
        src = w_ap[:, col0:col0 + 256].rearrange("(kc p) w -> p kc w", p=128)
        dst = wslab[:, buf * 4096:(buf + 1) * 4096].rearrange(
            "p (kc w) -> p kc w", kc=16)
        return g.dma_start(out=dst, in_=src)

    with nc.Block() as blk:
        b.blk = blk
        wslab = b.sb_tmp("wslab", [128, 2 * 4096], BF16)

        def p0s(sy):
            for dst, src in ((identb, c_ident), (identf, c_identf),
                             (h128b, c_h128), (h64bdb, c_h64bd),
                             (lau, c_lau), (law, c_law),
                             (lat_u, c_lat_u), (lat_w, c_lat_w),
                             (glo_u, c_glo_u), (glo_w, c_glo_w),
                             (onesb, c_ones), (bdones, c_bdones),
                             (repl2, c_repl2)):
                b.inc(sy.dma_start(out=dst[:, :], in_=src), "s_hld", 16)
        blk.sync(p0s)

        def p0g(g):
            b.inc(g.dma_start(out=bias_r[:, :], in_=b_qkv[None, :]), "s_gld", 16)
            b.inc(g.dma_start(out=su_c[:, :],
                              in_=su_ug.rearrange("(c p) -> p c", p=128),
                              allow_slow_non_contiguous=True), "s_gld", 16)
            b.inc(g.dma_start(out=wsdp_c[:, :],
                              in_=ws_dp.rearrange("(c p) -> p c", p=128),
                              allow_slow_non_contiguous=True), "s_gld", 16)
            b.inc(g.dma_start(out=sstar[:, :], in_=ws_up_p), "s_gld", 16)
            b.inc(g.dma_start(out=sgate[:, :], in_=ws_gt_p), "s_gld", 16)
            b.inc(g.dma_start(out=evf[:, 0:64], in_=su_dp_p), "s_gld", 16)
            for dst, src in ((wupbf, wu_in), (wgtbf, wg_in), (wdnbf, wd_in)):
                for i in range(8):
                    r = src.shape[0] // 8
                    b.inc(g.dma_start(out=dst[i * r:(i + 1) * r, :],
                                      in_=src[i * r:(i + 1) * r, :]),
                          "s_gld", 16)
        blk.gpsimd(p0g)

        def p0v(v):
            b.wait_all(v, ("s_gld",))
            v.tensor_scalar(su_c[:, :], su_c[:, :],
                            1.0 / (np.sqrt(2048.0) * 64.0), None, ALU.mult)
            v.tensor_tensor(sstar[:, :], sstar[:, :], evf[:, 0:64], ALU.mult)
            v.drain()
            v.tensor_scalar(sstar[:, :], sstar[:, :],
                            1.0 / np.sqrt(8192.0), None, ALU.mult)
            v.tensor_scalar(sgate[:, :], sgate[:, :], 64.0, None, ALU.mult)
            v.tensor_scalar(wsdp_c[:, :], wsdp_c[:, :], 64.0, None, ALU.mult)
            v.drain()
            b.inc(v.nop(), "s_dv")
        blk.vector(p0v)
        b.barrier()

        xnT = b.sb_tmp("xnT", [128, 16 * L], BF16)
        cosT = b.sb_tmp("cosT", [128, L], F32)
        sinT = b.sb_tmp("sinT", [128, L], F32)
        xtile = b.sb_tmp("xtile", [128, 2 * D], F32)
        xn_tm = b.sb_tmp("xn_tm", [128, 2 * D], BF16)

        def p1c(sy):
            for dst, src in ((cosT, c_cos), (sinT, c_sin)):
                b.inc(sy.dma_start(out=dst[:, :], in_=src), "s_hld", 16)
        blk.sync(p1c)
        b.barrier()

        hld0 = (cnt["s_hld"], cnt["s_h2"])
        dv0, ac0, mm0 = cnt["s_dv"], cnt["s_ac"], cnt["s_mm"]
        XP = ("s_hld", "s_h2")

        stats_dv, apply_dv, evac_dv = {}, {}, {}

        def p1s(sy):
            for tt in range(NTILE):
                if tt >= 2:
                    sy.wait_ge(sem["s_dv"], apply_dv[tt - 2])
                b.inc(sy.dma_start(out=xtile[:, (tt % 2) * D:(tt % 2 + 1) * D],
                                   in_=x_in[tt * 128:(tt + 1) * 128, :]),
                      XP[tt % 2], 16)

        def p1v(v):
            def evac(e):
                for c4 in range(4):
                    gidx = e * 4 + c4
                    v.wait_ge(sem["s_mm"], mm0 + gidx + 1)
                    bank = pb[gidx % 2]
                    for j in range(4):
                        c = c4 * 4 + j
                        v.tensor_copy(
                            xnT[:, c * L + e * 128: c * L + (e + 1) * 128],
                            bank[:, j * 128:(j + 1) * 128])
                    v.drain()
                    evac_dv[gidx] = b.inc(v.nop(), "s_dv")
            for tt in range(NTILE):
                xt = xtile[:, (tt % 2) * D:(tt % 2 + 1) * D]
                v.wait_ge(sem[XP[tt % 2]], hld0[tt % 2] + 16 * (tt // 2 + 1))
                for i in range(4):
                    v.bn_stats(stats6[:, 6 * i:6 * i + 6],
                               xt[:, 512 * i:512 * (i + 1)])
                v.drain()
                v.bn_aggr(mv2[:, 0:2], stats6[:, :])
                v.drain()
                v.tensor_copy(ln_mean[:, tt:tt + 1], mv2[:, 0:1])
                v.tensor_scalar(mv2[:, 2:3], mv2[:, 1:2], 1e-5, None, ALU.add)
                v.drain()
                v.reciprocal(mv2[:, 3:4], mv2[:, 2:3])
                v.drain()
                stats_dv[tt] = b.inc(v.nop(), "s_dv")
                v.wait_ge(sem["s_ac"], ac0 + tt + 1)
                v.tensor_scalar(xn_tm[:, (tt % 2) * D:(tt % 2 + 1) * D], xt,
                                mv2[:, 0:1], ln_rstd[:, tt:tt + 1],
                                ALU.subtract, ALU.mult)
                v.drain()
                apply_dv[tt] = b.inc(v.nop(), "s_dv")
                if tt >= 1:
                    evac(tt - 1)
            evac(NTILE - 1)
        blk.vector(p1v)
        blk.sync(p1s)

        def p1a(s):
            for tt in range(NTILE):
                s.wait_ge(sem["s_dv"], stats_dv[tt])
                s.activation(out=ln_rstd[:, tt:tt + 1], in_=mv2[:, 3:4],
                             func=AF.Sqrt)
                s.drain()
                b.inc(s.nop(), "s_ac")
        blk.scalar(p1a)

        def p1t(t):
            for tt in range(NTILE):
                t.wait_ge(sem["s_dv"], apply_dv[tt])
                xn = xn_tm[:, (tt % 2) * D:(tt % 2 + 1) * D]
                for c4 in range(4):
                    gidx = tt * 4 + c4
                    if gidx >= 2:
                        t.wait_ge(sem["s_dv"], evac_dv[gidx - 2])
                    bank = pb[gidx % 2]
                    for j in range(4):
                        c = c4 * 4 + j
                        t.matmul(bank[:, j * 128:(j + 1) * 128],
                                 xn[:, c * 128:(c + 1) * 128], identb[:, :],
                                 start=True, stop=True)
                    t.drain()
                    b.inc(t.nop(), "s_mm")
        blk.tensor(p1t)
        b.barrier()
        b.free_tmp("xtile", "xn_tm")
        b.tap("xnT", xnT[:, :], [128, 16 * L], BF16)
        if STOP_AFTER == "p1":
            _tail(b, y_out, evf); b.st.close(); return b

        def emit_qk(wcol0, tok0, ntok, dE, dO, ctab, stab):
            ng = ntok // 512
            GP = ("s_gld", "s_g2")
            mm_entry = cnt["s_mm"]
            rope_dv = {}
            mm_at = {}
            for G in range(8):
                buf = G % 2

                def gload(g, G=G, buf=buf):
                    if G >= 2:
                        g.wait_ge(sem["s_mm"], mm_at[G - 2])
                    else:
                        g.wait_ge(sem["s_mm"], mm_entry)
                    b.inc(load_wslab(g, w_qkv, wcol0 + 256 * G, buf),
                          GP[buf], 16)
                blk.gpsimd(gload)
                gld_nm, gld_at = GP[buf], cnt[GP[buf]]

                def tmm(t, G=G, buf=buf, gld_nm=gld_nm, gld_at=gld_at):
                    t.wait_ge(sem[gld_nm], gld_at)
                    if G == 0:
                        b.wait_all(t, ("s_dv",))
                    else:
                        t.wait_ge(sem["s_dv"], rope_dv[G - 1])
                    for eo in range(2):
                        for tg in range(ng):
                            bank = pb[eo * ng + tg]
                            for kc in range(16):
                                t.matmul(bank[:, :], wsl_eo(buf, kc, eo),
                                         xnT[:, kc * L + tok0 + tg * 512:
                                             kc * L + tok0 + (tg + 1) * 512],
                                         start=(kc == 0), stop=False)
                            t.matmul(bank[:, :], bias_eo(wcol0 + 256 * G, eo),
                                     onesb[:, 0:512], start=False, stop=True)
                    t.drain()
                    b.inc(t.nop(), "s_mm")
                blk.tensor(tmm)
                mm_at[G] = cnt["s_mm"]

                def vrope(v, G=G, at=mm_at[G], ng=ng):
                    v.wait_ge(sem["s_mm"], at)
                    for tg in range(ng):
                        pe, po = pb[tg], pb[ng + tg]
                        ct = ctab[:, tg * 512:(tg + 1) * 512]
                        sn = stab[:, tg * 512:(tg + 1) * 512]
                        s0, s1, s2, s3 = (evf[:, i * 512:(i + 1) * 512]
                                          for i in range(4))
                        v.tensor_tensor(s0, pe[:, :], ct, ALU.mult)
                        v.tensor_tensor(s1, po[:, :], sn, ALU.mult)
                        v.drain()
                        v.tensor_tensor(s2, s0, s1, ALU.subtract)
                        v.drain()
                        v.tensor_tensor(s0, pe[:, :], sn, ALU.mult)
                        v.tensor_tensor(s1, po[:, :], ct, ALU.mult)
                        dstE = dE.ap().rearrange("p (t g c) -> p t g c",
                                                 g=8, c=128)[:, 4 * tg:4 * tg + 4,
                                                             G, :]
                        v.tensor_copy(dstE,
                                      s2.rearrange("p (a c) -> p a c", a=4))
                        v.drain()
                        v.tensor_tensor(s3, s0, s1, ALU.add)
                        v.drain()
                        dstO = dO.ap().rearrange("p (t g c) -> p t g c",
                                                 g=8, c=128)[:, 4 * tg:4 * tg + 4,
                                                             G, :]
                        v.tensor_copy(dstO,
                                      s3.rearrange("p (a c) -> p a c", a=4))
                        v.drain()
                    b.inc(v.nop(), "s_dv")
                blk.vector(vrope)
                rope_dv[G] = cnt["s_dv"]

        kEb = b.sb_tmp("kEb", [128, NTILE * 1024], BF16)
        kOb = b.sb_tmp("kOb", [128, NTILE * 1024], BF16)
        emit_qk(D, 0, L, kEb, kOb, cosT, sinT)
        b.barrier()

        def spillk(sy):
            b.inc(sy.dma_start(out=kE_hbm[:, :], in_=kEb[:, :]), "s_st", 16)
            b.inc(sy.dma_start(out=kO_hbm[:, :], in_=kOb[:, :]), "s_st", 16)
        blk.sync(spillk)
        b.tap("kE", kEb[:, :], [128, NTILE * 1024], BF16)
        b.barrier()
        b.free_tmp("kEb", "kOb", "cosT", "sinT")

        qE = b.sb_tmp("qE", [128, 8 * TOWN], BF16)
        qO = b.sb_tmp("qO", [128, 8 * TOWN], BF16)
        cosQ = b.sb_tmp("cosQ", [128, TOWN], F32)
        sinQ = b.sb_tmp("sinQ", [128, TOWN], F32)

        def p3c(sy):
            for dst, src in ((cosQ, c_cosq), (sinQ, c_sinq)):
                b.inc(sy.dma_start(out=dst[:, :], in_=src), "s_hld", 16)
        blk.sync(p3c)
        b.barrier()
        emit_qk(0, 0, TOWN, qE, qO, cosQ, sinQ)
        b.barrier()
        b.free_tmp("cosQ", "sinQ")
        b.tap("qE", qE[:, :], [128, 8 * TOWN], BF16)
        if STOP_AFTER == "p3":
            _tail(b, y_out, evf); b.st.close(); return b

        vst = b.sb_tmp("vst", [128, 2 * 2048], BF16)
        vdv_prev = [cnt["s_dv"]]
        cnt_vsp = {0: cnt["s_st"], 1: cnt["s_s2"]}
        v_mm_entry = cnt["s_mm"]
        vmm = {}
        for sl8 in range(8):
            buf = sl8 % 2

            def gload(g, sl8=sl8, buf=buf):
                if sl8 >= 2:
                    g.wait_ge(sem["s_mm"], vmm[(sl8 - 2, 1)])
                else:
                    g.wait_ge(sem["s_mm"], v_mm_entry)
                b.inc(load_wslab(g, w_qkv, 2 * D + 256 * sl8, buf),
                      ("s_gld", "s_g2")[buf], 16)
            blk.gpsimd(gload)
            gld_nm = ("s_gld", "s_g2")[buf]
            gld_at = cnt[gld_nm]

            for h8 in range(2):
                sp = (2 * sl8 + h8) % 2

                def tmm(t, sl8=sl8, buf=buf, h8=h8, gld_nm=gld_nm,
                        gld_at=gld_at):
                    t.wait_ge(sem[gld_nm], gld_at)
                    t.wait_ge(sem["s_dv"], vdv_prev[0])
                    for tg8 in range(8):
                        tt = h8 * 8 + tg8
                        bank = pb[tg8]
                        for kc in range(16):
                            t.matmul(bank[:, 0:256],
                                     xnT[:, kc * L + tt * 128:
                                         kc * L + (tt + 1) * 128],
                                     wsl(buf, kc, 0, 256),
                                     start=(kc == 0), stop=False)
                        t.matmul(bank[:, 0:256], onesb[:, 0:128],
                                 bias_r[:, 2 * D + 256 * sl8:
                                        2 * D + 256 * (sl8 + 1)],
                                 start=False, stop=True)
                    t.drain()
                    b.inc(t.nop(), "s_mm")
                blk.tensor(tmm)
                vmm[(sl8, h8)] = cnt["s_mm"]

                def vev(v, sl8=sl8, h8=h8, sp=sp, at=vmm[(sl8, h8)],
                        vw=None):
                    v.wait_ge(sem["s_mm"], at)
                    nm2 = ("s_st", "s_s2")[sp]
                    v.wait_ge(sem[nm2], cnt_vsp[sp])
                    for tg8 in range(8):
                        v.tensor_copy(vst[:, sp * 2048 + tg8 * 256:
                                          sp * 2048 + (tg8 + 1) * 256],
                                      pb[tg8][:, 0:256])
                    v.drain()
                    b.inc(v.nop(), "s_dv")
                blk.vector(vev)
                vdv_prev = [cnt["s_dv"]]

                def vsp(sy, sl8=sl8, h8=h8, sp=sp, at=vdv_prev[0]):
                    sy.wait_ge(sem["s_dv"], at)
                    dst = v_hbm.rearrange("p (tt c) -> p tt c", tt=NTILE)[
                        :, h8 * 8:(h8 + 1) * 8, 256 * sl8:256 * (sl8 + 1)]
                    b.inc(sy.dma_start(
                        out=dst,
                        in_=vst[:, sp * 2048:(sp + 1) * 2048].rearrange(
                            "p (tg c) -> p tg c", tg=8)),
                        ("s_st", "s_s2")[sp], 16)
                blk.sync(vsp)
                cnt_vsp[sp] = cnt[("s_st", "s_s2")[sp]]
        b.barrier()
        b.free_tmp("vst")
        b.tap("v_tm", v_hbm[:, :], [128, NTILE * D], BF16)

        pqsb = b.sb_tmp("pqsb", [M, D], BF16)

        def p5l(g):
            b.inc(g.dma_start(out=pqsb[:, :], in_=pool_q), "s_gld", 16)
        blk.gpsimd(p5l)
        gld5 = cnt["s_gld"]

        def p5t(t):
            t.wait_ge(sem["s_gld"], gld5)
            for eo in range(2):
                for G in range(8):
                    w3 = pqsb[:, 256 * G:256 * (G + 1)].rearrange(
                        "p (slot d par) -> p slot d par", slot=4, par=2)
                    t.matmul(pb[eo][:, G * 4:(G + 1) * 4], w3[:, :, :, eo],
                             identb[0:4, 0:4], start=True, stop=True)
            t.drain()
            b.inc(t.nop(), "s_mm")
        blk.tensor(p5t)
        mm5 = cnt["s_mm"]

        def p5v(v):
            v.wait_ge(sem["s_mm"], mm5)
            v.tensor_copy(pqE[:, :], pb[0][:, 0:32])
            v.tensor_copy(pqO[:, :], pb[1][:, 0:32])
            v.drain()
            b.inc(v.nop(), "s_dv")
        blk.vector(p5v)
        b.barrier()
        b.free_tmp("pqsb", "xnT")
        if STOP_AFTER == "p4":
            _tail(b, y_out, evf); b.st.close(); return b

        OL = b.sb_tmp("OL", [128, 16 * TOWN], BF16)
        lmf = b.sb_tmp("lmf", [128, 16 * NL], F32)
        kEt = b.sb_tmp("kEt", [128, 2 * 1024], BF16)
        kOt = b.sb_tmp("kOt", [128, 2 * 1024], BF16)
        vt = b.sb_tmp("vt", [128, 2 * D], BF16)

        pool_mm_at, pool_dv_at = {}, {}
        loc_dv = [cnt["s_dv"]]
        for tt in range(NTILE):
            buf = tt % 2

            def krel(sy, tt=tt, buf=buf):
                if tt >= 2:
                    sy.wait_ge(sem["s_mm"], pool_mm_at[tt - 2])
                b.inc(sy.dma_start(out=kEt[:, buf * 1024:(buf + 1) * 1024],
                                   in_=kE_hbm[:, tt * 1024:(tt + 1) * 1024]),
                      ("s_hld", "s_h2")[buf], 16)
                b.inc(sy.dma_start(out=kOt[:, buf * 1024:(buf + 1) * 1024],
                                   in_=kO_hbm[:, tt * 1024:(tt + 1) * 1024]),
                      ("s_hld", "s_h2")[buf], 16)
                b.inc(sy.dma_start(out=vt[:, buf * D:(buf + 1) * D],
                                   in_=v_hbm[:, tt * D:(tt + 1) * D]),
                      ("s_hld", "s_h2")[buf], 16)
            blk.sync(krel)
            k_nm, at_k = ("s_hld", "s_h2")[buf], cnt[("s_hld", "s_h2")[buf]]
            kEv = kEt[:, buf * 1024:(buf + 1) * 1024]
            kOv = kOt[:, buf * 1024:(buf + 1) * 1024]

            if tt < NOWN and not P6_NO_LOCAL:
                for hg in range(8):
                    def lmm(t, tt=tt, hg=hg, nm=k_nm, at=at_k):
                        t.wait_ge(sem[nm], at)
                        t.wait_ge(sem["s_dv"], loc_dv[0])
                        for hh in range(4):
                            t.matmul(pb[hh][:, 0:128], lau[:, :], law[:, :],
                                     start=True, stop=False,
                                     skip_group_check=True)
                        for hh in range(4):
                            h = hg * 4 + hh
                            G, s4 = h // 4, 32 * (h % 4)
                            bank = pb[hh]
                            t.drain()
                            for bl in range(2):
                                for qq, kk in ((qE, kEv), (qO, kOv)):
                                    t.matmul(
                                        bank[64 * bl:64 * bl + 64,
                                             64 * bl:64 * bl + 64],
                                        qq[s4:s4 + 32,
                                           tt * 1024 + G * 128 + 64 * bl:
                                           tt * 1024 + G * 128 + 64 * bl + 64],
                                        kk[s4:s4 + 32,
                                           G * 128 + 64 * bl:
                                           G * 128 + 64 * bl + 64],
                                        start=False,
                                        stop=(bl == 1 and kk is kOv),
                                        tile_position=_tp(s4, 64 * bl),
                                        skip_group_check=True)
                        t.drain()
                        b.inc(t.nop(), "s_mm")
                    blk.tensor(lmm)
                    at_s = cnt["s_mm"]

                    def lsm(v, at=at_s):
                        v.wait_ge(sem["s_mm"], at)
                        for hh in range(4):
                            v.reduce_max(nmax[:, hh:hh + 1], pb[hh][:, 0:128],
                                         AXL, negate=True)
                        v.drain()
                        b.inc(v.nop(), "s_dv")
                    blk.vector(lsm)
                    at_max = cnt["s_dv"]

                    def lexp(s, at=at_max):
                        s.wait_ge(sem["s_dv"], at)
                        for hh in range(4):
                            s.activation(out=ebf[:, hh * 128:(hh + 1) * 128],
                                         in_=pb[hh][:, 0:128], func=AF.Exp,
                                         bias=nmax[:, hh:hh + 1],
                                         accum_out=mv2[:, hh:hh + 1])
                        s.drain()
                        b.inc(s.nop(), "s_ac")
                    blk.scalar(lexp)
                    at_exp = cnt["s_ac"]

                    def lnrm(v, at=at_exp):
                        v.wait_ge(sem["s_ac"], at)
                        v.reciprocal(mv2[:, 4:8], mv2[:, 0:4])
                        v.drain()
                        for hh in range(4):
                            v.tensor_scalar(ebf[:, hh * 128:(hh + 1) * 128],
                                            ebf[:, hh * 128:(hh + 1) * 128],
                                            mv2[:, 4 + hh:5 + hh], None,
                                            ALU.mult)
                        v.drain()
                        b.inc(v.nop(), "s_dv")
                    blk.vector(lnrm)
                    at_nrm = cnt["s_dv"]

                    def ltr(t, at=at_nrm):
                        t.wait_ge(sem["s_dv"], at)
                        for hh in range(4):
                            t.matmul(pb[4 + hh][:, 0:128],
                                     ebf[:, hh * 128:(hh + 1) * 128],
                                     identb[:, :], start=True, stop=True)
                        t.drain()
                        b.inc(t.nop(), "s_mm")
                    blk.tensor(ltr)
                    at_tr = cnt["s_mm"]

                    def levt(v, at=at_tr):
                        v.wait_ge(sem["s_mm"], at)
                        for hh in range(4):
                            v.tensor_copy(eb2[:, hh * 128:(hh + 1) * 128],
                                          pb[4 + hh][:, 0:128])
                        v.drain()
                        b.inc(v.nop(), "s_dv")
                    blk.vector(levt)
                    at_evt = cnt["s_dv"]

                    def lav(t, tt=tt, hg=hg, at=at_evt):
                        t.wait_ge(sem["s_dv"], at)
                        for hh in range(4):
                            h = hg * 4 + hh
                            bank = pb[(hh // 2) * 2]
                            t.matmul(bank[64 * (h % 2):64 * (h % 2) + 64,
                                          0:128],
                                     vt[:, (tt % 2) * D + h * 64:
                                        (tt % 2) * D + h * 64 + 64],
                                     eb2[:, hh * 128:(hh + 1) * 128],
                                     start=True, stop=(hh % 2 == 1),
                                     tile_position=_tp(0, 64 * (h % 2)),
                                     skip_group_check=True)
                        t.drain()
                        b.inc(t.nop(), "s_mm")
                    blk.tensor(lav)
                    at_av = cnt["s_mm"]

                    def lole(v, tt=tt, hg=hg, at=at_av):
                        v.wait_ge(sem["s_mm"], at)
                        for hh in (0, 2):
                            hp = (hg * 4 + hh) // 2
                            v.tensor_copy(
                                OL[:, hp * 1024 + tt * 128:
                                   hp * 1024 + (tt + 1) * 128],
                                pb[hh][:, 0:128])
                        v.drain()
                        b.inc(v.nop(), "s_dv")
                    blk.vector(lole)
                    loc_dv = [cnt["s_dv"]]

            if P6_NO_POOL:
                def pskip(t, tt=tt, nm=k_nm, at=at_k):
                    t.wait_ge(sem[nm], at)
                    t.drain()
                    b.inc(t.nop(), "s_mm")
                blk.tensor(pskip)
                pool_mm_at[tt] = cnt["s_mm"]

                def pskip2(v, at=pool_mm_at[tt]):
                    v.wait_ge(sem["s_mm"], at)
                    b.inc(v.nop(), "s_dv")
                blk.vector(pskip2)
                pool_dv_at[tt] = cnt["s_dv"]
                continue

            def pmm(t, tt=tt, nm=k_nm, at=at_k):
                t.wait_ge(sem[nm], at)
                if tt > 0:
                    t.wait_ge(sem["s_dv"], pool_dv_at[tt - 1])
                for slot in range(4):
                    s4 = 32 * slot
                    t.drain()
                    for G in range(8):
                        h = 4 * G + slot
                        for bl in range(2):
                            for eo, (pq_, kk) in enumerate(((pqE, kEv),
                                                            (pqO, kOv))):
                                t.matmul(pb[6][64 * bl:64 * bl + 64,
                                               4 * h:4 * h + 4],
                                         kk[s4:s4 + 32, G * 128 + 64 * bl:
                                            G * 128 + 64 * bl + 64],
                                         pq_[s4:s4 + 32, 4 * G:4 * G + 4],
                                         start=(eo == 0), stop=(eo == 1),
                                         tile_position=_tp(s4, 64 * bl),
                                         skip_group_check=True)
                t.drain()
                b.inc(t.nop(), "s_mm")
            blk.tensor(pmm)
            at_ps = cnt["s_mm"]

            def pexp(s, at=at_ps):
                s.wait_ge(sem["s_mm"], at)
                s.activation(out=ebf[:, 0:128], in_=pb[6][:, 0:128],
                             func=AF.Exp, scale=0.125)
                s.drain()
                b.inc(s.nop(), "s_ac")
            blk.scalar(pexp)
            at_pe = cnt["s_ac"]

            def psmm(t, at=at_pe):
                t.wait_ge(sem["s_ac"], at)
                t.matmul(pb[7][0:2, 0:128], bdones[:, :], ebf[:, 0:128],
                         start=True, stop=True)
                t.drain()
                b.inc(t.nop(), "s_mm")
            blk.tensor(psmm)
            at_psu = cnt["s_mm"]

            def prec(v, at=at_psu):
                v.wait_ge(sem["s_mm"], at)
                v.reciprocal(evf[0:2, 0:128], pb[7][0:2, 0:128])
                v.drain()
                b.inc(v.nop(), "s_dv")
            blk.vector(prec)
            at_pre = cnt["s_dv"]

            def prep(t, at=at_pre):
                t.wait_ge(sem["s_dv"], at)
                t.matmul(pb[7][:, 0:128], repl2[:, :], evf[0:2, 0:128],
                         start=True, stop=True)
                t.drain()
                b.inc(t.nop(), "s_mm")
            blk.tensor(prep)
            at_prp = cnt["s_mm"]

            def pnrm(v, at=at_prp):
                v.wait_ge(sem["s_mm"], at)
                v.tensor_tensor(ebf[:, 0:128], ebf[:, 0:128],
                                pb[7][:, 0:128], ALU.mult)
                v.drain()
                b.inc(v.nop(), "s_dv")
            blk.vector(pnrm)
            at_pn = cnt["s_dv"]

            def pav(t, tt=tt, at=at_pn):
                t.wait_ge(sem["s_dv"], at)
                for bl in range(2):
                    t.drain()
                    for h in range(H):
                        t.matmul(pb[6][64 * (h % 2):64 * (h % 2) + 64,
                                       (h // 2) * 8 + bl * 4:
                                       (h // 2) * 8 + bl * 4 + 4],
                                 vt[64 * bl:64 * bl + 64,
                                    (tt % 2) * D + h * 64:
                                    (tt % 2) * D + h * 64 + 64],
                                 ebf[64 * bl:64 * bl + 64, 4 * h:4 * h + 4],
                                 start=True, stop=True,
                                 tile_position=_tp(64 * bl, 64 * (h % 2)),
                                 skip_group_check=True)
                t.drain()
                b.inc(t.nop(), "s_mm")
            blk.tensor(pav)
            pool_mm_at[tt] = cnt["s_mm"]

            def plm(v, tt=tt, at=pool_mm_at[tt]):
                v.wait_ge(sem["s_mm"], at)
                for hp in range(16):
                    v.tensor_copy(lmf[:, hp * 128 + tt * 8:
                                      hp * 128 + tt * 8 + 8],
                                  pb[6][:, hp * 8: hp * 8 + 8])
                v.drain()
                b.inc(v.nop(), "s_dv")
            blk.vector(plm)
            pool_dv_at[tt] = cnt["s_dv"]

        b.barrier()
        b.tap("lmf_pool", lmf[:, :], [128, 16 * NL], F32)
        b.tap("OL", OL[:, :], [128, 16 * TOWN], BF16)
        if STOP_AFTER == "p6":
            _tail(b, y_out, evf); b.st.close(); return b

        lmfb = b.sb_tmp("lmfb", [128, 16 * NL], BF16)

        def lmcp(v):
            for c in range(16):
                v.tensor_copy(lmfb[:, c * 128:(c + 1) * 128],
                              lmf[:, c * 128:(c + 1) * 128])
            v.drain()
            b.inc(v.nop(), "s_dv")
        blk.vector(lmcp)
        b.barrier()

        def load_bias(name_ap, n, off=0):
            def f(g):
                b.inc(g.dma_start(out=bias_r[:, off:off + n],
                                  in_=name_ap[None, :]), "s_gld", 16)
            blk.gpsimd(f)
        load_bias(b_lm_qkv, 3 * D)
        b.barrier()

        def fm_proj(w_ap, wc0, boff, dst, n_rg, accum=False, upd_bf=None):
            fm_mm, fm_dv = {}, {}
            mm_entry = cnt["s_mm"]
            for rgp in range(0, n_rg, 2):
                buf = (rgp // 2) % 2

                def gload(g, rgp=rgp, buf=buf):
                    if rgp >= 4:
                        g.wait_ge(sem["s_mm"], fm_mm[rgp - 4])
                    else:
                        g.wait_ge(sem["s_mm"], mm_entry)
                    b.inc(load_wslab(g, w_ap, wc0 + rgp * 128, buf),
                          ("s_gld", "s_g2")[buf], 16)
                blk.gpsimd(gload)
                gld_nm = ("s_gld", "s_g2")[buf]
                gld_at = cnt[gld_nm]

                def tmm(t, rgp=rgp, buf=buf, gld_nm=gld_nm, gld_at=gld_at):
                    t.wait_ge(sem[gld_nm], gld_at)
                    if rgp == 0:
                        b.wait_all(t, ("s_dv",))
                    if rgp >= 4:
                        t.wait_ge(sem["s_dv"], fm_dv[rgp - 4])
                    for j in range(2):
                        bank = pb[(rgp + j) % 4]
                        for kc in range(16):
                            t.matmul(bank[:, 0:128],
                                     wsl(buf, kc, j * 128, 128),
                                     lmfb[:, kc * 128:(kc + 1) * 128],
                                     start=(kc == 0), stop=False)
                        t.matmul(bank[:, 0:128],
                                 bias_r[:, boff + wc0 + (rgp + j) * 128:
                                        boff + wc0 + (rgp + j + 1) * 128],
                                 onesb[:, 0:128], start=False, stop=True)
                    t.drain()
                    b.inc(t.nop(), "s_mm")
                blk.tensor(tmm)
                fm_mm[rgp] = cnt["s_mm"]

                def vev(v, rgp=rgp, at=fm_mm[rgp]):
                    v.wait_ge(sem["s_mm"], at)
                    for j in range(2):
                        c = rgp + j
                        if accum:
                            v.tensor_tensor(lmf[:, c * 128:(c + 1) * 128],
                                            lmf[:, c * 128:(c + 1) * 128],
                                            pb[c % 4][:, 0:128], ALU.add)
                        else:
                            v.tensor_copy(dst[:, c * 128:(c + 1) * 128],
                                          pb[c % 4][:, 0:128])
                    v.drain()
                    if accum:
                        for j in range(2):
                            c = rgp + j
                            v.tensor_copy(upd_bf[:, c * 128:(c + 1) * 128],
                                          lmf[:, c * 128:(c + 1) * 128])
                        v.drain()
                    b.inc(v.nop(), "s_dv")
                blk.vector(vev)
                fm_dv[rgp] = cnt["s_dv"]

        lqsb = b.sb_tmp("lqsb", [128, 16 * 128], BF16)
        lksb = b.sb_tmp("lksb", [128, 16 * 128], BF16)
        fm_proj(w_lm_qkv, 0, 0, lqsb, 16)
        fm_proj(w_lm_qkv, D, 0, lksb, 16)
        b.barrier()

        lv_tm = b.sb_tmp("lv_tm", [128, D], BF16)

        def latmaj(w_ap, wc0, boff, dst):
            lv_mm, lv_dv = {}, {}
            mm_entry = cnt["s_mm"]
            for sl8 in range(8):
                buf = sl8 % 2

                def gload(g, sl8=sl8, buf=buf):
                    if sl8 >= 2:
                        g.wait_ge(sem["s_mm"], lv_mm[sl8 - 2])
                    else:
                        g.wait_ge(sem["s_mm"], mm_entry)
                    b.inc(load_wslab(g, w_ap, wc0 + 256 * sl8, buf),
                          ("s_gld", "s_g2")[buf], 16)
                blk.gpsimd(gload)
                gld_nm = ("s_gld", "s_g2")[buf]
                gld_at = cnt[gld_nm]

                def tmm(t, sl8=sl8, buf=buf, gld_nm=gld_nm, gld_at=gld_at):
                    t.wait_ge(sem[gld_nm], gld_at)
                    if sl8 == 0:
                        b.wait_all(t, ("s_dv",))
                    if sl8 >= 2:
                        t.wait_ge(sem["s_dv"], lv_dv[sl8 - 2])
                    bank = pb[4 + sl8 % 2]
                    for kc in range(16):
                        t.matmul(bank[:, 0:256],
                                 lmfb[:, kc * 128:(kc + 1) * 128],
                                 wsl(buf, kc, 0, 256),
                                 start=(kc == 0), stop=False)
                    t.matmul(bank[:, 0:256], onesb[:, 0:128],
                             bias_r[:, boff + 256 * sl8:boff + 256 * (sl8 + 1)],
                             start=False, stop=True)
                    t.drain()
                    b.inc(t.nop(), "s_mm")
                blk.tensor(tmm)
                lv_mm[sl8] = cnt["s_mm"]

                def vev(v, sl8=sl8, at=lv_mm[sl8]):
                    v.wait_ge(sem["s_mm"], at)
                    v.tensor_copy(dst[:, 256 * sl8:256 * (sl8 + 1)],
                                  pb[4 + sl8 % 2][:, 0:256])
                    v.drain()
                    b.inc(v.nop(), "s_dv")
                blk.vector(vev)
                lv_dv[sl8] = cnt["s_dv"]

        latmaj(w_lm_qkv, 2 * D, 2 * D, lv_tm)
        b.barrier()
        b.tap("lq", lqsb[:, :], [128, 16 * 128], BF16)
        b.tap("lv", lv_tm[:, :], [128, D], BF16)

        losb = b.sb_tmp("losb", [128, 16 * 128], BF16)
        lat_dv = [cnt["s_dv"]]
        for h in range(H):
            c2, base = h // 2, 64 * (h % 2)

            def lamm(t, h=h, c2=c2, base=base):
                t.wait_ge(sem["s_dv"], lat_dv[0])
                t.drain()
                t.matmul(pb[0][:, 0:128],
                         lqsb[base:base + 64, c2 * 128:(c2 + 1) * 128],
                         lksb[base:base + 64, c2 * 128:(c2 + 1) * 128],
                         start=True, stop=False, tile_position=_tp(base, 0),
                         skip_group_check=True)
                t.drain()
                t.matmul(pb[0][:, 0:128], lat_u[:, :], lat_w[:, :],
                         start=False, stop=True, skip_group_check=True)
                t.drain()
                b.inc(t.nop(), "s_mm")
            blk.tensor(lamm)
            at1 = cnt["s_mm"]

            def lasm(v, at=at1):
                v.wait_ge(sem["s_mm"], at)
                v.reduce_max(nmax[:, 0:1], pb[0][:, 0:128], AXL, negate=True)
                v.drain()
                v.tensor_scalar(nmax[:, 1:2], nmax[:, 0:1], 0.125, None,
                                ALU.mult)
                v.drain()
                b.inc(v.nop(), "s_dv")
            blk.vector(lasm)
            at2 = cnt["s_dv"]

            def laexp(s, at=at2):
                s.wait_ge(sem["s_dv"], at)
                s.activation(out=ebf[:, 0:128], in_=pb[0][:, 0:128],
                             func=AF.Exp, bias=nmax[:, 1:2], scale=0.125,
                             accum_out=mv2[:, 0:1])
                s.drain()
                b.inc(s.nop(), "s_ac")
            blk.scalar(laexp)
            at3 = cnt["s_ac"]

            def lanrm(v, at=at3):
                v.wait_ge(sem["s_ac"], at)
                v.reciprocal(mv2[:, 1:2], mv2[:, 0:1])
                v.drain()
                v.tensor_scalar(ebf[:, 0:128], ebf[:, 0:128], mv2[:, 1:2],
                                None, ALU.mult)
                v.drain()
                b.inc(v.nop(), "s_dv")
            blk.vector(lanrm)
            at4 = cnt["s_dv"]

            def latr(t, at=at4):
                t.wait_ge(sem["s_dv"], at)
                t.matmul(pb[1][:, 0:128], ebf[:, 0:128], identb[:, :],
                         start=True, stop=True)
                t.drain()
                b.inc(t.nop(), "s_mm")
            blk.tensor(latr)
            at5 = cnt["s_mm"]

            def laev(v, at=at5):
                v.wait_ge(sem["s_mm"], at)
                v.tensor_copy(eb2[:, 0:128], pb[1][:, 0:128])
                v.drain()
                b.inc(v.nop(), "s_dv")
            blk.vector(laev)
            at6 = cnt["s_dv"]

            def laav(t, h=h, at=at6):
                t.wait_ge(sem["s_dv"], at)
                t.matmul(pb[2][64 * (h % 2):64 * (h % 2) + 64, 0:128],
                         lv_tm[:, h * 64:(h + 1) * 64], eb2[:, 0:128],
                         start=True, stop=(h % 2 == 1),
                         tile_position=_tp(0, 64 * (h % 2)),
                         skip_group_check=True)
                t.drain()
                b.inc(t.nop(), "s_mm")
            blk.tensor(laav)
            at7 = cnt["s_mm"]

            def laol(v, h=h, at=at7):
                v.wait_ge(sem["s_mm"], at)
                if h % 2 == 1:
                    v.tensor_copy(losb[:, (h // 2) * 128:(h // 2 + 1) * 128],
                                  pb[2][:, 0:128])
                    v.drain()
                b.inc(v.nop(), "s_dv")
            blk.vector(laol)
            lat_dv = [cnt["s_dv"]]
        b.barrier()
        b.free_tmp("lqsb", "lksb", "lv_tm")
        b.tap("lo", losb[:, :], [128, 16 * 128], BF16)

        load_bias(b_lm_out, D)
        b.barrier()

        def fm_proj_src(w_ap, boff, src_sb):
            fa_mm, fa_dv = {}, {}
            mm_entry = cnt["s_mm"]
            for rgp in range(0, 16, 2):
                buf = (rgp // 2) % 2

                def gload(g, rgp=rgp, buf=buf):
                    if rgp >= 4:
                        g.wait_ge(sem["s_mm"], fa_mm[rgp - 4])
                    else:
                        g.wait_ge(sem["s_mm"], mm_entry)
                    b.inc(load_wslab(g, w_ap, rgp * 128, buf),
                          ("s_gld", "s_g2")[buf], 16)
                blk.gpsimd(gload)
                gld_nm = ("s_gld", "s_g2")[buf]
                gld_at = cnt[gld_nm]

                def tmm(t, rgp=rgp, buf=buf, gld_nm=gld_nm, gld_at=gld_at):
                    t.wait_ge(sem[gld_nm], gld_at)
                    if rgp == 0:
                        b.wait_all(t, ("s_dv",))
                    if rgp >= 4:
                        t.wait_ge(sem["s_dv"], fa_dv[rgp - 4])
                    for j in range(2):
                        bank = pb[(rgp + j) % 4]
                        for kc in range(16):
                            t.matmul(bank[:, 0:128],
                                     wsl(buf, kc, j * 128, 128),
                                     src_sb[:, kc * 128:(kc + 1) * 128],
                                     start=(kc == 0), stop=False)
                        t.matmul(bank[:, 0:128],
                                 bias_r[:, boff + (rgp + j) * 128:
                                        boff + (rgp + j + 1) * 128],
                                 onesb[:, 0:128], start=False, stop=True)
                    t.drain()
                    b.inc(t.nop(), "s_mm")
                blk.tensor(tmm)
                fa_mm[rgp] = cnt["s_mm"]

                def vev(v, rgp=rgp, at=fa_mm[rgp]):
                    v.wait_ge(sem["s_mm"], at)
                    for j in range(2):
                        c = rgp + j
                        v.tensor_tensor(lmf[:, c * 128:(c + 1) * 128],
                                        lmf[:, c * 128:(c + 1) * 128],
                                        pb[c % 4][:, 0:128], ALU.add)
                    v.drain()
                    for j in range(2):
                        c = rgp + j
                        v.tensor_copy(lmfb[:, c * 128:(c + 1) * 128],
                                      lmf[:, c * 128:(c + 1) * 128])
                    v.drain()
                    b.inc(v.nop(), "s_dv")
                blk.vector(vev)
                fa_dv[rgp] = cnt["s_dv"]

        fm_proj_src(w_lm_out, 0, losb)
        b.barrier()
        b.free_tmp("losb")
        b.tap("lmf_postlm", lmf[:, :], [128, 16 * NL], F32)

        load_bias(b_l2t_q, D)
        b.barrier()
        l2qE = b.sb_tmp("l2qE", [128, 8 * 128], BF16)
        l2qO = b.sb_tmp("l2qO", [128, 8 * 128], BF16)
        l2_mm, l2_dv = {}, {}
        l2_mm_entry = cnt["s_mm"]
        for G in range(8):
            buf = G % 2

            def gload(g, G=G, buf=buf):
                if G >= 2:
                    g.wait_ge(sem["s_mm"], l2_mm[G - 2])
                else:
                    g.wait_ge(sem["s_mm"], l2_mm_entry)
                b.inc(load_wslab(g, w_l2t_q, 256 * G, buf),
                      ("s_gld", "s_g2")[buf], 16)
            blk.gpsimd(gload)
            gld_nm = ("s_gld", "s_g2")[buf]
            gld_at = cnt[gld_nm]

            def tmm(t, G=G, buf=buf, gld_nm=gld_nm, gld_at=gld_at):
                t.wait_ge(sem[gld_nm], gld_at)
                if G == 0:
                    b.wait_all(t, ("s_dv",))
                else:
                    t.wait_ge(sem["s_dv"], l2_dv[G - 1])
                for eo in range(2):
                    bank = pb[eo]
                    for kc in range(16):
                        t.matmul(bank[:, 0:128], wsl_eo(buf, kc, eo),
                                 lmfb[:, kc * 128:(kc + 1) * 128],
                                 start=(kc == 0), stop=False)
                    t.matmul(bank[:, 0:128], bias_eo(256 * G, eo),
                             onesb[:, 0:128], start=False, stop=True)
                t.drain()
                b.inc(t.nop(), "s_mm")
            blk.tensor(tmm)
            l2_mm[G] = cnt["s_mm"]

            def vev(v, G=G, at=l2_mm[G]):
                v.wait_ge(sem["s_mm"], at)
                v.tensor_copy(l2qE[:, G * 128:(G + 1) * 128], pb[0][:, 0:128])
                v.tensor_copy(l2qO[:, G * 128:(G + 1) * 128], pb[1][:, 0:128])
                v.drain()
                b.inc(v.nop(), "s_dv")
            blk.vector(vev)
            l2_dv[G] = cnt["s_dv"]
        b.barrier()

        rof = b.sb_tmp("rof", [128, 16 * 128], F32)
        ro_mm_at, ro_dv_at = {}, {}
        for tt in range(NTILE):
            buf = tt % 2

            def krel2(sy, tt=tt, buf=buf):
                if tt >= 2:
                    sy.wait_ge(sem["s_mm"], ro_mm_at[tt - 2])
                b.inc(sy.dma_start(out=kEt[:, buf * 1024:(buf + 1) * 1024],
                                   in_=kE_hbm[:, tt * 1024:(tt + 1) * 1024]),
                      ("s_hld", "s_h2")[buf], 16)
                b.inc(sy.dma_start(out=kOt[:, buf * 1024:(buf + 1) * 1024],
                                   in_=kO_hbm[:, tt * 1024:(tt + 1) * 1024]),
                      ("s_hld", "s_h2")[buf], 16)
                b.inc(sy.dma_start(out=vt[:, buf * D:(buf + 1) * D],
                                   in_=v_hbm[:, tt * D:(tt + 1) * D]),
                      ("s_hld", "s_h2")[buf], 16)
            blk.sync(krel2)
            k_nm, at_k = ("s_hld", "s_h2")[buf], cnt[("s_hld", "s_h2")[buf]]
            kEv = kEt[:, buf * 1024:(buf + 1) * 1024]
            kOv = kOt[:, buf * 1024:(buf + 1) * 1024]

            def rmm(t, tt=tt, nm=k_nm, at=at_k):
                t.wait_ge(sem[nm], at)
                if tt > 0:
                    t.wait_ge(sem["s_dv"], ro_dv_at[tt - 1])
                else:
                    b.wait_all(t, ("s_dv",))
                for slot in range(4):
                    s4 = 32 * slot
                    t.drain()
                    for G in range(8):
                        h = 4 * G + slot
                        for bl in range(2):
                            lat0 = (tt * 2 + bl) * 4
                            for eo, (lq_, kk) in enumerate(((l2qE, kEv),
                                                            (l2qO, kOv))):
                                t.matmul(pb[6][64 * bl:64 * bl + 64,
                                               4 * h:4 * h + 4],
                                         kk[s4:s4 + 32, G * 128 + 64 * bl:
                                            G * 128 + 64 * bl + 64],
                                         lq_[s4:s4 + 32, G * 128 + lat0:
                                             G * 128 + lat0 + 4],
                                         start=(eo == 0), stop=(eo == 1),
                                         tile_position=_tp(s4, 64 * bl),
                                         skip_group_check=True)
                t.drain()
                b.inc(t.nop(), "s_mm")
            blk.tensor(rmm)
            at_rs = cnt["s_mm"]

            def rexp(s, at=at_rs):
                s.wait_ge(sem["s_mm"], at)
                s.activation(out=ebf[:, 0:128], in_=pb[6][:, 0:128],
                             func=AF.Exp, scale=0.125)
                s.drain()
                b.inc(s.nop(), "s_ac")
            blk.scalar(rexp)
            at_re = cnt["s_ac"]

            def rsmm(t, at=at_re):
                t.wait_ge(sem["s_ac"], at)
                t.matmul(pb[7][0:2, 0:128], bdones[:, :], ebf[:, 0:128],
                         start=True, stop=True)
                t.drain()
                b.inc(t.nop(), "s_mm")
            blk.tensor(rsmm)
            at_rsu = cnt["s_mm"]

            def rrec(v, at=at_rsu):
                v.wait_ge(sem["s_mm"], at)
                v.reciprocal(evf[0:2, 0:128], pb[7][0:2, 0:128])
                v.drain()
                b.inc(v.nop(), "s_dv")
            blk.vector(rrec)
            at_rre = cnt["s_dv"]

            def rrep(t, at=at_rre):
                t.wait_ge(sem["s_dv"], at)
                t.matmul(pb[7][:, 0:128], repl2[:, :], evf[0:2, 0:128],
                         start=True, stop=True)
                t.drain()
                b.inc(t.nop(), "s_mm")
            blk.tensor(rrep)
            at_rrp = cnt["s_mm"]

            def rnrm(v, at=at_rrp):
                v.wait_ge(sem["s_mm"], at)
                v.tensor_tensor(ebf[:, 0:128], ebf[:, 0:128],
                                pb[7][:, 0:128], ALU.mult)
                v.drain()
                b.inc(v.nop(), "s_dv")
            blk.vector(rnrm)
            at_rn = cnt["s_dv"]

            def rav(t, tt=tt, at=at_rn):
                t.wait_ge(sem["s_dv"], at)
                for bl in range(2):
                    t.drain()
                    for h in range(H):
                        t.matmul(pb[6][64 * (h % 2):64 * (h % 2) + 64,
                                       (h // 2) * 8 + bl * 4:
                                       (h // 2) * 8 + bl * 4 + 4],
                                 vt[64 * bl:64 * bl + 64,
                                    (tt % 2) * D + h * 64:
                                    (tt % 2) * D + h * 64 + 64],
                                 ebf[64 * bl:64 * bl + 64, 4 * h:4 * h + 4],
                                 start=True, stop=True,
                                 tile_position=_tp(64 * bl, 64 * (h % 2)),
                                 skip_group_check=True)
                t.drain()
                b.inc(t.nop(), "s_mm")
            blk.tensor(rav)
            ro_mm_at[tt] = cnt["s_mm"]

            def rlm(v, tt=tt, at=ro_mm_at[tt]):
                v.wait_ge(sem["s_mm"], at)
                for hp in range(16):
                    v.tensor_copy(rof[:, hp * 128 + tt * 8:
                                      hp * 128 + tt * 8 + 8],
                                  pb[6][:, hp * 8: hp * 8 + 8])
                v.drain()
                b.inc(v.nop(), "s_dv")
            blk.vector(rlm)
            ro_dv_at[tt] = cnt["s_dv"]
        b.barrier()
        b.free_tmp("l2qE", "l2qO", "vt", "kEt", "kOt")

        rosb = b.sb_tmp("rosb", [128, 16 * 128], BF16)

        def rocp(v):
            for c in range(16):
                v.tensor_copy(rosb[:, c * 128:(c + 1) * 128],
                              rof[:, c * 128:(c + 1) * 128])
            v.drain()
            b.inc(v.nop(), "s_dv")
        blk.vector(rocp)
        b.barrier()
        b.free_tmp("rof")
        b.tap("ro", rosb[:, :], [128, 16 * 128], BF16)

        load_bias(b_l2t_out, D)
        b.barrier()
        fm_proj_src(w_l2t_out, 0, rosb)
        b.barrier()
        b.free_tmp("rosb")
        b.tap("lmf_final", lmf[:, :], [128, 16 * NL], F32)

        load_bias(b_lm_kv, 2 * D)
        b.barrier()
        gkE = b.sb_tmp("gkE", [128, 8 * 136], BF16)
        gkO = b.sb_tmp("gkO", [128, 8 * 136], BF16)
        gv_tm = b.sb_tmp("gv_tm", [128, D], BF16)
        gk_mm, gk_dv = {}, {}
        gk_mm_entry = cnt["s_mm"]
        for G in range(8):
            buf = G % 2

            def gload(g, G=G, buf=buf):
                if G >= 2:
                    g.wait_ge(sem["s_mm"], gk_mm[G - 2])
                else:
                    g.wait_ge(sem["s_mm"], gk_mm_entry)
                b.inc(load_wslab(g, w_lm_kv, 256 * G, buf),
                      ("s_gld", "s_g2")[buf], 16)
            blk.gpsimd(gload)
            gld_nm = ("s_gld", "s_g2")[buf]
            gld_at = cnt[gld_nm]

            def tmm(t, G=G, buf=buf, gld_nm=gld_nm, gld_at=gld_at):
                t.wait_ge(sem[gld_nm], gld_at)
                if G == 0:
                    b.wait_all(t, ("s_dv",))
                else:
                    t.wait_ge(sem["s_dv"], gk_dv[G - 1])
                for eo in range(2):
                    bank = pb[eo]
                    for kc in range(16):
                        t.matmul(bank[:, 0:128], wsl_eo(buf, kc, eo),
                                 lmfb[:, kc * 128:(kc + 1) * 128],
                                 start=(kc == 0), stop=False)
                    t.matmul(bank[:, 0:128], bias_eo(256 * G, eo),
                             onesb[:, 0:128], start=False, stop=True)
                t.drain()
                b.inc(t.nop(), "s_mm")
            blk.tensor(tmm)
            gk_mm[G] = cnt["s_mm"]

            def vev(v, G=G, at=gk_mm[G]):
                v.wait_ge(sem["s_mm"], at)
                for dst, bank in ((gkE, pb[0]), (gkO, pb[1])):
                    v.memset(dst[:, G * 136 + 128:(G + 1) * 136], 0.0)
                    v.tensor_copy(dst[:, G * 136:G * 136 + 128],
                                  bank[:, 0:128])
                v.drain()
                b.inc(v.nop(), "s_dv")
            blk.vector(vev)
            gk_dv[G] = cnt["s_dv"]
        b.barrier()
        latmaj(w_lm_kv, D, D, gv_tm)
        b.barrier()
        b.free_tmp("lmfb", "lmf")
        b.tap("gkE", gkE[:, :], [128, 8 * 136], BF16)
        b.tap("gv", gv_tm[:, :], [128, D], BF16)
        if STOP_AFTER == "p7":
            _tail(b, y_out, evf); b.st.close(); return b

        xr = b.sb_tmp("xr", [128, 2 * D], F32)
        xrb = b.sb_tmp("xrb", [128, 2 * D], BF16)
        xT_own = b.sb_tmp("xT_own", [128, 16 * TOWN], BF16)
        hg0 = (cnt["s_hld"], cnt["s_h2"])
        gx_dv0 = cnt["s_dv"]
        XP = ("s_hld", "s_h2")

        gx_mm0 = cnt["s_mm"]
        gconv_dv, gevac_dv = {}, {}

        def gxl(sy):
            for tt in range(NOWN):
                if tt >= 2:
                    sy.wait_ge(sem["s_dv"], gconv_dv[tt - 2])
                b.inc(sy.dma_start(out=xr[:, (tt % 2) * D:(tt % 2 + 1) * D],
                                   in_=x_in[tt * 128:(tt + 1) * 128, :]),
                      XP[tt % 2], 16)

        def gxc(v):
            def evac(e):
                for c4 in range(4):
                    gidx = e * 4 + c4
                    v.wait_ge(sem["s_mm"], gx_mm0 + gidx + 1)
                    bank = pb[gidx % 2]
                    for j in range(4):
                        c = c4 * 4 + j
                        v.tensor_copy(
                            xT_own[:, c * TOWN + e * 128:
                                   c * TOWN + (e + 1) * 128],
                            bank[:, j * 128:(j + 1) * 128])
                    v.drain()
                    gevac_dv[gidx] = b.inc(v.nop(), "s_dv")
            for tt in range(NOWN):
                v.wait_ge(sem[XP[tt % 2]], hg0[tt % 2] + 16 * (tt // 2 + 1))
                v.tensor_copy(xrb[:, (tt % 2) * D:(tt % 2 + 1) * D],
                              xr[:, (tt % 2) * D:(tt % 2 + 1) * D])
                v.drain()
                gconv_dv[tt] = b.inc(v.nop(), "s_dv")
                if tt >= 1:
                    evac(tt - 1)
            evac(NOWN - 1)
        blk.vector(gxc)
        blk.sync(gxl)

        def gxt(t):
            for tt in range(NOWN):
                t.wait_ge(sem["s_dv"], gconv_dv[tt])
                for c4 in range(4):
                    gidx = tt * 4 + c4
                    if gidx >= 2:
                        t.wait_ge(sem["s_dv"], gevac_dv[gidx - 2])
                    bank = pb[gidx % 2]
                    for j in range(4):
                        c = c4 * 4 + j
                        t.matmul(bank[:, j * 128:(j + 1) * 128],
                                 xrb[:, (tt % 2) * D + c * 128:
                                     (tt % 2) * D + (c + 1) * 128],
                                 identb[:, :], start=True, stop=True)
                    t.drain()
                    b.inc(t.nop(), "s_mm")
        blk.tensor(gxt)
        b.barrier()
        b.free_tmp("xr", "xrb")

        g_sb = b.sb_tmp("g_sb", [32, TOWN], F32)
        load_bias(b_gate, H)

        def ggl(g):
            b.wait_all(g, ("s_mm",))
            src = w_gate_i.rearrange("(kc p) w -> p kc w", p=128)
            dst = wslab[:, 0:16 * 32].rearrange("p (kc w) -> p kc w", kc=16)
            b.inc(g.dma_start(out=dst, in_=src), "s_gld", 16)
        blk.gpsimd(ggl)
        ggl_at = cnt["s_gld"]

        def gmm(t):
            t.wait_ge(sem["s_gld"], ggl_at)
            for th in range(2):
                bank = pb[4 + th]
                for kc in range(16):
                    t.matmul(bank[0:32, :],
                             wslab[:, kc * 32:(kc + 1) * 32],
                             xT_own[:, kc * TOWN + th * 512:
                                    kc * TOWN + (th + 1) * 512],
                             start=(kc == 0), stop=False)
                t.matmul(bank[0:32, :], bias_r[:, 0:32], onesb[:, 0:512],
                         start=False, stop=True)
            t.drain()
            b.inc(t.nop(), "s_mm")
        blk.tensor(gmm)
        gmm_at = cnt["s_mm"]

        def gsig(s):
            s.wait_ge(sem["s_mm"], gmm_at)
            for th in range(2):
                s.activation(out=g_sb[:, th * 512:(th + 1) * 512],
                             in_=pb[4 + th][0:32, :], func=AF.Sigmoid)
            s.drain()
            b.inc(s.nop(), "s_ac")
        blk.scalar(gsig)
        b.barrier()
        b.free_tmp("xT_own")
        b.tap("gate", g_sb[:, :], [32, TOWN], F32)

        comb = b.sb_tmp("comb", [128, 16 * TOWN], BF16)
        gaterep = b.sb_tmp("gaterep", [32, 16 * 128], F32)

        def p9c(sy):
            b.inc(sy.dma_start(out=gaterep[:, :], in_=c_gaterep), "s_hld", 16)
        blk.sync(p9c)
        b.barrier()
        glo_dv = [cnt["s_dv"]]
        for c in range(16):
            for h in (2 * c, 2 * c + 1):
                G, s4 = h // 4, 32 * (h % 4)
                for w in range(8):
                    def gsc(t, h=h, G=G, s4=s4, w=w, c=c):
                        t.wait_ge(sem["s_dv"], glo_dv[0])
                        if c == 0 and h == 2 * c and w == 0:
                            b.wait_all(t, ("s_ac",))
                        t.drain()
                        t.matmul(pb[0][:, 0:136],
                                 qE[s4:s4 + 32, w * 1024 + G * 128:
                                    w * 1024 + G * 128 + 128],
                                 gkE[s4:s4 + 32, G * 136:G * 136 + 136],
                                 start=True, stop=False,
                                 tile_position=_tp(s4),
                                 skip_group_check=True)
                        t.matmul(pb[0][:, 0:136],
                                 qO[s4:s4 + 32, w * 1024 + G * 128:
                                    w * 1024 + G * 128 + 128],
                                 gkO[s4:s4 + 32, G * 136:G * 136 + 136],
                                 start=False, stop=False,
                                 tile_position=_tp(s4),
                                 skip_group_check=True)
                        t.drain()
                        t.matmul(pb[0][:, 0:129],
                                 glo_u[:, w * 128:(w + 1) * 128],
                                 glo_w[:, :], start=False, stop=True,
                                 skip_group_check=True)
                        t.drain()
                        b.inc(t.nop(), "s_mm")
                    blk.tensor(gsc)
                    at1 = cnt["s_mm"]

                    def gsm(v, at=at1):
                        v.wait_ge(sem["s_mm"], at)
                        v.reduce_max(nmax[:, 0:1], pb[0][:, 0:129], AXL,
                                     negate=True)
                        v.drain()
                        b.inc(v.nop(), "s_dv")
                    blk.vector(gsm)
                    at2 = cnt["s_dv"]

                    def gex(s, at=at2):
                        s.wait_ge(sem["s_dv"], at)
                        s.activation(out=ebf[:, 0:129], in_=pb[0][:, 0:129],
                                     func=AF.Exp, bias=nmax[:, 0:1],
                                     accum_out=mv2[:, 0:1])
                        s.drain()
                        b.inc(s.nop(), "s_ac")
                    blk.scalar(gex)
                    at3 = cnt["s_ac"]

                    def gnm(v, at=at3):
                        v.wait_ge(sem["s_ac"], at)
                        v.reciprocal(mv2[:, 1:2], mv2[:, 0:1])
                        v.drain()
                        v.tensor_scalar(ebf[:, 0:128], ebf[:, 0:128],
                                        mv2[:, 1:2], None, ALU.mult)
                        v.drain()
                        b.inc(v.nop(), "s_dv")
                    blk.vector(gnm)
                    at4 = cnt["s_dv"]

                    def gtr(t, at=at4):
                        t.wait_ge(sem["s_dv"], at)
                        t.matmul(pb[1][:, 0:128], ebf[:, 0:128], identb[:, :],
                                 start=True, stop=True)
                        t.drain()
                        b.inc(t.nop(), "s_mm")
                    blk.tensor(gtr)
                    at5 = cnt["s_mm"]

                    def gev(v, at=at5):
                        v.wait_ge(sem["s_mm"], at)
                        v.tensor_copy(eb2[:, 0:128], pb[1][:, 0:128])
                        v.drain()
                        b.inc(v.nop(), "s_dv")
                    blk.vector(gev)
                    at6 = cnt["s_dv"]

                    def gav(t, h=h, w=w, at=at6):
                        t.wait_ge(sem["s_dv"], at)
                        t.matmul(pb[4 + w // 4][64 * (h % 2):64 * (h % 2) + 64,
                                                (w % 4) * 128:(w % 4 + 1) * 128],
                                 gv_tm[:, h * 64:(h + 1) * 64],
                                 eb2[:, 0:128], start=True, stop=True,
                                 tile_position=_tp(0, 64 * (h % 2)),
                                 skip_group_check=True)
                        t.drain()
                        b.inc(t.nop(), "s_mm")
                    blk.tensor(gav)
                    glo_dv = [cnt["s_dv"]]
            at_og = cnt["s_mm"]

            def grep(t, c=c, at=at_og):
                t.wait_ge(sem["s_mm"], at)
                for th in range(2):
                    t.matmul(pb[6 + th][:, :],
                             gaterep[:, c * 128:(c + 1) * 128],
                             g_sb[:, th * 512:(th + 1) * 512],
                             start=True, stop=True)
                t.drain()
                b.inc(t.nop(), "s_mm")
            blk.tensor(grep)
            at_gr = cnt["s_mm"]

            def gcmb(v, c=c, at=at_gr):
                v.wait_ge(sem["s_mm"], at)
                for th in range(2):
                    og = pb[4 + th]
                    gr_ = pb[6 + th]
                    olc = OL[:, c * 1024 + th * 512: c * 1024 + (th + 1) * 512]
                    t0 = evf[:, 0:512]
                    t1 = evf[:, 512:1024]
                    v.tensor_tensor(t0, og[:, :], olc, ALU.subtract)
                    v.drain()
                    v.tensor_tensor(t1, t0, gr_[:, :], ALU.mult)
                    v.drain()
                    v.tensor_tensor(comb[:, c * 1024 + th * 512:
                                         c * 1024 + (th + 1) * 512],
                                    t1, olc, ALU.add)
                    v.drain()
                b.inc(v.nop(), "s_dv")
            blk.vector(gcmb)
            glo_dv = [cnt["s_dv"]]
        b.barrier()
        b.free_tmp("qE", "qO", "OL", "gkE", "gkO", "gv_tm", "g_sb")
        b.tap("comb", comb[:, :], [128, 16 * TOWN], BF16)

        load_bias(b_out, D)
        b.barrier()
        so_mm, so_dv, so_st = {}, {}, {}
        so_mm_entry = cnt["s_mm"]
        for rgp in range(0, 16, 2):
            buf = (rgp // 2) % 2

            def gload(g, rgp=rgp, buf=buf):
                if rgp >= 4:
                    g.wait_ge(sem["s_mm"], so_mm[rgp - 4])
                else:
                    g.wait_ge(sem["s_mm"], so_mm_entry)
                b.inc(load_wslab(g, w_out_i, rgp * 128, buf),
                      ("s_gld", "s_g2")[buf], 16)
            blk.gpsimd(gload)
            gld_nm = ("s_gld", "s_g2")[buf]
            gld_at = cnt[gld_nm]

            def tmm(t, rgp=rgp, buf=buf, gld_nm=gld_nm, gld_at=gld_at):
                t.wait_ge(sem[gld_nm], gld_at)
                if rgp == 0:
                    b.wait_all(t, ("s_dv",))
                else:
                    t.wait_ge(sem["s_dv"], so_dv[rgp - 2])
                for j in range(2):
                    for th in range(2):
                        bank = pb[2 * j + th]
                        for kc in range(16):
                            t.matmul(bank[:, :],
                                     wsl(buf, kc, j * 128, 128),
                                     comb[:, kc * TOWN + th * 512:
                                          kc * TOWN + (th + 1) * 512],
                                     start=(kc == 0), stop=False)
                        t.matmul(bank[:, :],
                                 bias_r[:, (rgp + j) * 128:
                                        (rgp + j + 1) * 128],
                                 onesb[:, 0:512], start=False, stop=True)
                t.drain()
                b.inc(t.nop(), "s_mm")
            blk.tensor(tmm)
            so_mm[rgp] = cnt["s_mm"]

            def vev(v, rgp=rgp, at=so_mm[rgp]):
                v.wait_ge(sem["s_mm"], at)
                if rgp >= 2:
                    nm2, at2 = so_st[rgp - 2]
                    v.wait_ge(sem[nm2], at2)
                for j in range(2):
                    for th in range(2):
                        v.tensor_copy(evf[:, (2 * j + th) * 512:
                                          (2 * j + th + 1) * 512],
                                      pb[2 * j + th][:, :])
                v.drain()
                b.inc(v.nop(), "s_dv")
            blk.vector(vev)
            so_dv[rgp] = cnt["s_dv"]

            st_nm = ("s_st", "s_s2")[(rgp // 2) % 2]

            def sspill(sy, rgp=rgp, st_nm=st_nm, at=so_dv[rgp]):
                sy.wait_ge(sem["s_dv"], at)
                for j in range(2):
                    b.inc(sy.dma_start(
                        out=s_hbm[:, (rgp + j) * TOWN:(rgp + j + 1) * TOWN],
                        in_=evf[:, 2 * j * 512:(2 * j + 2) * 512]),
                        st_nm, 16)
            blk.sync(sspill)
            so_st[rgp] = (st_nm, cnt[st_nm])
        b.barrier()
        b.free_tmp("comb", "gaterep")
        if STOP_AFTER == "p9":
            _tail(b, y_out, evf); b.st.close(); return b

        b.free_tmp("wslab")

        wdTsb = b.sb_tmp("wdTsb", [128, 2 * D], BF16)
        gsb2 = b.sb_tmp("gsb2", [128, 2 * D], BF16)
        gb_dv, gb_st = {}, {}
        for i in range(64):
            buf = i % 2

            def gx(sy, i=i, buf=buf):
                if i > 0:
                    sy.wait_ge(sem["s_st"], gb_st[i - 1])
                else:
                    b.wait_all(sy)
                b.inc(sy.dma_start(out=wdTsb[:, buf * D:(buf + 1) * D],
                                   in_=wdnbf[:, i * 128:(i + 1) * 128],
                                   transpose=True), "s_hld", 16)
                sy.wait_ge(sem["s_hld"], cnt["s_hld"])
            blk.sync(gx)
            at_x = cnt["s_hld"]

            def gm(t, i=i, buf=buf, at=at_x):
                t.wait_ge(sem["s_hld"], at)
                if i >= 2:
                    t.wait_ge(sem["s_dv"], gb_dv[i - 2])
                for q4 in range(4):
                    t.matmul(pb[q4][:, :], h128b[:, :],
                             wdTsb[:, buf * D + q4 * 512:
                                   buf * D + (q4 + 1) * 512],
                             start=True, stop=True)
                t.drain()
                b.inc(t.nop(), "s_mm")
            blk.tensor(gm)
            at_m = cnt["s_mm"]

            def ge(v, i=i, buf=buf, at=at_m):
                v.wait_ge(sem["s_mm"], at)
                for q4 in range(4):
                    v.tensor_copy(gsb2[:, buf * D + q4 * 512:
                                       buf * D + (q4 + 1) * 512],
                                  pb[q4][:, :])
                v.drain()
                b.inc(v.nop(), "s_dv")
            blk.vector(ge)
            gb_dv[i] = cnt["s_dv"]

            def gst(sy, i=i, buf=buf, at=gb_dv[i]):
                sy.wait_ge(sem["s_dv"], at)
                b.inc(sy.dma_start(out=G_hbm[i * 128:(i + 1) * 128, :],
                                   in_=gsb2[:, buf * D:(buf + 1) * D]),
                      "s_st", 16)
            blk.sync(gst)
            gb_st[i] = cnt["s_st"]
        b.barrier()
        b.free_tmp("wdTsb", "gsb2")

        xr2 = b.sb_tmp("xr2", [128, 2 * D], F32)
        xnb2 = b.sb_tmp("xnb2", [128, 2 * D], BF16)
        xnTo = b.sb_tmp("xnTo", [128, 16 * TOWN], BF16)
        hr0 = (cnt["s_hld"], cnt["s_h2"])
        fr_dv0 = cnt["s_dv"]
        XP2 = ("s_hld", "s_h2")

        fr_mm0 = cnt["s_mm"]
        fconv_dv, fevac_dv = {}, {}

        def fxl(sy):
            for tt in range(NOWN):
                if tt >= 2:
                    sy.wait_ge(sem["s_dv"], fconv_dv[tt - 2])
                b.inc(sy.dma_start(out=xr2[:, (tt % 2) * D:(tt % 2 + 1) * D],
                                   in_=x_in[tt * 128:(tt + 1) * 128, :]),
                      XP2[tt % 2], 16)

        def fxv(v):
            def evac(e):
                for c4 in range(4):
                    gidx = e * 4 + c4
                    v.wait_ge(sem["s_mm"], fr_mm0 + gidx + 1)
                    bank = pb[gidx % 2]
                    for j in range(4):
                        c = c4 * 4 + j
                        v.tensor_copy(
                            xnTo[:, c * TOWN + e * 128:
                                 c * TOWN + (e + 1) * 128],
                            bank[:, j * 128:(j + 1) * 128])
                    v.drain()
                    fevac_dv[gidx] = b.inc(v.nop(), "s_dv")
            for tt in range(NOWN):
                v.wait_ge(sem[XP2[tt % 2]], hr0[tt % 2] + 16 * (tt // 2 + 1))
                v.tensor_scalar(xnb2[:, (tt % 2) * D:(tt % 2 + 1) * D],
                                xr2[:, (tt % 2) * D:(tt % 2 + 1) * D],
                                ln_mean[:, tt:tt + 1], ln_rstd[:, tt:tt + 1],
                                ALU.subtract, ALU.mult)
                v.drain()
                fconv_dv[tt] = b.inc(v.nop(), "s_dv")
                if tt >= 1:
                    evac(tt - 1)
            evac(NOWN - 1)
        blk.vector(fxv)
        blk.sync(fxl)

        def fxt(t):
            for tt in range(NOWN):
                t.wait_ge(sem["s_dv"], fconv_dv[tt])
                for c4 in range(4):
                    gidx = tt * 4 + c4
                    if gidx >= 2:
                        t.wait_ge(sem["s_dv"], fevac_dv[gidx - 2])
                    bank = pb[gidx % 2]
                    for j in range(4):
                        c = c4 * 4 + j
                        t.matmul(bank[:, j * 128:(j + 1) * 128],
                                 xnb2[:, (tt % 2) * D + c * 128:
                                      (tt % 2) * D + (c + 1) * 128],
                                 identb[:, :], start=True, stop=True)
                    t.drain()
                    b.inc(t.nop(), "s_mm")
        blk.tensor(fxt)
        b.barrier()
        b.free_tmp("xr2", "xnb2")

        ust = b.sb_tmp("ust", [128, 16 * TOWN], F32)

        def fu(v):
            for c in range(16):
                v.tensor_scalar(xnTo[:, c * TOWN:(c + 1) * TOWN],
                                xnTo[:, c * TOWN:(c + 1) * TOWN],
                                su_c[:, c:c + 1], None, ALU.mult)
            v.drain()
            b.inc(v.nop(), "s_dv")
        blk.vector(fu)
        at_u = cnt["s_dv"]
        fw1_mm0 = cnt["s_mm"]
        fw1_dv0 = cnt["s_dv"]

        def fw1(t, at=at_u):
            t.wait_ge(sem["s_dv"], at)
            for grp in range(8):
                if grp >= 2:
                    t.wait_ge(sem["s_dv"], fw1_dv0 + grp - 1)
                for k4 in range(4):
                    idx = grp * 4 + k4
                    c, th = idx // 2, idx % 2
                    t.matmul(pb[(grp % 2) * 4 + k4][:, :], h128b[:, :],
                             xnTo[:, c * TOWN + th * 512:
                                  c * TOWN + (th + 1) * 512],
                             start=True, stop=True)
                t.drain()
                b.inc(t.nop(), "s_mm")
        blk.tensor(fw1)

        def fw1e(v):
            for grp in range(8):
                v.wait_ge(sem["s_mm"], fw1_mm0 + grp + 1)
                for k4 in range(4):
                    idx = grp * 4 + k4
                    c, th = idx // 2, idx % 2
                    v.tensor_copy(ust[:, c * TOWN + th * 512:
                                      c * TOWN + (th + 1) * 512],
                                  pb[(grp % 2) * 4 + k4][:, :])
                v.drain()
                b.inc(v.nop(), "s_dv")
        blk.vector(fw1e)
        b.barrier()
        b.free_tmp("xnTo")
        x1b = b.sb_tmp("x1b", [128, 16 * TOWN], BF16)

        def fbf(v):
            tmp = evf[:, 0:1024]
            for st_ in range(4):
                stp = 1 << st_
                last = (st_ == 3)
                for base in range(0, 16, 2 * stp):
                    for off in range(stp):
                        a = (base + off) * TOWN
                        bb = (base + off + stp) * TOWN
                        av = ust[:, a:a + TOWN]
                        bv = ust[:, bb:bb + TOWN]
                        v.tensor_tensor(tmp, av, bv, ALU.subtract)
                        v.drain()
                        if last:
                            v.tensor_tensor(x1b[:, a:a + TOWN], av, bv,
                                            ALU.add)
                            v.drain()
                            v.tensor_copy(x1b[:, bb:bb + TOWN], tmp)
                        else:
                            v.tensor_tensor(av, av, bv, ALU.add)
                            v.drain()
                            v.tensor_copy(bv, tmp)
                        v.drain()
            b.inc(v.nop(), "s_dv")
        blk.vector(fbf)
        b.barrier()
        b.free_tmp("ust")
        b.tap("x1", x1b[:, :], [128, 16 * TOWN], BF16)

        hball = b.sb_tmp("hball", [128, 64 * TOWN], BF16)
        wupT = b.sb_tmp("wupT", [128, 2048], BF16)
        wgtT = b.sb_tmp("wgtT", [128, 2048], BF16)
        ff_mm = [cnt["s_mm"]]
        ff_dv = [cnt["s_dv"]]
        for g in range(64):
            buf = g % 2

            def fx(sy, g=g, buf=buf):
                if g > 0:
                    sy.wait_ge(sem["s_mm"], ff_mm[0])
                for wt_sb, wbf_h in ((wupT, wupbf), (wgtT, wgtbf)):
                    for dlt in range(2):
                        dst = wt_sb.ap().rearrange("p (m l) -> p m l", m=16)[
                            :, :, dlt * 64:(dlt + 1) * 64]
                        b.inc(sy.dma_start(
                            out=dst,
                            in_=wbf_h[2 * g + dlt::128, :],
                            transpose=True), "s_hld", 16)
                        sy.wait_ge(sem["s_hld"], cnt["s_hld"])
            blk.sync(fx)
            at_w = cnt["s_hld"]

            def fmm(t, g=g, buf=buf, at=at_w):
                t.wait_ge(sem["s_hld"], at)
                if g == 0:
                    b.wait_all(t, ("s_dv",))
                else:
                    t.wait_ge(sem["s_dv"], ff_dv[0])
                for th in range(2):
                    for ug, wt in ((0, wupT), (1, wgtT)):
                        bank = pb[2 * th + ug]
                        for kc in range(16):
                            t.matmul(bank[:, :],
                                     wt[:, kc * 128:(kc + 1) * 128],
                                     x1b[:, kc * TOWN + th * 512:
                                         kc * TOWN + (th + 1) * 512],
                                     start=(kc == 0), stop=(kc == 15))
                t.drain()
                b.inc(t.nop(), "s_mm")
            blk.tensor(fmm)
            at_upg = cnt["s_mm"]

            def fsil(s, g=g, at=at_upg):
                s.wait_ge(sem["s_mm"], at)
                for th in range(2):
                    s.activation(out=ebf[:, th * 512:(th + 1) * 512],
                                 in_=pb[2 * th + 1][:, :], func=AF.Silu,
                                 scale=sgate[:, g:g + 1])
                s.drain()
                b.inc(s.nop(), "s_ac")
            blk.scalar(fsil)
            at_sil = cnt["s_ac"]

            def fstt(v, g=g, at=at_sil):
                v.wait_ge(sem["s_ac"], at)
                for th in range(2):
                    v.scalar_tensor_tensor(
                        hball[:, g * TOWN + th * 512: g * TOWN + (th + 1) * 512],
                        pb[2 * th][:, :], sstar[:, g:g + 1],
                        ebf[:, th * 512:(th + 1) * 512], ALU.mult, ALU.mult)
                v.drain()
                b.inc(v.nop(), "s_dv")
            blk.vector(fstt)
            at_h = cnt["s_dv"]

            def fbd(t, g=g, at=at_h):
                t.wait_ge(sem["s_dv"], at)
                for th in range(2):
                    t.matmul(pb[4 + th][:, :], h64bdb[:, :],
                             hball[:, g * TOWN + th * 512:
                                   g * TOWN + (th + 1) * 512],
                             start=True, stop=True)
                t.drain()
                b.inc(t.nop(), "s_mm")
            if not SKIP_BD:
                blk.tensor(fbd)
                ff_mm = [cnt["s_mm"]]

                def fx2(v, g=g, at=ff_mm[0]):
                    v.wait_ge(sem["s_mm"], at)
                    for th in range(2):
                        v.tensor_copy(hball[:, g * TOWN + th * 512:
                                            g * TOWN + (th + 1) * 512],
                                      pb[4 + th][:, :])
                    v.drain()
                    b.inc(v.nop(), "s_dv")
                blk.vector(fx2)
                ff_dv = [cnt["s_dv"]]
            else:
                def fx2n(v, g=g, at=at_h):
                    v.wait_ge(sem["s_dv"], at)
                    b.inc(v.nop(), "s_dv")
                blk.vector(fx2n)
                ff_mm = [cnt["s_mm"]]
                ff_dv = [cnt["s_dv"]]
        b.barrier()
        b.free_tmp("x1b", "wupT", "wgtT")
        b.tap("x2", hball[:, :], [128, 64 * TOWN], BF16)

        gtl = b.sb_tmp("gtl", [128, 2 * 512], BF16)
        gout = b.sb_tmp("gout", [128, 4096], F32)
        dn_mm = [cnt["s_mm"]]
        for hg in range(4):
            for g in range(64):
                buf = g % 2

                def dgl(sy, hg=hg, g=g, buf=buf):
                    if (hg, g) != (0, 0):
                        sy.wait_ge(sem["s_mm"], dn_mm[0])
                    src = G_hbm.rearrange("(i dd) c -> dd i c", dd=128)[
                        2 * g:2 * g + 2, :, hg * 512:(hg + 1) * 512]
                    b.inc(sy.dma_start(out=gtl[:, buf * 512:(buf + 1) * 512],
                                       in_=src), ("s_hld", "s_h2")[buf], 16)
                blk.sync(dgl)
                g_nm = ("s_hld", "s_h2")[buf]
                at_g = cnt[g_nm]

                def dmm(t, hg=hg, g=g, buf=buf, nm=g_nm, at=at_g):
                    t.wait_ge(sem[nm], at)
                    if hg > 0 and g == 0:
                        t.wait_ge(sem["s_ac"], dn_ac[0])
                    for hc in range(4):
                        for th in range(2):
                            t.matmul(pb[hc * 2 + th][:, :],
                                     gtl[:, buf * 512 + hc * 128:
                                         buf * 512 + (hc + 1) * 128],
                                     hball[:, g * TOWN + th * 512:
                                           g * TOWN + (th + 1) * 512],
                                     start=(g == 0), stop=(g == 63),
                                     skip_group_check=True)
                    t.drain()
                    b.inc(t.nop(), "s_mm")
                blk.tensor(dmm)
                dn_mm = [cnt["s_mm"]]

            def dsil(s, hg=hg, at=dn_mm[0]):
                s.wait_ge(sem["s_mm"], at)
                if hg > 0:
                    s.wait_ge(sem["s_st"], dn_st[0])
                for hc in range(4):
                    c = hg * 4 + hc
                    for th in range(2):
                        s.activation(out=gout[:, hc * 1024 + th * 512:
                                             hc * 1024 + (th + 1) * 512],
                                     in_=pb[hc * 2 + th][:, :], func=AF.Silu,
                                     scale=wsdp_c[:, c:c + 1])
                s.drain()
                b.inc(s.nop(), "s_ac")
            blk.scalar(dsil)
            dn_ac = [cnt["s_ac"]]

            def dsp(sy, hg=hg, at=dn_ac[0]):
                sy.wait_ge(sem["s_ac"], at)
                b.inc(sy.dma_start(out=g_hbm[:, hg * 4096:(hg + 1) * 4096],
                                   in_=gout[:, :]), "s_st", 16)
            blk.sync(dsp)
            dn_st = [cnt["s_st"]]
        b.barrier()
        b.free_tmp("hball", "gtl", "gout")
        if STOP_AFTER == "p10":
            _tail(b, y_out, evf); b.st.close(); return b

        sfm = b.sb_tmp("sfm", [128, 16 * 128], F32)
        gfm = b.sb_tmp("gfm", [128, 16 * 128], F32)
        stm = b.sb_tmp("stm", [128, D], F32)
        gtm = b.sb_tmp("gtm", [128, D], F32)
        xfin = b.sb_tmp("xfin", [128, D], F32)
        ofin = b.sb_tmp("ofin", [128, D], F32)
        fin_dv = [cnt["s_dv"]]
        fin_st = [cnt["s_st"]]
        for tt in range(NOWN):
            def frl(sy, tt=tt):
                sy.wait_ge(sem["s_dv"], fin_dv[0])
                sy.wait_ge(sem["s_st"], fin_st[0])
                b.inc(sy.dma_start(out=xfin[:, :],
                                   in_=x_in[tt * 128:(tt + 1) * 128, :]),
                      "s_hld", 16)
                b.inc(sy.dma_start(
                    out=sfm.ap().rearrange("p (c w) -> p c w", c=16),
                    in_=s_hbm.rearrange("p (c w) -> p c w", c=16)[
                        :, :, tt * 128:(tt + 1) * 128]), "s_hld", 16)
                b.inc(sy.dma_start(
                    out=gfm.ap().rearrange("p (c w) -> p c w", c=16),
                    in_=g_hbm.rearrange("p (c w) -> p c w", c=16)[
                        :, :, tt * 128:(tt + 1) * 128]), "s_hld", 16)
            blk.sync(frl)
            at_r = cnt["s_hld"]

            def ftr(t, tt=tt, at=at_r):
                t.wait_ge(sem["s_hld"], at)
                if tt > 0:
                    t.wait_ge(sem["s_dv"], fin_dv[0])
                for qq, src in ((0, sfm), (4, gfm)):
                    for c4 in range(4):
                        bank = pb[qq + c4]
                        for j in range(4):
                            c = c4 * 4 + j
                            t.matmul(bank[:, j * 128:(j + 1) * 128],
                                     src[:, c * 128:(c + 1) * 128],
                                     identf[:, :], start=True, stop=True)
                t.drain()
                b.inc(t.nop(), "s_mm")
            blk.tensor(ftr)
            at_t = cnt["s_mm"]

            def fev(v, tt=tt, at=at_t):
                v.wait_ge(sem["s_mm"], at)
                for c4 in range(4):
                    v.tensor_copy(stm[:, c4 * 512:(c4 + 1) * 512],
                                  pb[c4][:, :])
                    v.tensor_copy(gtm[:, c4 * 512:(c4 + 1) * 512],
                                  pb[4 + c4][:, :])
                v.drain()
                v.tensor_tensor(ofin[:, :], stm[:, :], gtm[:, :], ALU.mult)
                v.drain()
                v.tensor_tensor(ofin[:, :], ofin[:, :], xfin[:, :], ALU.add)
                v.drain()
                b.inc(v.nop(), "s_dv")
            blk.vector(fev)
            fin_dv = [cnt["s_dv"]]

            def fst(sy, tt=tt, at=fin_dv[0]):
                sy.wait_ge(sem["s_dv"], at)
                b.inc(sy.dma_start(out=y_out[tt * 128:(tt + 1) * 128, :],
                                   in_=ofin[:, :]), "s_st", 16)
            blk.sync(fst)
            fin_st = [cnt["s_st"]]

        def waitout(sy):
            sy.wait_ge(sem["s_st"], cnt["s_st"])
        blk.sync(waitout)

    b.st.close()
    return b


def _tail(b, y_out, some):
    b.barrier()

    def fin(v):
        v.memset(some[:, 0:D], 0.0)
        v.drain()
        b.inc(v.nop(), "s_dv")
    b.blk.vector(fin)
    at = b.cnt["s_dv"]

    def fin2(sy):
        sy.wait_ge(b.sem["s_dv"], at)
        for tt in range(TOWN // 128):
            b.inc(sy.dma_start(out=y_out[tt * 128:(tt + 1) * 128, :],
                               in_=some[:, 0:D]), "s_st", 16)
        sy.wait_ge(b.sem["s_st"], b.cnt["s_st"])
    b.blk.sync(fin2)


def make_in_map(inp, core):
    bidx, half = core // 2, core % 2
    x = np.ascontiguousarray(inp["L0_attn_x"])[bidx]
    xr = np.ascontiguousarray(
        np.concatenate([x[half * TOWN:], x[:half * TOWN]], axis=0))
    c = _host_consts(half)

    def perm(vec):
        v = np.asarray(vec, dtype=np.float32).reshape(64, 128)
        out = np.empty((128, 64), dtype=np.float32)
        for g in range(64):
            for dlt in range(2):
                out[dlt * 64:(dlt + 1) * 64, g] = v[:, 2 * g + dlt]
        return out

    m = dict(
        x=xr,
        w_qkv=inp["L0_attn_w_qkv"], b_qkv=inp["L0_attn_b_qkv"],
        pool_q=inp["L0_attn_pool_q"],
        w_lm_qkv=inp["L0_attn_w_lm_qkv"], b_lm_qkv=inp["L0_attn_b_lm_qkv"],
        w_lm_out=inp["L0_attn_w_lm_out"], b_lm_out=inp["L0_attn_b_lm_out"],
        w_l2t_q=inp["L0_attn_w_l2t_q"], b_l2t_q=inp["L0_attn_b_l2t_q"],
        w_l2t_out=inp["L0_attn_w_l2t_out"], b_l2t_out=inp["L0_attn_b_l2t_out"],
        w_lm_kv=inp["L0_attn_w_lm_kv"], b_lm_kv=inp["L0_attn_b_lm_kv"],
        w_out=inp["L0_attn_w_out"], b_out=inp["L0_attn_b_out"],
        w_gate=inp["L0_attn_w_gate"], b_gate=inp["L0_attn_b_gate"],
        su_ug=inp["L0_ffn_SU_ug"],
        ws_dp=inp["L0_ffn_Wscale_dp"],
        wu=inp["L0_ffn_W_up"], wg=inp["L0_ffn_W_gate"], wd=inp["L0_ffn_W_down"],
        su_dp_p=perm(inp["L0_ffn_SU_dp"]),
        ws_up_p=perm(inp["L0_ffn_Wscale_ug"][:INTER]),
        ws_gt_p=perm(inp["L0_ffn_Wscale_ug"][INTER:]),
        **c,
    )
    return {k: np.ascontiguousarray(v) for k, v in m.items()}


def kernel(**inp):
    prog = build_program()
    in_maps = [make_in_map(inp, core) for core in range(8)]
    res = run_bass_kernel_spmd(prog.nc, in_maps, core_ids=list(range(8)))
    out = np.empty((B, L, D), dtype=np.float32)
    for core in range(8):
        bidx, half = core // 2, core % 2
        y = np.asarray(res.results[core]["y"])
        out[bidx, half * TOWN:(half + 1) * TOWN, :] = y
    return out


# version 04
# speedup vs baseline: 1.1857x; 1.1857x over previous
import contextlib

import numpy as np
import ml_dtypes

import concourse.bass as bass
import concourse.mybir as mybir
from concourse.bass_utils import run_bass_kernel_spmd

F32 = mybir.dt.float32
BF16 = mybir.dt.bfloat16
AF = mybir.ActivationFunctionType
ALU = mybir.AluOpType
AXL = mybir.AxisListType.X

D = 2048
H = 32
M = 4
S = 64
B = 4
L = 2048
NB = L // S
NL = NB * M
TOWN = 1024
NTILE = 16
NOWN = 8
INTER = 8192
BIG = 60.0
DUM = -25.0

bf16 = ml_dtypes.bfloat16

DEBUG_TAPS = ()
STOP_AFTER = None
SKIP_BD = False
P6_NO_LOCAL = False
P6_NO_POOL = False


def _had(n):
    h = np.array([[1.0]])
    while h.shape[0] < n:
        h = np.block([[h, h], [h, -h]])
    return h


def _host_consts(half):
    c = {}
    c["c_ident"] = np.eye(128).astype(np.float32).astype(bf16)
    c["c_identf"] = np.eye(128).astype(np.float32)
    c["c_h128"] = _had(128).astype(np.float32).astype(bf16)
    h64 = _had(64)
    bd = np.zeros((128, 128))
    bd[:64, :64] = h64
    bd[64:, 64:] = h64
    c["c_h64bd"] = bd.astype(np.float32).astype(bf16)
    pos = (np.arange(L) + half * TOWN) % L
    inv = 1.0 / 10000.0 ** (np.arange(0, 64, 2, dtype=np.float64) / 64)
    fr = pos[None, :] * inv[np.arange(128) % 32][:, None]
    c["c_cosq"] = (0.125 * np.cos(fr[:, :TOWN])).astype(np.float32)
    c["c_sinq"] = (0.125 * np.sin(fr[:, :TOWN])).astype(np.float32)
    c["c_cos"] = np.cos(fr).astype(np.float32)
    c["c_sin"] = np.sin(fr).astype(np.float32)
    u2 = np.zeros((67, 128)); w2 = np.zeros((67, 128))
    p = np.arange(128)
    u2[0] = 1.0;        w2[0] = -BIG
    u2[1] = (p < 64);   w2[1] = BIG * (p < 64)
    u2[2] = (p >= 64);  w2[2] = BIG * (p >= 64)
    for r in range(1, 64):
        u2[2 + r] = (p % 64) < r
        w2[2 + r] = -BIG * ((p % 64) == r)
    c["c_lau"] = u2.astype(np.float32).astype(bf16)
    c["c_law"] = w2.astype(np.float32).astype(bf16)
    tblk = (np.arange(NB) + half * 16) % NB
    blk_l = tblk[np.arange(NL) // M]
    ula = np.zeros((32, NL)); wla = np.zeros((32, NL))
    for r in range(32):
        ula[r] = (blk_l < r)
        wla[r] = -8.0 * BIG * (blk_l == r)
    c["c_lat_u"] = ula.astype(np.float32).astype(bf16)
    c["c_lat_w"] = wla.astype(np.float32).astype(bf16)
    blk_q = tblk[np.arange(TOWN) // S]
    ug = np.zeros((33, TOWN)); wg = np.zeros((33, NL + 1))
    for r in range(NB):
        ug[r] = (blk_q == r)
        wg[r, :NL] = -BIG * (blk_l >= r)
    ug[32] = 1.0
    wg[32, NL] = DUM
    c["c_glo_u"] = ug.astype(np.float32).astype(bf16)
    c["c_glo_w"] = wg.astype(np.float32).astype(bf16)
    c["c_ones"] = np.ones((1, 512), dtype=np.float32).astype(bf16)
    bd1 = np.zeros((128, 2)); bd1[:64, 0] = 1.0; bd1[64:, 1] = 1.0
    c["c_bdones"] = bd1.astype(np.float32).astype(bf16)
    rp = np.zeros((2, 128)); rp[0, :64] = 1.0; rp[1, 64:] = 1.0
    c["c_repl2"] = rp.astype(np.float32)
    gr = np.zeros((32, 16 * 128))
    for ch in range(16):
        gr[2 * ch, ch * 128: ch * 128 + 64] = 1.0
        gr[2 * ch + 1, ch * 128 + 64: ch * 128 + 128] = 1.0
    c["c_gaterep"] = gr.astype(np.float32)
    return c


class Build:
    def __init__(self):
        self.nc = bass.Bass("TRN2", target_bir_lowering=False)
        self.st = contextlib.ExitStack()
        self.cnt = {}
        self.sem = {}
        for name in ("s_gld", "s_g2", "s_pc", "s_hld", "s_h2", "s_mm", "s_dv",
                     "s_ac", "s_st", "s_s2"):
            self.sem[name] = self.st.enter_context(self.nc.semaphore(name))
            self.cnt[name] = 0
        self._mem = {}
        self.blk = None

    def sb(self, name, shape, dtype):
        return self.st.enter_context(self.nc.sbuf_tensor(name, shape, dtype))

    def init_arena(self, nbytes):
        self.arena = self.st.enter_context(
            self.nc.sbuf_tensor("arena", [128, nbytes], mybir.dt.uint8))
        self.arena_base = self.nc.lookup_mloc(self.arena).addr
        self.free_list = [(0, nbytes)]
        self.blocks = {}

    def sb_tmp(self, name, shape, dtype):
        per_part = 1
        for s in shape[1:]:
            per_part *= s
        size = (per_part * mybir.dt.size(dtype) + 31) // 32 * 32
        for i, (off, sz) in enumerate(self.free_list):
            if sz >= size:
                if sz == size:
                    self.free_list.pop(i)
                else:
                    self.free_list[i] = (off + size, sz - size)
                break
        else:
            raise MemoryError(f"arena OOM for {name} ({size} B/partition); "
                              f"free={self.free_list}")
        h = self.nc.virtual_sbuf_tensor(name, shape, dtype)
        self.nc._tensor(h.name, h.shape, h.dtype, type="SB")
        mloc = self.nc.lookup_mloc(h)
        mloc.allocated = True
        mloc.addr = self.arena_base + off
        self.blocks[name] = (off, size)
        return h

    def free_tmp(self, *names):
        for name in names:
            off, size = self.blocks.pop(name)
            self.free_list.append((off, size))
        self.free_list.sort()
        out = []
        for off, sz in self.free_list:
            if out and out[-1][0] + out[-1][1] == off:
                out[-1] = (out[-1][0], out[-1][1] + sz)
            else:
                out.append((off, sz))
        self.free_list = [tuple(x) for x in out]

    def psum(self, name, shape):
        return self.st.enter_context(self.nc.psum_tensor(name, shape, F32))

    def inc(self, instr, name, k=1):
        instr.then_inc(self.sem[name], k)
        self.cnt[name] += k
        return self.cnt[name]

    def wait_all(self, eng, names=None):
        for n in (names or list(self.sem)):
            if self.cnt[n]:
                eng.wait_ge(self.sem[n], self.cnt[n])

    def barrier(self):
        blk = self.blk
        snap = dict(self.cnt)
        for engf in (blk.tensor, blk.vector, blk.scalar, blk.sync, blk.gpsimd):
            def f(e, snap=snap):
                for n, v in snap.items():
                    if v:
                        e.wait_ge(self.sem[n], v)
            engf(f)

    def tap(self, name, src_ap, shape, dtype):
        if name not in DEBUG_TAPS:
            return
        t = self.nc.dram_tensor("tap_" + name, shape, dtype,
                                kind="ExternalOutput").ap()
        self.barrier()

        def f(sy):
            self.inc(sy.dma_start(out=t, in_=src_ap), "s_st", 16)
        self.blk.sync(f)
        self.barrier()


def _tp(lhs_base, out_base=0):
    if lhs_base == 0 and out_base == 0:
        return None
    return (lhs_base, out_base)


def build_program():
    b = Build()
    nc = b.nc

    def din(name, shape, dtype=F32):
        return nc.dram_tensor(name, shape, dtype, kind="ExternalInput").ap()

    x_in = din("x", [L, D])
    w_qkv = din("w_qkv", [D, 3 * D])
    pool_q = din("pool_q", [M, D])
    w_lm_qkv = din("w_lm_qkv", [D, 3 * D])
    w_lm_out = din("w_lm_out", [D, D])
    w_l2t_q = din("w_l2t_q", [D, D])
    w_l2t_out = din("w_l2t_out", [D, D])
    w_lm_kv = din("w_lm_kv", [D, 2 * D])
    w_out_i = din("w_out", [D, D])
    w_gate_i = din("w_gate", [D, H])
    b_qkv = din("b_qkv", [3 * D])
    b_lm_qkv = din("b_lm_qkv", [3 * D])
    b_lm_out = din("b_lm_out", [D])
    b_l2t_q = din("b_l2t_q", [D])
    b_l2t_out = din("b_l2t_out", [D])
    b_lm_kv = din("b_lm_kv", [2 * D])
    b_out = din("b_out", [D])
    b_gate = din("b_gate", [H])
    su_ug = din("su_ug", [D])
    su_dp_p = din("su_dp_p", [128, 64])
    ws_up_p = din("ws_up_p", [128, 64])
    ws_gt_p = din("ws_gt_p", [128, 64])
    ws_dp = din("ws_dp", [D])
    wu_in = din("wu", [INTER, D])
    wg_in = din("wg", [INTER, D])
    wd_in = din("wd", [D, INTER])
    c_ident = din("c_ident", [128, 128], BF16)
    c_identf = din("c_identf", [128, 128])
    c_h128 = din("c_h128", [128, 128], BF16)
    c_h64bd = din("c_h64bd", [128, 128], BF16)
    c_cos = din("c_cos", [128, L])
    c_sin = din("c_sin", [128, L])
    c_cosq = din("c_cosq", [128, TOWN])
    c_sinq = din("c_sinq", [128, TOWN])
    c_lau = din("c_lau", [67, 128], BF16)
    c_law = din("c_law", [67, 128], BF16)
    c_lat_u = din("c_lat_u", [32, NL], BF16)
    c_lat_w = din("c_lat_w", [32, NL], BF16)
    c_glo_u = din("c_glo_u", [33, TOWN], BF16)
    c_glo_w = din("c_glo_w", [33, NL + 1], BF16)
    c_ones = din("c_ones", [1, 512], BF16)
    c_bdones = din("c_bdones", [128, 2], BF16)
    c_repl2 = din("c_repl2", [2, 128])
    c_gaterep = din("c_gaterep", [32, 16 * 128])

    y_out = nc.dram_tensor("y", [TOWN, D], F32, kind="ExternalOutput").ap()

    def dscr(name, shape, dtype):
        return nc.dram_tensor(name, shape, dtype, kind="Internal").ap()

    kE_hbm = dscr("kE_hbm", [128, NTILE * 1024], BF16)
    kO_hbm = dscr("kO_hbm", [128, NTILE * 1024], BF16)
    v_hbm = dscr("v_hbm", [128, NTILE * D], BF16)
    s_hbm = dscr("s_hbm", [128, 16 * TOWN], F32)
    g_hbm = dscr("g_hbm", [128, 16 * TOWN], F32)
    wupbf = dscr("wupbf", [INTER, D], BF16)
    wgtbf = dscr("wgtbf", [INTER, D], BF16)
    wdnbf = dscr("wdnbf", [D, INTER], BF16)
    G_hbm = dscr("G_hbm", [INTER, D], BF16)

    identb = b.sb("identb", [128, 128], BF16)
    identf = b.sb("identf", [128, 128], F32)
    h128b = b.sb("h128b", [128, 128], BF16)
    h64bdb = b.sb("h64bdb", [128, 128], BF16)
    lau = b.sb("lau", [67, 128], BF16)
    law = b.sb("law", [67, 128], BF16)
    lat_u = b.sb("lat_u", [32, NL], BF16)
    lat_w = b.sb("lat_w", [32, NL], BF16)
    glo_u = b.sb("glo_u", [33, TOWN], BF16)
    glo_w = b.sb("glo_w", [33, NL + 1], BF16)
    onesb = b.sb("onesb", [1, 512], BF16)
    bdones = b.sb("bdones", [128, 2], BF16)
    repl2 = b.sb("repl2", [2, 128], F32)
    ln_mean = b.sb("ln_mean", [128, NTILE], F32)
    ln_rstd = b.sb("ln_rstd", [128, NTILE], F32)
    su_c = b.sb("su_c", [128, 16], F32)
    sstar = b.sb("sstar", [128, 64], F32)
    sgate = b.sb("sgate", [128, 64], F32)
    wsdp_c = b.sb("wsdp_c", [128, 16], F32)
    pqE = b.sb("pqE", [128, 8 * M], BF16)
    pqO = b.sb("pqO", [128, 8 * M], BF16)
    bias_r = b.sb("bias_r", [1, 3 * D], BF16)
    stats6 = b.sb("stats6", [128, 24], F32)
    mv2 = b.sb("mv2", [128, 8], F32)
    evf = b.sb("evf", [128, 2048], F32)
    ebf = b.sb("ebf", [128, 1024], BF16)
    eb2 = b.sb("eb2", [128, 512], BF16)
    nmax = b.sb("nmax", [128, 8], F32)

    pb = [b.psum(f"pb{i}", [128, 512]) for i in range(8)]
    b.init_arena(174 * 1024)

    sem, cnt = b.sem, b.cnt

    WQ = {"w_qkv": w_qkv, "w_lm_qkv": w_lm_qkv, "w_lm_out": w_lm_out,
          "w_l2t_q": w_l2t_q, "w_l2t_out": w_l2t_out, "w_lm_kv": w_lm_kv,
          "w_out": w_out_i}

    def wsl(buf, kc, c0, w):
        return wslab[:, buf * 4096 + kc * 256 + c0: buf * 4096 + kc * 256 + c0 + w]

    def wsl_eo(buf, kc, eo):
        sl = wslab[:, buf * 4096 + kc * 256: buf * 4096 + (kc + 1) * 256]
        w3 = sl.rearrange("p (slot d par) -> p slot d par", slot=4, par=2)
        return w3[:, :, :, eo]

    def bias_eo(off, eo):
        b3 = bias_r[:, off:off + 256].rearrange(
            "o (slot d par) -> o slot d par", slot=4, par=2)
        return b3[:, :, :, eo]

    def load_wslab(g, w_ap, col0, buf):
        src = w_ap[:, col0:col0 + 256].rearrange("(kc p) w -> p kc w", p=128)
        dst = wslab[:, buf * 4096:(buf + 1) * 4096].rearrange(
            "p (kc w) -> p kc w", kc=16)
        return g.dma_start(out=dst, in_=src)

    with nc.Block() as blk:
        b.blk = blk
        wslab = b.sb_tmp("wslab", [128, 2 * 4096], BF16)

        def p0s(sy):
            for dst, src in ((identb, c_ident), (identf, c_identf),
                             (h128b, c_h128), (h64bdb, c_h64bd),
                             (lau, c_lau), (law, c_law),
                             (lat_u, c_lat_u), (lat_w, c_lat_w),
                             (glo_u, c_glo_u), (glo_w, c_glo_w),
                             (onesb, c_ones), (bdones, c_bdones),
                             (repl2, c_repl2)):
                b.inc(sy.dma_start(out=dst[:, :], in_=src), "s_hld", 16)
        blk.sync(p0s)

        def p0g(g):
            b.inc(g.dma_start(out=bias_r[:, :], in_=b_qkv[None, :]), "s_gld", 16)
            b.inc(g.dma_start(out=su_c[:, :],
                              in_=su_ug.rearrange("(c p) -> p c", p=128),
                              allow_slow_non_contiguous=True), "s_gld", 16)
            b.inc(g.dma_start(out=wsdp_c[:, :],
                              in_=ws_dp.rearrange("(c p) -> p c", p=128),
                              allow_slow_non_contiguous=True), "s_gld", 16)
            b.inc(g.dma_start(out=sstar[:, :], in_=ws_up_p), "s_gld", 16)
            b.inc(g.dma_start(out=sgate[:, :], in_=ws_gt_p), "s_gld", 16)
            b.inc(g.dma_start(out=evf[:, 0:64], in_=su_dp_p), "s_gld", 16)
        blk.gpsimd(p0g)

        def p0v(v):
            b.wait_all(v, ("s_gld",))
            v.tensor_scalar(su_c[:, :], su_c[:, :],
                            1.0 / (np.sqrt(2048.0) * 64.0), None, ALU.mult)
            v.tensor_tensor(sstar[:, :], sstar[:, :], evf[:, 0:64], ALU.mult)
            v.drain()
            v.tensor_scalar(sstar[:, :], sstar[:, :],
                            1.0 / np.sqrt(8192.0), None, ALU.mult)
            v.tensor_scalar(sgate[:, :], sgate[:, :], 64.0, None, ALU.mult)
            v.tensor_scalar(wsdp_c[:, :], wsdp_c[:, :], 64.0, None, ALU.mult)
            v.drain()
            b.inc(v.nop(), "s_dv")
        blk.vector(p0v)
        b.barrier()

        def p0pc(g):
            for dst, srct in ((wupbf, wu_in), (wgtbf, wg_in), (wdnbf, wd_in)):
                for i in range(8):
                    r = srct.shape[0] // 8
                    b.inc(g.dma_start(out=dst[i * r:(i + 1) * r, :],
                                      in_=srct[i * r:(i + 1) * r, :]),
                          "s_pc", 16)
        blk.gpsimd(p0pc)

        xnT = b.sb_tmp("xnT", [128, 16 * L], BF16)
        cosT = b.sb_tmp("cosT", [128, L], F32)
        sinT = b.sb_tmp("sinT", [128, L], F32)
        xtile = b.sb_tmp("xtile", [128, 2 * D], F32)
        xn_tm = b.sb_tmp("xn_tm", [128, 2 * D], BF16)

        def p1c(sy):
            for dst, src in ((cosT, c_cos), (sinT, c_sin)):
                b.inc(sy.dma_start(out=dst[:, :], in_=src), "s_hld", 16)
        blk.sync(p1c)
        b.barrier()

        hld0 = (cnt["s_hld"], cnt["s_h2"])
        dv0, ac0, mm0 = cnt["s_dv"], cnt["s_ac"], cnt["s_mm"]
        XP = ("s_hld", "s_h2")

        stats_dv, apply_dv, evac_dv = {}, {}, {}

        def p1s(sy):
            for tt in range(NTILE):
                if tt >= 2:
                    sy.wait_ge(sem["s_dv"], apply_dv[tt - 2])
                b.inc(sy.dma_start(out=xtile[:, (tt % 2) * D:(tt % 2 + 1) * D],
                                   in_=x_in[tt * 128:(tt + 1) * 128, :]),
                      XP[tt % 2], 16)

        def p1v(v):
            def evac(e):
                for c4 in range(4):
                    gidx = e * 4 + c4
                    v.wait_ge(sem["s_mm"], mm0 + gidx + 1)
                    bank = pb[gidx % 2]
                    for j in range(4):
                        c = c4 * 4 + j
                        v.tensor_copy(
                            xnT[:, c * L + e * 128: c * L + (e + 1) * 128],
                            bank[:, j * 128:(j + 1) * 128])
                    v.drain()
                    evac_dv[gidx] = b.inc(v.nop(), "s_dv")
            for tt in range(NTILE):
                xt = xtile[:, (tt % 2) * D:(tt % 2 + 1) * D]
                v.wait_ge(sem[XP[tt % 2]], hld0[tt % 2] + 16 * (tt // 2 + 1))
                for i in range(4):
                    v.bn_stats(stats6[:, 6 * i:6 * i + 6],
                               xt[:, 512 * i:512 * (i + 1)])
                v.drain()
                v.bn_aggr(mv2[:, 0:2], stats6[:, :])
                v.drain()
                v.tensor_copy(ln_mean[:, tt:tt + 1], mv2[:, 0:1])
                v.tensor_scalar(mv2[:, 2:3], mv2[:, 1:2], 1e-5, None, ALU.add)
                v.drain()
                v.reciprocal(mv2[:, 3:4], mv2[:, 2:3])
                v.drain()
                stats_dv[tt] = b.inc(v.nop(), "s_dv")
                v.wait_ge(sem["s_ac"], ac0 + tt + 1)
                v.tensor_scalar(xn_tm[:, (tt % 2) * D:(tt % 2 + 1) * D], xt,
                                mv2[:, 0:1], ln_rstd[:, tt:tt + 1],
                                ALU.subtract, ALU.mult)
                v.drain()
                apply_dv[tt] = b.inc(v.nop(), "s_dv")
                if tt >= 1:
                    evac(tt - 1)
            evac(NTILE - 1)
        blk.vector(p1v)
        blk.sync(p1s)

        def p1a(s):
            for tt in range(NTILE):
                s.wait_ge(sem["s_dv"], stats_dv[tt])
                s.activation(out=ln_rstd[:, tt:tt + 1], in_=mv2[:, 3:4],
                             func=AF.Sqrt)
                s.drain()
                b.inc(s.nop(), "s_ac")
        blk.scalar(p1a)

        def p1t(t):
            for tt in range(NTILE):
                t.wait_ge(sem["s_dv"], apply_dv[tt])
                xn = xn_tm[:, (tt % 2) * D:(tt % 2 + 1) * D]
                for c4 in range(4):
                    gidx = tt * 4 + c4
                    if gidx >= 2:
                        t.wait_ge(sem["s_dv"], evac_dv[gidx - 2])
                    bank = pb[gidx % 2]
                    for j in range(4):
                        c = c4 * 4 + j
                        t.matmul(bank[:, j * 128:(j + 1) * 128],
                                 xn[:, c * 128:(c + 1) * 128], identb[:, :],
                                 start=True, stop=True)
                    t.drain()
                    b.inc(t.nop(), "s_mm")
        blk.tensor(p1t)
        b.barrier()
        b.free_tmp("xtile", "xn_tm")
        b.tap("xnT", xnT[:, :], [128, 16 * L], BF16)
        if STOP_AFTER == "p1":
            _tail(b, y_out, evf); b.st.close(); return b

        def emit_qk(wcol0, tok0, ntok, dE, dO, ctab, stab):
            ng = ntok // 512
            GP = ("s_gld", "s_g2")
            mm_entry = cnt["s_mm"]
            rope_dv = {}
            mm_at = {}
            for G in range(8):
                buf = G % 2

                def gload(g, G=G, buf=buf):
                    if G >= 2:
                        g.wait_ge(sem["s_mm"], mm_at[G - 2])
                    else:
                        g.wait_ge(sem["s_mm"], mm_entry)
                    b.inc(load_wslab(g, w_qkv, wcol0 + 256 * G, buf),
                          GP[buf], 16)
                blk.gpsimd(gload)
                gld_nm, gld_at = GP[buf], cnt[GP[buf]]

                def tmm(t, G=G, buf=buf, gld_nm=gld_nm, gld_at=gld_at):
                    t.wait_ge(sem[gld_nm], gld_at)
                    if G == 0:
                        b.wait_all(t, ("s_dv",))
                    else:
                        t.wait_ge(sem["s_dv"], rope_dv[G - 1])
                    for eo in range(2):
                        for tg in range(ng):
                            bank = pb[eo * ng + tg]
                            for kc in range(16):
                                t.matmul(bank[:, :], wsl_eo(buf, kc, eo),
                                         xnT[:, kc * L + tok0 + tg * 512:
                                             kc * L + tok0 + (tg + 1) * 512],
                                         start=(kc == 0), stop=False)
                            t.matmul(bank[:, :], bias_eo(wcol0 + 256 * G, eo),
                                     onesb[:, 0:512], start=False, stop=True)
                    t.drain()
                    b.inc(t.nop(), "s_mm")
                blk.tensor(tmm)
                mm_at[G] = cnt["s_mm"]

                def vrope(v, G=G, at=mm_at[G], ng=ng):
                    v.wait_ge(sem["s_mm"], at)
                    for tg in range(ng):
                        pe, po = pb[tg], pb[ng + tg]
                        ct = ctab[:, tg * 512:(tg + 1) * 512]
                        sn = stab[:, tg * 512:(tg + 1) * 512]
                        s0, s1, s2, s3 = (evf[:, i * 512:(i + 1) * 512]
                                          for i in range(4))
                        v.tensor_tensor(s0, pe[:, :], ct, ALU.mult)
                        v.tensor_tensor(s1, po[:, :], sn, ALU.mult)
                        v.drain()
                        v.tensor_tensor(s2, s0, s1, ALU.subtract)
                        v.drain()
                        v.tensor_tensor(s0, pe[:, :], sn, ALU.mult)
                        v.tensor_tensor(s1, po[:, :], ct, ALU.mult)
                        dstE = dE.ap().rearrange("p (t g c) -> p t g c",
                                                 g=8, c=128)[:, 4 * tg:4 * tg + 4,
                                                             G, :]
                        v.tensor_copy(dstE,
                                      s2.rearrange("p (a c) -> p a c", a=4))
                        v.drain()
                        v.tensor_tensor(s3, s0, s1, ALU.add)
                        v.drain()
                        dstO = dO.ap().rearrange("p (t g c) -> p t g c",
                                                 g=8, c=128)[:, 4 * tg:4 * tg + 4,
                                                             G, :]
                        v.tensor_copy(dstO,
                                      s3.rearrange("p (a c) -> p a c", a=4))
                        v.drain()
                    b.inc(v.nop(), "s_dv")
                blk.vector(vrope)
                rope_dv[G] = cnt["s_dv"]

        kEb = b.sb_tmp("kEb", [128, NTILE * 1024], BF16)
        kOb = b.sb_tmp("kOb", [128, NTILE * 1024], BF16)
        emit_qk(D, 0, L, kEb, kOb, cosT, sinT)
        b.barrier()

        def spillk(sy):
            b.inc(sy.dma_start(out=kE_hbm[:, :], in_=kEb[:, :]), "s_st", 16)
            b.inc(sy.dma_start(out=kO_hbm[:, :], in_=kOb[:, :]), "s_st", 16)
        blk.sync(spillk)
        b.tap("kE", kEb[:, :], [128, NTILE * 1024], BF16)
        b.barrier()
        b.free_tmp("kEb", "kOb", "cosT", "sinT")

        qE = b.sb_tmp("qE", [128, 8 * TOWN], BF16)
        qO = b.sb_tmp("qO", [128, 8 * TOWN], BF16)
        cosQ = b.sb_tmp("cosQ", [128, TOWN], F32)
        sinQ = b.sb_tmp("sinQ", [128, TOWN], F32)

        def p3c(sy):
            for dst, src in ((cosQ, c_cosq), (sinQ, c_sinq)):
                b.inc(sy.dma_start(out=dst[:, :], in_=src), "s_hld", 16)
        blk.sync(p3c)
        b.barrier()
        emit_qk(0, 0, TOWN, qE, qO, cosQ, sinQ)
        b.barrier()
        b.free_tmp("cosQ", "sinQ")
        b.tap("qE", qE[:, :], [128, 8 * TOWN], BF16)
        if STOP_AFTER == "p3":
            _tail(b, y_out, evf); b.st.close(); return b

        vst = b.sb_tmp("vst", [128, 2 * 2048], BF16)
        vdv_prev = [cnt["s_dv"]]
        cnt_vsp = {0: cnt["s_st"], 1: cnt["s_s2"]}
        v_mm_entry = cnt["s_mm"]
        vmm = {}
        for sl8 in range(8):
            buf = sl8 % 2

            def gload(g, sl8=sl8, buf=buf):
                if sl8 >= 2:
                    g.wait_ge(sem["s_mm"], vmm[(sl8 - 2, 1)])
                else:
                    g.wait_ge(sem["s_mm"], v_mm_entry)
                b.inc(load_wslab(g, w_qkv, 2 * D + 256 * sl8, buf),
                      ("s_gld", "s_g2")[buf], 16)
            blk.gpsimd(gload)
            gld_nm = ("s_gld", "s_g2")[buf]
            gld_at = cnt[gld_nm]

            for h8 in range(2):
                sp = (2 * sl8 + h8) % 2

                def tmm(t, sl8=sl8, buf=buf, h8=h8, gld_nm=gld_nm,
                        gld_at=gld_at):
                    t.wait_ge(sem[gld_nm], gld_at)
                    t.wait_ge(sem["s_dv"], vdv_prev[0])
                    for tg8 in range(8):
                        tt = h8 * 8 + tg8
                        bank = pb[tg8]
                        for kc in range(16):
                            t.matmul(bank[:, 0:256],
                                     xnT[:, kc * L + tt * 128:
                                         kc * L + (tt + 1) * 128],
                                     wsl(buf, kc, 0, 256),
                                     start=(kc == 0), stop=False)
                        t.matmul(bank[:, 0:256], onesb[:, 0:128],
                                 bias_r[:, 2 * D + 256 * sl8:
                                        2 * D + 256 * (sl8 + 1)],
                                 start=False, stop=True)
                    t.drain()
                    b.inc(t.nop(), "s_mm")
                blk.tensor(tmm)
                vmm[(sl8, h8)] = cnt["s_mm"]

                def vev(v, sl8=sl8, h8=h8, sp=sp, at=vmm[(sl8, h8)],
                        vw=None):
                    v.wait_ge(sem["s_mm"], at)
                    nm2 = ("s_st", "s_s2")[sp]
                    v.wait_ge(sem[nm2], cnt_vsp[sp])
                    for tg8 in range(8):
                        v.tensor_copy(vst[:, sp * 2048 + tg8 * 256:
                                          sp * 2048 + (tg8 + 1) * 256],
                                      pb[tg8][:, 0:256])
                    v.drain()
                    b.inc(v.nop(), "s_dv")
                blk.vector(vev)
                vdv_prev = [cnt["s_dv"]]

                def vsp(sy, sl8=sl8, h8=h8, sp=sp, at=vdv_prev[0]):
                    sy.wait_ge(sem["s_dv"], at)
                    dst = v_hbm.rearrange("p (tt c) -> p tt c", tt=NTILE)[
                        :, h8 * 8:(h8 + 1) * 8, 256 * sl8:256 * (sl8 + 1)]
                    b.inc(sy.dma_start(
                        out=dst,
                        in_=vst[:, sp * 2048:(sp + 1) * 2048].rearrange(
                            "p (tg c) -> p tg c", tg=8)),
                        ("s_st", "s_s2")[sp], 16)
                blk.sync(vsp)
                cnt_vsp[sp] = cnt[("s_st", "s_s2")[sp]]
        b.barrier()
        b.free_tmp("vst")
        b.tap("v_tm", v_hbm[:, :], [128, NTILE * D], BF16)

        pqsb = b.sb_tmp("pqsb", [M, D], BF16)

        def p5l(g):
            b.inc(g.dma_start(out=pqsb[:, :], in_=pool_q), "s_gld", 16)
        blk.gpsimd(p5l)
        gld5 = cnt["s_gld"]

        def p5t(t):
            t.wait_ge(sem["s_gld"], gld5)
            for eo in range(2):
                for G in range(8):
                    w3 = pqsb[:, 256 * G:256 * (G + 1)].rearrange(
                        "p (slot d par) -> p slot d par", slot=4, par=2)
                    t.matmul(pb[eo][:, G * 4:(G + 1) * 4], w3[:, :, :, eo],
                             identb[0:4, 0:4], start=True, stop=True)
            t.drain()
            b.inc(t.nop(), "s_mm")
        blk.tensor(p5t)
        mm5 = cnt["s_mm"]

        def p5v(v):
            v.wait_ge(sem["s_mm"], mm5)
            v.tensor_copy(pqE[:, :], pb[0][:, 0:32])
            v.tensor_copy(pqO[:, :], pb[1][:, 0:32])
            v.drain()
            b.inc(v.nop(), "s_dv")
        blk.vector(p5v)
        b.barrier()
        b.free_tmp("pqsb", "xnT")
        if STOP_AFTER == "p4":
            _tail(b, y_out, evf); b.st.close(); return b

        OL = b.sb_tmp("OL", [128, 16 * TOWN], BF16)
        lmf = b.sb_tmp("lmf", [128, 16 * NL], F32)
        kEt = b.sb_tmp("kEt", [128, 2 * 1024], BF16)
        kOt = b.sb_tmp("kOt", [128, 2 * 1024], BF16)
        vt = b.sb_tmp("vt", [128, 2 * D], BF16)

        pool_mm_at, pool_dv_at = {}, {}
        loc_dv = [cnt["s_dv"]]
        for tt in range(NTILE):
            buf = tt % 2

            def krel(sy, tt=tt, buf=buf):
                if tt >= 2:
                    sy.wait_ge(sem["s_mm"], pool_mm_at[tt - 2])
                b.inc(sy.dma_start(out=kEt[:, buf * 1024:(buf + 1) * 1024],
                                   in_=kE_hbm[:, tt * 1024:(tt + 1) * 1024]),
                      ("s_hld", "s_h2")[buf], 16)
                b.inc(sy.dma_start(out=kOt[:, buf * 1024:(buf + 1) * 1024],
                                   in_=kO_hbm[:, tt * 1024:(tt + 1) * 1024]),
                      ("s_hld", "s_h2")[buf], 16)
                b.inc(sy.dma_start(out=vt[:, buf * D:(buf + 1) * D],
                                   in_=v_hbm[:, tt * D:(tt + 1) * D]),
                      ("s_hld", "s_h2")[buf], 16)
            blk.sync(krel)
            k_nm, at_k = ("s_hld", "s_h2")[buf], cnt[("s_hld", "s_h2")[buf]]
            kEv = kEt[:, buf * 1024:(buf + 1) * 1024]
            kOv = kOt[:, buf * 1024:(buf + 1) * 1024]

            if tt < NOWN and not P6_NO_LOCAL:
                for hg in range(8):
                    def lmm(t, tt=tt, hg=hg, nm=k_nm, at=at_k):
                        t.wait_ge(sem[nm], at)
                        t.wait_ge(sem["s_dv"], loc_dv[0])
                        for hh in range(4):
                            t.matmul(pb[hh][:, 0:128], lau[:, :], law[:, :],
                                     start=True, stop=False,
                                     skip_group_check=True)
                        for hh in range(4):
                            h = hg * 4 + hh
                            G, s4 = h // 4, 32 * (h % 4)
                            bank = pb[hh]
                            t.drain()
                            for bl in range(2):
                                for qq, kk in ((qE, kEv), (qO, kOv)):
                                    t.matmul(
                                        bank[64 * bl:64 * bl + 64,
                                             64 * bl:64 * bl + 64],
                                        qq[s4:s4 + 32,
                                           tt * 1024 + G * 128 + 64 * bl:
                                           tt * 1024 + G * 128 + 64 * bl + 64],
                                        kk[s4:s4 + 32,
                                           G * 128 + 64 * bl:
                                           G * 128 + 64 * bl + 64],
                                        start=False,
                                        stop=(bl == 1 and kk is kOv),
                                        tile_position=_tp(s4, 64 * bl),
                                        skip_group_check=True)
                        t.drain()
                        b.inc(t.nop(), "s_mm")
                    blk.tensor(lmm)
                    at_s = cnt["s_mm"]

                    def lsm(v, at=at_s):
                        v.wait_ge(sem["s_mm"], at)
                        for hh in range(4):
                            v.reduce_max(nmax[:, hh:hh + 1], pb[hh][:, 0:128],
                                         AXL, negate=True)
                        v.drain()
                        b.inc(v.nop(), "s_dv")
                    blk.vector(lsm)
                    at_max = cnt["s_dv"]

                    def lexp(s, at=at_max):
                        s.wait_ge(sem["s_dv"], at)
                        for hh in range(4):
                            s.activation(out=ebf[:, hh * 128:(hh + 1) * 128],
                                         in_=pb[hh][:, 0:128], func=AF.Exp,
                                         bias=nmax[:, hh:hh + 1],
                                         accum_out=mv2[:, hh:hh + 1])
                        s.drain()
                        b.inc(s.nop(), "s_ac")
                    blk.scalar(lexp)
                    at_exp = cnt["s_ac"]

                    def lnrm(v, at=at_exp):
                        v.wait_ge(sem["s_ac"], at)
                        v.reciprocal(mv2[:, 4:8], mv2[:, 0:4])
                        v.drain()
                        for hh in range(4):
                            v.tensor_scalar(ebf[:, hh * 128:(hh + 1) * 128],
                                            ebf[:, hh * 128:(hh + 1) * 128],
                                            mv2[:, 4 + hh:5 + hh], None,
                                            ALU.mult)
                        v.drain()
                        b.inc(v.nop(), "s_dv")
                    blk.vector(lnrm)
                    at_nrm = cnt["s_dv"]

                    def ltr(t, at=at_nrm):
                        t.wait_ge(sem["s_dv"], at)
                        for hh in range(4):
                            t.matmul(pb[4 + hh][:, 0:128],
                                     ebf[:, hh * 128:(hh + 1) * 128],
                                     identb[:, :], start=True, stop=True)
                        t.drain()
                        b.inc(t.nop(), "s_mm")
                    blk.tensor(ltr)
                    at_tr = cnt["s_mm"]

                    def levt(v, at=at_tr):
                        v.wait_ge(sem["s_mm"], at)
                        for hh in range(4):
                            v.tensor_copy(eb2[:, hh * 128:(hh + 1) * 128],
                                          pb[4 + hh][:, 0:128])
                        v.drain()
                        b.inc(v.nop(), "s_dv")
                    blk.vector(levt)
                    at_evt = cnt["s_dv"]

                    def lav(t, tt=tt, hg=hg, at=at_evt):
                        t.wait_ge(sem["s_dv"], at)
                        for hh in range(4):
                            h = hg * 4 + hh
                            bank = pb[(hh // 2) * 2]
                            t.matmul(bank[64 * (h % 2):64 * (h % 2) + 64,
                                          0:128],
                                     vt[:, (tt % 2) * D + h * 64:
                                        (tt % 2) * D + h * 64 + 64],
                                     eb2[:, hh * 128:(hh + 1) * 128],
                                     start=True, stop=(hh % 2 == 1),
                                     tile_position=_tp(0, 64 * (h % 2)),
                                     skip_group_check=True)
                        t.drain()
                        b.inc(t.nop(), "s_mm")
                    blk.tensor(lav)
                    at_av = cnt["s_mm"]

                    def lole(v, tt=tt, hg=hg, at=at_av):
                        v.wait_ge(sem["s_mm"], at)
                        for hh in (0, 2):
                            hp = (hg * 4 + hh) // 2
                            v.tensor_copy(
                                OL[:, hp * 1024 + tt * 128:
                                   hp * 1024 + (tt + 1) * 128],
                                pb[hh][:, 0:128])
                        v.drain()
                        b.inc(v.nop(), "s_dv")
                    blk.vector(lole)
                    loc_dv = [cnt["s_dv"]]

            if P6_NO_POOL:
                def pskip(t, tt=tt, nm=k_nm, at=at_k):
                    t.wait_ge(sem[nm], at)
                    t.drain()
                    b.inc(t.nop(), "s_mm")
                blk.tensor(pskip)
                pool_mm_at[tt] = cnt["s_mm"]

                def pskip2(v, at=pool_mm_at[tt]):
                    v.wait_ge(sem["s_mm"], at)
                    b.inc(v.nop(), "s_dv")
                blk.vector(pskip2)
                pool_dv_at[tt] = cnt["s_dv"]
                continue

            def pmm(t, tt=tt, nm=k_nm, at=at_k):
                t.wait_ge(sem[nm], at)
                if tt > 0:
                    t.wait_ge(sem["s_dv"], pool_dv_at[tt - 1])
                for slot in range(4):
                    s4 = 32 * slot
                    t.drain()
                    for G in range(8):
                        h = 4 * G + slot
                        for bl in range(2):
                            for eo, (pq_, kk) in enumerate(((pqE, kEv),
                                                            (pqO, kOv))):
                                t.matmul(pb[6][64 * bl:64 * bl + 64,
                                               4 * h:4 * h + 4],
                                         kk[s4:s4 + 32, G * 128 + 64 * bl:
                                            G * 128 + 64 * bl + 64],
                                         pq_[s4:s4 + 32, 4 * G:4 * G + 4],
                                         start=(eo == 0), stop=(eo == 1),
                                         tile_position=_tp(s4, 64 * bl),
                                         skip_group_check=True)
                t.drain()
                b.inc(t.nop(), "s_mm")
            blk.tensor(pmm)
            at_ps = cnt["s_mm"]

            def pexp(s, at=at_ps):
                s.wait_ge(sem["s_mm"], at)
                s.activation(out=ebf[:, 0:128], in_=pb[6][:, 0:128],
                             func=AF.Exp, scale=0.125)
                s.drain()
                b.inc(s.nop(), "s_ac")
            blk.scalar(pexp)
            at_pe = cnt["s_ac"]

            def psmm(t, at=at_pe):
                t.wait_ge(sem["s_ac"], at)
                t.matmul(pb[7][0:2, 0:128], bdones[:, :], ebf[:, 0:128],
                         start=True, stop=True)
                t.drain()
                b.inc(t.nop(), "s_mm")
            blk.tensor(psmm)
            at_psu = cnt["s_mm"]

            def prec(v, at=at_psu):
                v.wait_ge(sem["s_mm"], at)
                v.reciprocal(evf[0:2, 0:128], pb[7][0:2, 0:128])
                v.drain()
                b.inc(v.nop(), "s_dv")
            blk.vector(prec)
            at_pre = cnt["s_dv"]

            def prep(t, at=at_pre):
                t.wait_ge(sem["s_dv"], at)
                t.matmul(pb[7][:, 0:128], repl2[:, :], evf[0:2, 0:128],
                         start=True, stop=True)
                t.drain()
                b.inc(t.nop(), "s_mm")
            blk.tensor(prep)
            at_prp = cnt["s_mm"]

            def pnrm(v, at=at_prp):
                v.wait_ge(sem["s_mm"], at)
                v.tensor_tensor(ebf[:, 0:128], ebf[:, 0:128],
                                pb[7][:, 0:128], ALU.mult)
                v.drain()
                b.inc(v.nop(), "s_dv")
            blk.vector(pnrm)
            at_pn = cnt["s_dv"]

            def pav(t, tt=tt, at=at_pn):
                t.wait_ge(sem["s_dv"], at)
                for bl in range(2):
                    t.drain()
                    for h in range(H):
                        t.matmul(pb[6][64 * (h % 2):64 * (h % 2) + 64,
                                       (h // 2) * 8 + bl * 4:
                                       (h // 2) * 8 + bl * 4 + 4],
                                 vt[64 * bl:64 * bl + 64,
                                    (tt % 2) * D + h * 64:
                                    (tt % 2) * D + h * 64 + 64],
                                 ebf[64 * bl:64 * bl + 64, 4 * h:4 * h + 4],
                                 start=True, stop=True,
                                 tile_position=_tp(64 * bl, 64 * (h % 2)),
                                 skip_group_check=True)
                t.drain()
                b.inc(t.nop(), "s_mm")
            blk.tensor(pav)
            pool_mm_at[tt] = cnt["s_mm"]

            def plm(v, tt=tt, at=pool_mm_at[tt]):
                v.wait_ge(sem["s_mm"], at)
                for hp in range(16):
                    v.tensor_copy(lmf[:, hp * 128 + tt * 8:
                                      hp * 128 + tt * 8 + 8],
                                  pb[6][:, hp * 8: hp * 8 + 8])
                v.drain()
                b.inc(v.nop(), "s_dv")
            blk.vector(plm)
            pool_dv_at[tt] = cnt["s_dv"]

        b.barrier()
        b.tap("lmf_pool", lmf[:, :], [128, 16 * NL], F32)
        b.tap("OL", OL[:, :], [128, 16 * TOWN], BF16)
        if STOP_AFTER == "p6":
            _tail(b, y_out, evf); b.st.close(); return b

        lmfb = b.sb_tmp("lmfb", [128, 16 * NL], BF16)

        def lmcp(v):
            for c in range(16):
                v.tensor_copy(lmfb[:, c * 128:(c + 1) * 128],
                              lmf[:, c * 128:(c + 1) * 128])
            v.drain()
            b.inc(v.nop(), "s_dv")
        blk.vector(lmcp)
        b.barrier()

        def load_bias(name_ap, n, off=0):
            def f(g):
                b.inc(g.dma_start(out=bias_r[:, off:off + n],
                                  in_=name_ap[None, :]), "s_gld", 16)
            blk.gpsimd(f)
        load_bias(b_lm_qkv, 3 * D)
        b.barrier()

        def fm_proj(w_ap, wc0, boff, dst, n_rg, accum=False, upd_bf=None):
            fm_mm, fm_dv = {}, {}
            mm_entry = cnt["s_mm"]
            for rgp in range(0, n_rg, 2):
                buf = (rgp // 2) % 2

                def gload(g, rgp=rgp, buf=buf):
                    if rgp >= 4:
                        g.wait_ge(sem["s_mm"], fm_mm[rgp - 4])
                    else:
                        g.wait_ge(sem["s_mm"], mm_entry)
                    b.inc(load_wslab(g, w_ap, wc0 + rgp * 128, buf),
                          ("s_gld", "s_g2")[buf], 16)
                blk.gpsimd(gload)
                gld_nm = ("s_gld", "s_g2")[buf]
                gld_at = cnt[gld_nm]

                def tmm(t, rgp=rgp, buf=buf, gld_nm=gld_nm, gld_at=gld_at):
                    t.wait_ge(sem[gld_nm], gld_at)
                    if rgp == 0:
                        b.wait_all(t, ("s_dv",))
                    if rgp >= 4:
                        t.wait_ge(sem["s_dv"], fm_dv[rgp - 4])
                    for j in range(2):
                        bank = pb[(rgp + j) % 4]
                        for kc in range(16):
                            t.matmul(bank[:, 0:128],
                                     wsl(buf, kc, j * 128, 128),
                                     lmfb[:, kc * 128:(kc + 1) * 128],
                                     start=(kc == 0), stop=False)
                        t.matmul(bank[:, 0:128],
                                 bias_r[:, boff + wc0 + (rgp + j) * 128:
                                        boff + wc0 + (rgp + j + 1) * 128],
                                 onesb[:, 0:128], start=False, stop=True)
                    t.drain()
                    b.inc(t.nop(), "s_mm")
                blk.tensor(tmm)
                fm_mm[rgp] = cnt["s_mm"]

                def vev(v, rgp=rgp, at=fm_mm[rgp]):
                    v.wait_ge(sem["s_mm"], at)
                    for j in range(2):
                        c = rgp + j
                        if accum:
                            v.tensor_tensor(lmf[:, c * 128:(c + 1) * 128],
                                            lmf[:, c * 128:(c + 1) * 128],
                                            pb[c % 4][:, 0:128], ALU.add)
                        else:
                            v.tensor_copy(dst[:, c * 128:(c + 1) * 128],
                                          pb[c % 4][:, 0:128])
                    v.drain()
                    if accum:
                        for j in range(2):
                            c = rgp + j
                            v.tensor_copy(upd_bf[:, c * 128:(c + 1) * 128],
                                          lmf[:, c * 128:(c + 1) * 128])
                        v.drain()
                    b.inc(v.nop(), "s_dv")
                blk.vector(vev)
                fm_dv[rgp] = cnt["s_dv"]

        lqsb = b.sb_tmp("lqsb", [128, 16 * 128], BF16)
        lksb = b.sb_tmp("lksb", [128, 16 * 128], BF16)
        fm_proj(w_lm_qkv, 0, 0, lqsb, 16)
        fm_proj(w_lm_qkv, D, 0, lksb, 16)
        b.barrier()

        lv_tm = b.sb_tmp("lv_tm", [128, D], BF16)

        def latmaj(w_ap, wc0, boff, dst):
            lv_mm, lv_dv = {}, {}
            mm_entry = cnt["s_mm"]
            for sl8 in range(8):
                buf = sl8 % 2

                def gload(g, sl8=sl8, buf=buf):
                    if sl8 >= 2:
                        g.wait_ge(sem["s_mm"], lv_mm[sl8 - 2])
                    else:
                        g.wait_ge(sem["s_mm"], mm_entry)
                    b.inc(load_wslab(g, w_ap, wc0 + 256 * sl8, buf),
                          ("s_gld", "s_g2")[buf], 16)
                blk.gpsimd(gload)
                gld_nm = ("s_gld", "s_g2")[buf]
                gld_at = cnt[gld_nm]

                def tmm(t, sl8=sl8, buf=buf, gld_nm=gld_nm, gld_at=gld_at):
                    t.wait_ge(sem[gld_nm], gld_at)
                    if sl8 == 0:
                        b.wait_all(t, ("s_dv",))
                    if sl8 >= 2:
                        t.wait_ge(sem["s_dv"], lv_dv[sl8 - 2])
                    bank = pb[4 + sl8 % 2]
                    for kc in range(16):
                        t.matmul(bank[:, 0:256],
                                 lmfb[:, kc * 128:(kc + 1) * 128],
                                 wsl(buf, kc, 0, 256),
                                 start=(kc == 0), stop=False)
                    t.matmul(bank[:, 0:256], onesb[:, 0:128],
                             bias_r[:, boff + 256 * sl8:boff + 256 * (sl8 + 1)],
                             start=False, stop=True)
                    t.drain()
                    b.inc(t.nop(), "s_mm")
                blk.tensor(tmm)
                lv_mm[sl8] = cnt["s_mm"]

                def vev(v, sl8=sl8, at=lv_mm[sl8]):
                    v.wait_ge(sem["s_mm"], at)
                    v.tensor_copy(dst[:, 256 * sl8:256 * (sl8 + 1)],
                                  pb[4 + sl8 % 2][:, 0:256])
                    v.drain()
                    b.inc(v.nop(), "s_dv")
                blk.vector(vev)
                lv_dv[sl8] = cnt["s_dv"]

        latmaj(w_lm_qkv, 2 * D, 2 * D, lv_tm)
        b.barrier()
        b.tap("lq", lqsb[:, :], [128, 16 * 128], BF16)
        b.tap("lv", lv_tm[:, :], [128, D], BF16)

        losb = b.sb_tmp("losb", [128, 16 * 128], BF16)
        lat_dv = [cnt["s_dv"]]
        for h in range(H):
            c2, base = h // 2, 64 * (h % 2)

            def lamm(t, h=h, c2=c2, base=base):
                t.wait_ge(sem["s_dv"], lat_dv[0])
                t.drain()
                t.matmul(pb[0][:, 0:128],
                         lqsb[base:base + 64, c2 * 128:(c2 + 1) * 128],
                         lksb[base:base + 64, c2 * 128:(c2 + 1) * 128],
                         start=True, stop=False, tile_position=_tp(base, 0),
                         skip_group_check=True)
                t.drain()
                t.matmul(pb[0][:, 0:128], lat_u[:, :], lat_w[:, :],
                         start=False, stop=True, skip_group_check=True)
                t.drain()
                b.inc(t.nop(), "s_mm")
            blk.tensor(lamm)
            at1 = cnt["s_mm"]

            def lasm(v, at=at1):
                v.wait_ge(sem["s_mm"], at)
                v.reduce_max(nmax[:, 0:1], pb[0][:, 0:128], AXL, negate=True)
                v.drain()
                v.tensor_scalar(nmax[:, 1:2], nmax[:, 0:1], 0.125, None,
                                ALU.mult)
                v.drain()
                b.inc(v.nop(), "s_dv")
            blk.vector(lasm)
            at2 = cnt["s_dv"]

            def laexp(s, at=at2):
                s.wait_ge(sem["s_dv"], at)
                s.activation(out=ebf[:, 0:128], in_=pb[0][:, 0:128],
                             func=AF.Exp, bias=nmax[:, 1:2], scale=0.125,
                             accum_out=mv2[:, 0:1])
                s.drain()
                b.inc(s.nop(), "s_ac")
            blk.scalar(laexp)
            at3 = cnt["s_ac"]

            def lanrm(v, at=at3):
                v.wait_ge(sem["s_ac"], at)
                v.reciprocal(mv2[:, 1:2], mv2[:, 0:1])
                v.drain()
                v.tensor_scalar(ebf[:, 0:128], ebf[:, 0:128], mv2[:, 1:2],
                                None, ALU.mult)
                v.drain()
                b.inc(v.nop(), "s_dv")
            blk.vector(lanrm)
            at4 = cnt["s_dv"]

            def latr(t, at=at4):
                t.wait_ge(sem["s_dv"], at)
                t.matmul(pb[1][:, 0:128], ebf[:, 0:128], identb[:, :],
                         start=True, stop=True)
                t.drain()
                b.inc(t.nop(), "s_mm")
            blk.tensor(latr)
            at5 = cnt["s_mm"]

            def laev(v, at=at5):
                v.wait_ge(sem["s_mm"], at)
                v.tensor_copy(eb2[:, 0:128], pb[1][:, 0:128])
                v.drain()
                b.inc(v.nop(), "s_dv")
            blk.vector(laev)
            at6 = cnt["s_dv"]

            def laav(t, h=h, at=at6):
                t.wait_ge(sem["s_dv"], at)
                t.matmul(pb[2][64 * (h % 2):64 * (h % 2) + 64, 0:128],
                         lv_tm[:, h * 64:(h + 1) * 64], eb2[:, 0:128],
                         start=True, stop=(h % 2 == 1),
                         tile_position=_tp(0, 64 * (h % 2)),
                         skip_group_check=True)
                t.drain()
                b.inc(t.nop(), "s_mm")
            blk.tensor(laav)
            at7 = cnt["s_mm"]

            def laol(v, h=h, at=at7):
                v.wait_ge(sem["s_mm"], at)
                if h % 2 == 1:
                    v.tensor_copy(losb[:, (h // 2) * 128:(h // 2 + 1) * 128],
                                  pb[2][:, 0:128])
                    v.drain()
                b.inc(v.nop(), "s_dv")
            blk.vector(laol)
            lat_dv = [cnt["s_dv"]]
        b.barrier()
        b.free_tmp("lqsb", "lksb", "lv_tm")
        b.tap("lo", losb[:, :], [128, 16 * 128], BF16)

        load_bias(b_lm_out, D)
        b.barrier()

        def fm_proj_src(w_ap, boff, src_sb):
            fa_mm, fa_dv = {}, {}
            mm_entry = cnt["s_mm"]
            for rgp in range(0, 16, 2):
                buf = (rgp // 2) % 2

                def gload(g, rgp=rgp, buf=buf):
                    if rgp >= 4:
                        g.wait_ge(sem["s_mm"], fa_mm[rgp - 4])
                    else:
                        g.wait_ge(sem["s_mm"], mm_entry)
                    b.inc(load_wslab(g, w_ap, rgp * 128, buf),
                          ("s_gld", "s_g2")[buf], 16)
                blk.gpsimd(gload)
                gld_nm = ("s_gld", "s_g2")[buf]
                gld_at = cnt[gld_nm]

                def tmm(t, rgp=rgp, buf=buf, gld_nm=gld_nm, gld_at=gld_at):
                    t.wait_ge(sem[gld_nm], gld_at)
                    if rgp == 0:
                        b.wait_all(t, ("s_dv",))
                    if rgp >= 4:
                        t.wait_ge(sem["s_dv"], fa_dv[rgp - 4])
                    for j in range(2):
                        bank = pb[(rgp + j) % 4]
                        for kc in range(16):
                            t.matmul(bank[:, 0:128],
                                     wsl(buf, kc, j * 128, 128),
                                     src_sb[:, kc * 128:(kc + 1) * 128],
                                     start=(kc == 0), stop=False)
                        t.matmul(bank[:, 0:128],
                                 bias_r[:, boff + (rgp + j) * 128:
                                        boff + (rgp + j + 1) * 128],
                                 onesb[:, 0:128], start=False, stop=True)
                    t.drain()
                    b.inc(t.nop(), "s_mm")
                blk.tensor(tmm)
                fa_mm[rgp] = cnt["s_mm"]

                def vev(v, rgp=rgp, at=fa_mm[rgp]):
                    v.wait_ge(sem["s_mm"], at)
                    for j in range(2):
                        c = rgp + j
                        v.tensor_tensor(lmf[:, c * 128:(c + 1) * 128],
                                        lmf[:, c * 128:(c + 1) * 128],
                                        pb[c % 4][:, 0:128], ALU.add)
                    v.drain()
                    for j in range(2):
                        c = rgp + j
                        v.tensor_copy(lmfb[:, c * 128:(c + 1) * 128],
                                      lmf[:, c * 128:(c + 1) * 128])
                    v.drain()
                    b.inc(v.nop(), "s_dv")
                blk.vector(vev)
                fa_dv[rgp] = cnt["s_dv"]

        fm_proj_src(w_lm_out, 0, losb)
        b.barrier()
        b.free_tmp("losb")
        b.tap("lmf_postlm", lmf[:, :], [128, 16 * NL], F32)

        load_bias(b_l2t_q, D)
        b.barrier()
        l2qE = b.sb_tmp("l2qE", [128, 8 * 128], BF16)
        l2qO = b.sb_tmp("l2qO", [128, 8 * 128], BF16)
        l2_mm, l2_dv = {}, {}
        l2_mm_entry = cnt["s_mm"]
        for G in range(8):
            buf = G % 2

            def gload(g, G=G, buf=buf):
                if G >= 2:
                    g.wait_ge(sem["s_mm"], l2_mm[G - 2])
                else:
                    g.wait_ge(sem["s_mm"], l2_mm_entry)
                b.inc(load_wslab(g, w_l2t_q, 256 * G, buf),
                      ("s_gld", "s_g2")[buf], 16)
            blk.gpsimd(gload)
            gld_nm = ("s_gld", "s_g2")[buf]
            gld_at = cnt[gld_nm]

            def tmm(t, G=G, buf=buf, gld_nm=gld_nm, gld_at=gld_at):
                t.wait_ge(sem[gld_nm], gld_at)
                if G == 0:
                    b.wait_all(t, ("s_dv",))
                else:
                    t.wait_ge(sem["s_dv"], l2_dv[G - 1])
                for eo in range(2):
                    bank = pb[eo]
                    for kc in range(16):
                        t.matmul(bank[:, 0:128], wsl_eo(buf, kc, eo),
                                 lmfb[:, kc * 128:(kc + 1) * 128],
                                 start=(kc == 0), stop=False)
                    t.matmul(bank[:, 0:128], bias_eo(256 * G, eo),
                             onesb[:, 0:128], start=False, stop=True)
                t.drain()
                b.inc(t.nop(), "s_mm")
            blk.tensor(tmm)
            l2_mm[G] = cnt["s_mm"]

            def vev(v, G=G, at=l2_mm[G]):
                v.wait_ge(sem["s_mm"], at)
                v.tensor_copy(l2qE[:, G * 128:(G + 1) * 128], pb[0][:, 0:128])
                v.tensor_copy(l2qO[:, G * 128:(G + 1) * 128], pb[1][:, 0:128])
                v.drain()
                b.inc(v.nop(), "s_dv")
            blk.vector(vev)
            l2_dv[G] = cnt["s_dv"]
        b.barrier()

        rof = b.sb_tmp("rof", [128, 16 * 128], F32)
        ro_mm_at, ro_dv_at = {}, {}
        for tt in range(NTILE):
            buf = tt % 2

            def krel2(sy, tt=tt, buf=buf):
                if tt >= 2:
                    sy.wait_ge(sem["s_mm"], ro_mm_at[tt - 2])
                b.inc(sy.dma_start(out=kEt[:, buf * 1024:(buf + 1) * 1024],
                                   in_=kE_hbm[:, tt * 1024:(tt + 1) * 1024]),
                      ("s_hld", "s_h2")[buf], 16)
                b.inc(sy.dma_start(out=kOt[:, buf * 1024:(buf + 1) * 1024],
                                   in_=kO_hbm[:, tt * 1024:(tt + 1) * 1024]),
                      ("s_hld", "s_h2")[buf], 16)
                b.inc(sy.dma_start(out=vt[:, buf * D:(buf + 1) * D],
                                   in_=v_hbm[:, tt * D:(tt + 1) * D]),
                      ("s_hld", "s_h2")[buf], 16)
            blk.sync(krel2)
            k_nm, at_k = ("s_hld", "s_h2")[buf], cnt[("s_hld", "s_h2")[buf]]
            kEv = kEt[:, buf * 1024:(buf + 1) * 1024]
            kOv = kOt[:, buf * 1024:(buf + 1) * 1024]

            def rmm(t, tt=tt, nm=k_nm, at=at_k):
                t.wait_ge(sem[nm], at)
                if tt > 0:
                    t.wait_ge(sem["s_dv"], ro_dv_at[tt - 1])
                else:
                    b.wait_all(t, ("s_dv",))
                for slot in range(4):
                    s4 = 32 * slot
                    t.drain()
                    for G in range(8):
                        h = 4 * G + slot
                        for bl in range(2):
                            lat0 = (tt * 2 + bl) * 4
                            for eo, (lq_, kk) in enumerate(((l2qE, kEv),
                                                            (l2qO, kOv))):
                                t.matmul(pb[6][64 * bl:64 * bl + 64,
                                               4 * h:4 * h + 4],
                                         kk[s4:s4 + 32, G * 128 + 64 * bl:
                                            G * 128 + 64 * bl + 64],
                                         lq_[s4:s4 + 32, G * 128 + lat0:
                                             G * 128 + lat0 + 4],
                                         start=(eo == 0), stop=(eo == 1),
                                         tile_position=_tp(s4, 64 * bl),
                                         skip_group_check=True)
                t.drain()
                b.inc(t.nop(), "s_mm")
            blk.tensor(rmm)
            at_rs = cnt["s_mm"]

            def rexp(s, at=at_rs):
                s.wait_ge(sem["s_mm"], at)
                s.activation(out=ebf[:, 0:128], in_=pb[6][:, 0:128],
                             func=AF.Exp, scale=0.125)
                s.drain()
                b.inc(s.nop(), "s_ac")
            blk.scalar(rexp)
            at_re = cnt["s_ac"]

            def rsmm(t, at=at_re):
                t.wait_ge(sem["s_ac"], at)
                t.matmul(pb[7][0:2, 0:128], bdones[:, :], ebf[:, 0:128],
                         start=True, stop=True)
                t.drain()
                b.inc(t.nop(), "s_mm")
            blk.tensor(rsmm)
            at_rsu = cnt["s_mm"]

            def rrec(v, at=at_rsu):
                v.wait_ge(sem["s_mm"], at)
                v.reciprocal(evf[0:2, 0:128], pb[7][0:2, 0:128])
                v.drain()
                b.inc(v.nop(), "s_dv")
            blk.vector(rrec)
            at_rre = cnt["s_dv"]

            def rrep(t, at=at_rre):
                t.wait_ge(sem["s_dv"], at)
                t.matmul(pb[7][:, 0:128], repl2[:, :], evf[0:2, 0:128],
                         start=True, stop=True)
                t.drain()
                b.inc(t.nop(), "s_mm")
            blk.tensor(rrep)
            at_rrp = cnt["s_mm"]

            def rnrm(v, at=at_rrp):
                v.wait_ge(sem["s_mm"], at)
                v.tensor_tensor(ebf[:, 0:128], ebf[:, 0:128],
                                pb[7][:, 0:128], ALU.mult)
                v.drain()
                b.inc(v.nop(), "s_dv")
            blk.vector(rnrm)
            at_rn = cnt["s_dv"]

            def rav(t, tt=tt, at=at_rn):
                t.wait_ge(sem["s_dv"], at)
                for bl in range(2):
                    t.drain()
                    for h in range(H):
                        t.matmul(pb[6][64 * (h % 2):64 * (h % 2) + 64,
                                       (h // 2) * 8 + bl * 4:
                                       (h // 2) * 8 + bl * 4 + 4],
                                 vt[64 * bl:64 * bl + 64,
                                    (tt % 2) * D + h * 64:
                                    (tt % 2) * D + h * 64 + 64],
                                 ebf[64 * bl:64 * bl + 64, 4 * h:4 * h + 4],
                                 start=True, stop=True,
                                 tile_position=_tp(64 * bl, 64 * (h % 2)),
                                 skip_group_check=True)
                t.drain()
                b.inc(t.nop(), "s_mm")
            blk.tensor(rav)
            ro_mm_at[tt] = cnt["s_mm"]

            def rlm(v, tt=tt, at=ro_mm_at[tt]):
                v.wait_ge(sem["s_mm"], at)
                for hp in range(16):
                    v.tensor_copy(rof[:, hp * 128 + tt * 8:
                                      hp * 128 + tt * 8 + 8],
                                  pb[6][:, hp * 8: hp * 8 + 8])
                v.drain()
                b.inc(v.nop(), "s_dv")
            blk.vector(rlm)
            ro_dv_at[tt] = cnt["s_dv"]
        b.barrier()
        b.free_tmp("l2qE", "l2qO", "vt", "kEt", "kOt")

        rosb = b.sb_tmp("rosb", [128, 16 * 128], BF16)

        def rocp(v):
            for c in range(16):
                v.tensor_copy(rosb[:, c * 128:(c + 1) * 128],
                              rof[:, c * 128:(c + 1) * 128])
            v.drain()
            b.inc(v.nop(), "s_dv")
        blk.vector(rocp)
        b.barrier()
        b.free_tmp("rof")
        b.tap("ro", rosb[:, :], [128, 16 * 128], BF16)

        load_bias(b_l2t_out, D)
        b.barrier()
        fm_proj_src(w_l2t_out, 0, rosb)
        b.barrier()
        b.free_tmp("rosb")
        b.tap("lmf_final", lmf[:, :], [128, 16 * NL], F32)

        load_bias(b_lm_kv, 2 * D)
        b.barrier()
        gkE = b.sb_tmp("gkE", [128, 8 * 136], BF16)
        gkO = b.sb_tmp("gkO", [128, 8 * 136], BF16)
        gv_tm = b.sb_tmp("gv_tm", [128, D], BF16)
        gk_mm, gk_dv = {}, {}
        gk_mm_entry = cnt["s_mm"]
        for G in range(8):
            buf = G % 2

            def gload(g, G=G, buf=buf):
                if G >= 2:
                    g.wait_ge(sem["s_mm"], gk_mm[G - 2])
                else:
                    g.wait_ge(sem["s_mm"], gk_mm_entry)
                b.inc(load_wslab(g, w_lm_kv, 256 * G, buf),
                      ("s_gld", "s_g2")[buf], 16)
            blk.gpsimd(gload)
            gld_nm = ("s_gld", "s_g2")[buf]
            gld_at = cnt[gld_nm]

            def tmm(t, G=G, buf=buf, gld_nm=gld_nm, gld_at=gld_at):
                t.wait_ge(sem[gld_nm], gld_at)
                if G == 0:
                    b.wait_all(t, ("s_dv",))
                else:
                    t.wait_ge(sem["s_dv"], gk_dv[G - 1])
                for eo in range(2):
                    bank = pb[eo]
                    for kc in range(16):
                        t.matmul(bank[:, 0:128], wsl_eo(buf, kc, eo),
                                 lmfb[:, kc * 128:(kc + 1) * 128],
                                 start=(kc == 0), stop=False)
                    t.matmul(bank[:, 0:128], bias_eo(256 * G, eo),
                             onesb[:, 0:128], start=False, stop=True)
                t.drain()
                b.inc(t.nop(), "s_mm")
            blk.tensor(tmm)
            gk_mm[G] = cnt["s_mm"]

            def vev(v, G=G, at=gk_mm[G]):
                v.wait_ge(sem["s_mm"], at)
                for dst, bank in ((gkE, pb[0]), (gkO, pb[1])):
                    v.memset(dst[:, G * 136 + 128:(G + 1) * 136], 0.0)
                    v.tensor_copy(dst[:, G * 136:G * 136 + 128],
                                  bank[:, 0:128])
                v.drain()
                b.inc(v.nop(), "s_dv")
            blk.vector(vev)
            gk_dv[G] = cnt["s_dv"]
        b.barrier()
        latmaj(w_lm_kv, D, D, gv_tm)
        b.barrier()
        b.free_tmp("lmfb", "lmf")
        b.tap("gkE", gkE[:, :], [128, 8 * 136], BF16)
        b.tap("gv", gv_tm[:, :], [128, D], BF16)
        if STOP_AFTER == "p7":
            _tail(b, y_out, evf); b.st.close(); return b

        xr = b.sb_tmp("xr", [128, 2 * D], F32)
        xrb = b.sb_tmp("xrb", [128, 2 * D], BF16)
        xT_own = b.sb_tmp("xT_own", [128, 16 * TOWN], BF16)
        hg0 = (cnt["s_hld"], cnt["s_h2"])
        gx_dv0 = cnt["s_dv"]
        XP = ("s_hld", "s_h2")

        gx_mm0 = cnt["s_mm"]
        gconv_dv, gevac_dv = {}, {}

        def gxl(sy):
            for tt in range(NOWN):
                if tt >= 2:
                    sy.wait_ge(sem["s_dv"], gconv_dv[tt - 2])
                b.inc(sy.dma_start(out=xr[:, (tt % 2) * D:(tt % 2 + 1) * D],
                                   in_=x_in[tt * 128:(tt + 1) * 128, :]),
                      XP[tt % 2], 16)

        def gxc(v):
            def evac(e):
                for c4 in range(4):
                    gidx = e * 4 + c4
                    v.wait_ge(sem["s_mm"], gx_mm0 + gidx + 1)
                    bank = pb[gidx % 2]
                    for j in range(4):
                        c = c4 * 4 + j
                        v.tensor_copy(
                            xT_own[:, c * TOWN + e * 128:
                                   c * TOWN + (e + 1) * 128],
                            bank[:, j * 128:(j + 1) * 128])
                    v.drain()
                    gevac_dv[gidx] = b.inc(v.nop(), "s_dv")
            for tt in range(NOWN):
                v.wait_ge(sem[XP[tt % 2]], hg0[tt % 2] + 16 * (tt // 2 + 1))
                v.tensor_copy(xrb[:, (tt % 2) * D:(tt % 2 + 1) * D],
                              xr[:, (tt % 2) * D:(tt % 2 + 1) * D])
                v.drain()
                gconv_dv[tt] = b.inc(v.nop(), "s_dv")
                if tt >= 1:
                    evac(tt - 1)
            evac(NOWN - 1)
        blk.vector(gxc)
        blk.sync(gxl)

        def gxt(t):
            for tt in range(NOWN):
                t.wait_ge(sem["s_dv"], gconv_dv[tt])
                for c4 in range(4):
                    gidx = tt * 4 + c4
                    if gidx >= 2:
                        t.wait_ge(sem["s_dv"], gevac_dv[gidx - 2])
                    bank = pb[gidx % 2]
                    for j in range(4):
                        c = c4 * 4 + j
                        t.matmul(bank[:, j * 128:(j + 1) * 128],
                                 xrb[:, (tt % 2) * D + c * 128:
                                     (tt % 2) * D + (c + 1) * 128],
                                 identb[:, :], start=True, stop=True)
                    t.drain()
                    b.inc(t.nop(), "s_mm")
        blk.tensor(gxt)
        b.barrier()
        b.free_tmp("xr", "xrb")

        g_sb = b.sb_tmp("g_sb", [32, TOWN], F32)
        load_bias(b_gate, H)

        def ggl(g):
            b.wait_all(g, ("s_mm",))
            src = w_gate_i.rearrange("(kc p) w -> p kc w", p=128)
            dst = wslab[:, 0:16 * 32].rearrange("p (kc w) -> p kc w", kc=16)
            b.inc(g.dma_start(out=dst, in_=src), "s_gld", 16)
        blk.gpsimd(ggl)
        ggl_at = cnt["s_gld"]

        def gmm(t):
            t.wait_ge(sem["s_gld"], ggl_at)
            for th in range(2):
                bank = pb[4 + th]
                for kc in range(16):
                    t.matmul(bank[0:32, :],
                             wslab[:, kc * 32:(kc + 1) * 32],
                             xT_own[:, kc * TOWN + th * 512:
                                    kc * TOWN + (th + 1) * 512],
                             start=(kc == 0), stop=False)
                t.matmul(bank[0:32, :], bias_r[:, 0:32], onesb[:, 0:512],
                         start=False, stop=True)
            t.drain()
            b.inc(t.nop(), "s_mm")
        blk.tensor(gmm)
        gmm_at = cnt["s_mm"]

        def gsig(s):
            s.wait_ge(sem["s_mm"], gmm_at)
            for th in range(2):
                s.activation(out=g_sb[:, th * 512:(th + 1) * 512],
                             in_=pb[4 + th][0:32, :], func=AF.Sigmoid)
            s.drain()
            b.inc(s.nop(), "s_ac")
        blk.scalar(gsig)
        b.barrier()
        b.free_tmp("xT_own")
        b.tap("gate", g_sb[:, :], [32, TOWN], F32)

        comb = b.sb_tmp("comb", [128, 16 * TOWN], BF16)
        gaterep = b.sb_tmp("gaterep", [32, 16 * 128], F32)

        def p9c(sy):
            b.inc(sy.dma_start(out=gaterep[:, :], in_=c_gaterep), "s_hld", 16)
        blk.sync(p9c)
        b.barrier()
        glo_dv = [cnt["s_dv"]]
        for c in range(16):
            for h in (2 * c, 2 * c + 1):
                G, s4 = h // 4, 32 * (h % 4)
                for w in range(8):
                    def gsc(t, h=h, G=G, s4=s4, w=w, c=c):
                        t.wait_ge(sem["s_dv"], glo_dv[0])
                        if c == 0 and h == 2 * c and w == 0:
                            b.wait_all(t, ("s_ac",))
                        t.drain()
                        t.matmul(pb[0][:, 0:136],
                                 qE[s4:s4 + 32, w * 1024 + G * 128:
                                    w * 1024 + G * 128 + 128],
                                 gkE[s4:s4 + 32, G * 136:G * 136 + 136],
                                 start=True, stop=False,
                                 tile_position=_tp(s4),
                                 skip_group_check=True)
                        t.matmul(pb[0][:, 0:136],
                                 qO[s4:s4 + 32, w * 1024 + G * 128:
                                    w * 1024 + G * 128 + 128],
                                 gkO[s4:s4 + 32, G * 136:G * 136 + 136],
                                 start=False, stop=False,
                                 tile_position=_tp(s4),
                                 skip_group_check=True)
                        t.drain()
                        t.matmul(pb[0][:, 0:129],
                                 glo_u[:, w * 128:(w + 1) * 128],
                                 glo_w[:, :], start=False, stop=True,
                                 skip_group_check=True)
                        t.drain()
                        b.inc(t.nop(), "s_mm")
                    blk.tensor(gsc)
                    at1 = cnt["s_mm"]

                    def gsm(v, at=at1):
                        v.wait_ge(sem["s_mm"], at)
                        v.reduce_max(nmax[:, 0:1], pb[0][:, 0:129], AXL,
                                     negate=True)
                        v.drain()
                        b.inc(v.nop(), "s_dv")
                    blk.vector(gsm)
                    at2 = cnt["s_dv"]

                    def gex(s, at=at2):
                        s.wait_ge(sem["s_dv"], at)
                        s.activation(out=ebf[:, 0:129], in_=pb[0][:, 0:129],
                                     func=AF.Exp, bias=nmax[:, 0:1],
                                     accum_out=mv2[:, 0:1])
                        s.drain()
                        b.inc(s.nop(), "s_ac")
                    blk.scalar(gex)
                    at3 = cnt["s_ac"]

                    def gnm(v, at=at3):
                        v.wait_ge(sem["s_ac"], at)
                        v.reciprocal(mv2[:, 1:2], mv2[:, 0:1])
                        v.drain()
                        v.tensor_scalar(ebf[:, 0:128], ebf[:, 0:128],
                                        mv2[:, 1:2], None, ALU.mult)
                        v.drain()
                        b.inc(v.nop(), "s_dv")
                    blk.vector(gnm)
                    at4 = cnt["s_dv"]

                    def gtr(t, at=at4):
                        t.wait_ge(sem["s_dv"], at)
                        t.matmul(pb[1][:, 0:128], ebf[:, 0:128], identb[:, :],
                                 start=True, stop=True)
                        t.drain()
                        b.inc(t.nop(), "s_mm")
                    blk.tensor(gtr)
                    at5 = cnt["s_mm"]

                    def gev(v, at=at5):
                        v.wait_ge(sem["s_mm"], at)
                        v.tensor_copy(eb2[:, 0:128], pb[1][:, 0:128])
                        v.drain()
                        b.inc(v.nop(), "s_dv")
                    blk.vector(gev)
                    at6 = cnt["s_dv"]

                    def gav(t, h=h, w=w, at=at6):
                        t.wait_ge(sem["s_dv"], at)
                        t.matmul(pb[4 + w // 4][64 * (h % 2):64 * (h % 2) + 64,
                                                (w % 4) * 128:(w % 4 + 1) * 128],
                                 gv_tm[:, h * 64:(h + 1) * 64],
                                 eb2[:, 0:128], start=True, stop=True,
                                 tile_position=_tp(0, 64 * (h % 2)),
                                 skip_group_check=True)
                        t.drain()
                        b.inc(t.nop(), "s_mm")
                    blk.tensor(gav)
                    glo_dv = [cnt["s_dv"]]
            at_og = cnt["s_mm"]

            def grep(t, c=c, at=at_og):
                t.wait_ge(sem["s_mm"], at)
                for th in range(2):
                    t.matmul(pb[6 + th][:, :],
                             gaterep[:, c * 128:(c + 1) * 128],
                             g_sb[:, th * 512:(th + 1) * 512],
                             start=True, stop=True)
                t.drain()
                b.inc(t.nop(), "s_mm")
            blk.tensor(grep)
            at_gr = cnt["s_mm"]

            def gcmb(v, c=c, at=at_gr):
                v.wait_ge(sem["s_mm"], at)
                for th in range(2):
                    og = pb[4 + th]
                    gr_ = pb[6 + th]
                    olc = OL[:, c * 1024 + th * 512: c * 1024 + (th + 1) * 512]
                    t0 = evf[:, 0:512]
                    t1 = evf[:, 512:1024]
                    v.tensor_tensor(t0, og[:, :], olc, ALU.subtract)
                    v.drain()
                    v.tensor_tensor(t1, t0, gr_[:, :], ALU.mult)
                    v.drain()
                    v.tensor_tensor(comb[:, c * 1024 + th * 512:
                                         c * 1024 + (th + 1) * 512],
                                    t1, olc, ALU.add)
                    v.drain()
                b.inc(v.nop(), "s_dv")
            blk.vector(gcmb)
            glo_dv = [cnt["s_dv"]]
        b.barrier()
        b.free_tmp("qE", "qO", "OL", "gkE", "gkO", "gv_tm", "g_sb")
        b.tap("comb", comb[:, :], [128, 16 * TOWN], BF16)

        load_bias(b_out, D)
        b.barrier()
        so_mm, so_dv, so_st = {}, {}, {}
        so_mm_entry = cnt["s_mm"]
        for rgp in range(0, 16, 2):
            buf = (rgp // 2) % 2

            def gload(g, rgp=rgp, buf=buf):
                if rgp >= 4:
                    g.wait_ge(sem["s_mm"], so_mm[rgp - 4])
                else:
                    g.wait_ge(sem["s_mm"], so_mm_entry)
                b.inc(load_wslab(g, w_out_i, rgp * 128, buf),
                      ("s_gld", "s_g2")[buf], 16)
            blk.gpsimd(gload)
            gld_nm = ("s_gld", "s_g2")[buf]
            gld_at = cnt[gld_nm]

            def tmm(t, rgp=rgp, buf=buf, gld_nm=gld_nm, gld_at=gld_at):
                t.wait_ge(sem[gld_nm], gld_at)
                if rgp == 0:
                    b.wait_all(t, ("s_dv",))
                else:
                    t.wait_ge(sem["s_dv"], so_dv[rgp - 2])
                for j in range(2):
                    for th in range(2):
                        bank = pb[2 * j + th]
                        for kc in range(16):
                            t.matmul(bank[:, :],
                                     wsl(buf, kc, j * 128, 128),
                                     comb[:, kc * TOWN + th * 512:
                                          kc * TOWN + (th + 1) * 512],
                                     start=(kc == 0), stop=False)
                        t.matmul(bank[:, :],
                                 bias_r[:, (rgp + j) * 128:
                                        (rgp + j + 1) * 128],
                                 onesb[:, 0:512], start=False, stop=True)
                t.drain()
                b.inc(t.nop(), "s_mm")
            blk.tensor(tmm)
            so_mm[rgp] = cnt["s_mm"]

            def vev(v, rgp=rgp, at=so_mm[rgp]):
                v.wait_ge(sem["s_mm"], at)
                if rgp >= 2:
                    nm2, at2 = so_st[rgp - 2]
                    v.wait_ge(sem[nm2], at2)
                for j in range(2):
                    for th in range(2):
                        v.tensor_copy(evf[:, (2 * j + th) * 512:
                                          (2 * j + th + 1) * 512],
                                      pb[2 * j + th][:, :])
                v.drain()
                b.inc(v.nop(), "s_dv")
            blk.vector(vev)
            so_dv[rgp] = cnt["s_dv"]

            st_nm = ("s_st", "s_s2")[(rgp // 2) % 2]

            def sspill(sy, rgp=rgp, st_nm=st_nm, at=so_dv[rgp]):
                sy.wait_ge(sem["s_dv"], at)
                for j in range(2):
                    b.inc(sy.dma_start(
                        out=s_hbm[:, (rgp + j) * TOWN:(rgp + j + 1) * TOWN],
                        in_=evf[:, 2 * j * 512:(2 * j + 2) * 512]),
                        st_nm, 16)
            blk.sync(sspill)
            so_st[rgp] = (st_nm, cnt[st_nm])
        b.barrier()
        b.free_tmp("comb", "gaterep")
        if STOP_AFTER == "p9":
            _tail(b, y_out, evf); b.st.close(); return b

        b.free_tmp("wslab")

        wdTsb = b.sb_tmp("wdTsb", [128, 2 * D], BF16)
        gsb2 = b.sb_tmp("gsb2", [128, 2 * D], BF16)
        gb_dv, gb_st = {}, {}
        for i in range(64):
            buf = i % 2

            def gx(sy, i=i, buf=buf):
                if i > 0:
                    sy.wait_ge(sem["s_st"], gb_st[i - 1])
                else:
                    b.wait_all(sy)
                b.inc(sy.dma_start(out=wdTsb[:, buf * D:(buf + 1) * D],
                                   in_=wdnbf[:, i * 128:(i + 1) * 128],
                                   transpose=True), "s_hld", 16)
                sy.wait_ge(sem["s_hld"], cnt["s_hld"])
            blk.sync(gx)
            at_x = cnt["s_hld"]

            def gm(t, i=i, buf=buf, at=at_x):
                t.wait_ge(sem["s_hld"], at)
                if i >= 2:
                    t.wait_ge(sem["s_dv"], gb_dv[i - 2])
                for q4 in range(4):
                    t.matmul(pb[q4][:, :], h128b[:, :],
                             wdTsb[:, buf * D + q4 * 512:
                                   buf * D + (q4 + 1) * 512],
                             start=True, stop=True)
                t.drain()
                b.inc(t.nop(), "s_mm")
            blk.tensor(gm)
            at_m = cnt["s_mm"]

            def ge(v, i=i, buf=buf, at=at_m):
                v.wait_ge(sem["s_mm"], at)
                for q4 in range(4):
                    v.tensor_copy(gsb2[:, buf * D + q4 * 512:
                                       buf * D + (q4 + 1) * 512],
                                  pb[q4][:, :])
                v.drain()
                b.inc(v.nop(), "s_dv")
            blk.vector(ge)
            gb_dv[i] = cnt["s_dv"]

            def gst(sy, i=i, buf=buf, at=gb_dv[i]):
                sy.wait_ge(sem["s_dv"], at)
                b.inc(sy.dma_start(out=G_hbm[i * 128:(i + 1) * 128, :],
                                   in_=gsb2[:, buf * D:(buf + 1) * D]),
                      "s_st", 16)
            blk.sync(gst)
            gb_st[i] = cnt["s_st"]
        b.barrier()
        b.free_tmp("wdTsb", "gsb2")

        xr2 = b.sb_tmp("xr2", [128, 2 * D], F32)
        xnb2 = b.sb_tmp("xnb2", [128, 2 * D], BF16)
        xnTo = b.sb_tmp("xnTo", [128, 16 * TOWN], BF16)
        hr0 = (cnt["s_hld"], cnt["s_h2"])
        fr_dv0 = cnt["s_dv"]
        XP2 = ("s_hld", "s_h2")

        fr_mm0 = cnt["s_mm"]
        fconv_dv, fevac_dv = {}, {}

        def fxl(sy):
            for tt in range(NOWN):
                if tt >= 2:
                    sy.wait_ge(sem["s_dv"], fconv_dv[tt - 2])
                b.inc(sy.dma_start(out=xr2[:, (tt % 2) * D:(tt % 2 + 1) * D],
                                   in_=x_in[tt * 128:(tt + 1) * 128, :]),
                      XP2[tt % 2], 16)

        def fxv(v):
            def evac(e):
                for c4 in range(4):
                    gidx = e * 4 + c4
                    v.wait_ge(sem["s_mm"], fr_mm0 + gidx + 1)
                    bank = pb[gidx % 2]
                    for j in range(4):
                        c = c4 * 4 + j
                        v.tensor_copy(
                            xnTo[:, c * TOWN + e * 128:
                                 c * TOWN + (e + 1) * 128],
                            bank[:, j * 128:(j + 1) * 128])
                    v.drain()
                    fevac_dv[gidx] = b.inc(v.nop(), "s_dv")
            for tt in range(NOWN):
                v.wait_ge(sem[XP2[tt % 2]], hr0[tt % 2] + 16 * (tt // 2 + 1))
                v.tensor_scalar(xnb2[:, (tt % 2) * D:(tt % 2 + 1) * D],
                                xr2[:, (tt % 2) * D:(tt % 2 + 1) * D],
                                ln_mean[:, tt:tt + 1], ln_rstd[:, tt:tt + 1],
                                ALU.subtract, ALU.mult)
                v.drain()
                fconv_dv[tt] = b.inc(v.nop(), "s_dv")
                if tt >= 1:
                    evac(tt - 1)
            evac(NOWN - 1)
        blk.vector(fxv)
        blk.sync(fxl)

        def fxt(t):
            for tt in range(NOWN):
                t.wait_ge(sem["s_dv"], fconv_dv[tt])
                for c4 in range(4):
                    gidx = tt * 4 + c4
                    if gidx >= 2:
                        t.wait_ge(sem["s_dv"], fevac_dv[gidx - 2])
                    bank = pb[gidx % 2]
                    for j in range(4):
                        c = c4 * 4 + j
                        t.matmul(bank[:, j * 128:(j + 1) * 128],
                                 xnb2[:, (tt % 2) * D + c * 128:
                                      (tt % 2) * D + (c + 1) * 128],
                                 identb[:, :], start=True, stop=True)
                    t.drain()
                    b.inc(t.nop(), "s_mm")
        blk.tensor(fxt)
        b.barrier()
        b.free_tmp("xr2", "xnb2")

        ust = b.sb_tmp("ust", [128, 16 * TOWN], F32)

        def fu(v):
            for c in range(16):
                v.tensor_scalar(xnTo[:, c * TOWN:(c + 1) * TOWN],
                                xnTo[:, c * TOWN:(c + 1) * TOWN],
                                su_c[:, c:c + 1], None, ALU.mult)
            v.drain()
            b.inc(v.nop(), "s_dv")
        blk.vector(fu)
        at_u = cnt["s_dv"]
        fw1_mm0 = cnt["s_mm"]
        fw1_dv0 = cnt["s_dv"]

        def fw1(t, at=at_u):
            t.wait_ge(sem["s_dv"], at)
            for grp in range(8):
                if grp >= 2:
                    t.wait_ge(sem["s_dv"], fw1_dv0 + grp - 1)
                for k4 in range(4):
                    idx = grp * 4 + k4
                    c, th = idx // 2, idx % 2
                    t.matmul(pb[(grp % 2) * 4 + k4][:, :], h128b[:, :],
                             xnTo[:, c * TOWN + th * 512:
                                  c * TOWN + (th + 1) * 512],
                             start=True, stop=True)
                t.drain()
                b.inc(t.nop(), "s_mm")
        blk.tensor(fw1)

        def fw1e(v):
            for grp in range(8):
                v.wait_ge(sem["s_mm"], fw1_mm0 + grp + 1)
                for k4 in range(4):
                    idx = grp * 4 + k4
                    c, th = idx // 2, idx % 2
                    v.tensor_copy(ust[:, c * TOWN + th * 512:
                                      c * TOWN + (th + 1) * 512],
                                  pb[(grp % 2) * 4 + k4][:, :])
                v.drain()
                b.inc(v.nop(), "s_dv")
        blk.vector(fw1e)
        b.barrier()
        b.free_tmp("xnTo")
        x1b = b.sb_tmp("x1b", [128, 16 * TOWN], BF16)

        def fbf(v):
            tmp = evf[:, 0:1024]
            for st_ in range(4):
                stp = 1 << st_
                last = (st_ == 3)
                for base in range(0, 16, 2 * stp):
                    for off in range(stp):
                        a = (base + off) * TOWN
                        bb = (base + off + stp) * TOWN
                        av = ust[:, a:a + TOWN]
                        bv = ust[:, bb:bb + TOWN]
                        v.tensor_tensor(tmp, av, bv, ALU.subtract)
                        v.drain()
                        if last:
                            v.tensor_tensor(x1b[:, a:a + TOWN], av, bv,
                                            ALU.add)
                            v.drain()
                            v.tensor_copy(x1b[:, bb:bb + TOWN], tmp)
                        else:
                            v.tensor_tensor(av, av, bv, ALU.add)
                            v.drain()
                            v.tensor_copy(bv, tmp)
                        v.drain()
            b.inc(v.nop(), "s_dv")
        blk.vector(fbf)
        b.barrier()
        b.free_tmp("ust")
        b.tap("x1", x1b[:, :], [128, 16 * TOWN], BF16)

        hball = b.sb_tmp("hball", [128, 64 * TOWN], BF16)
        wupT = b.sb_tmp("wupT", [128, 2048], BF16)
        wgtT = b.sb_tmp("wgtT", [128, 2048], BF16)
        ff_mm = [cnt["s_mm"]]
        ff_dv = [cnt["s_dv"]]
        for g in range(64):
            buf = g % 2

            def fxu(sy, g=g):
                if g > 0:
                    sy.wait_ge(sem["s_mm"], ff_mm[0])
                for dlt in range(2):
                    dst = wupT.ap().rearrange("p (m l) -> p m l", m=16)[
                        :, :, dlt * 64:(dlt + 1) * 64]
                    b.inc(sy.dma_start(out=dst,
                                       in_=wupbf[2 * g + dlt::128, :],
                                       transpose=True), "s_hld", 16)
                    sy.wait_ge(sem["s_hld"], cnt["s_hld"])
            blk.sync(fxu)
            at_wu = cnt["s_hld"]

            def fmmu(t, g=g, at=at_wu):
                t.wait_ge(sem["s_hld"], at)
                if g == 0:
                    b.wait_all(t, ("s_dv",))
                else:
                    t.wait_ge(sem["s_dv"], ff_dv[0])
                for th in range(2):
                    bank = pb[2 * th]
                    for kc in range(16):
                        t.matmul(bank[:, :],
                                 wupT[:, kc * 128:(kc + 1) * 128],
                                 x1b[:, kc * TOWN + th * 512:
                                     kc * TOWN + (th + 1) * 512],
                                 start=(kc == 0), stop=(kc == 15))
                t.drain()
                b.inc(t.nop(), "s_mm")
            blk.tensor(fmmu)

            def fxg(sy, g=g):
                for dlt in range(2):
                    dst = wgtT.ap().rearrange("p (m l) -> p m l", m=16)[
                        :, :, dlt * 64:(dlt + 1) * 64]
                    b.inc(sy.dma_start(out=dst,
                                       in_=wgtbf[2 * g + dlt::128, :],
                                       transpose=True), "s_hld", 16)
                    sy.wait_ge(sem["s_hld"], cnt["s_hld"])
            blk.sync(fxg)
            at_wg = cnt["s_hld"]

            def fmmg(t, g=g, at=at_wg):
                t.wait_ge(sem["s_hld"], at)
                for th in range(2):
                    bank = pb[2 * th + 1]
                    for kc in range(16):
                        t.matmul(bank[:, :],
                                 wgtT[:, kc * 128:(kc + 1) * 128],
                                 x1b[:, kc * TOWN + th * 512:
                                     kc * TOWN + (th + 1) * 512],
                                 start=(kc == 0), stop=(kc == 15))
                t.drain()
                b.inc(t.nop(), "s_mm")
            blk.tensor(fmmg)
            at_upg = cnt["s_mm"]

            def fsil(s, g=g, at=at_upg):
                s.wait_ge(sem["s_mm"], at)
                for th in range(2):
                    s.activation(out=ebf[:, th * 512:(th + 1) * 512],
                                 in_=pb[2 * th + 1][:, :], func=AF.Silu,
                                 scale=sgate[:, g:g + 1])
                s.drain()
                b.inc(s.nop(), "s_ac")
            blk.scalar(fsil)
            at_sil = cnt["s_ac"]

            def fstt(v, g=g, at=at_sil):
                v.wait_ge(sem["s_ac"], at)
                for th in range(2):
                    v.scalar_tensor_tensor(
                        hball[:, g * TOWN + th * 512: g * TOWN + (th + 1) * 512],
                        pb[2 * th][:, :], sstar[:, g:g + 1],
                        ebf[:, th * 512:(th + 1) * 512], ALU.mult, ALU.mult)
                v.drain()
                b.inc(v.nop(), "s_dv")
            blk.vector(fstt)
            at_h = cnt["s_dv"]

            def fbd(t, g=g, at=at_h):
                t.wait_ge(sem["s_dv"], at)
                for th in range(2):
                    t.matmul(pb[4 + th][:, :], h64bdb[:, :],
                             hball[:, g * TOWN + th * 512:
                                   g * TOWN + (th + 1) * 512],
                             start=True, stop=True)
                t.drain()
                b.inc(t.nop(), "s_mm")
            if not SKIP_BD:
                blk.tensor(fbd)
                ff_mm = [cnt["s_mm"]]

                def fx2(v, g=g, at=ff_mm[0]):
                    v.wait_ge(sem["s_mm"], at)
                    for th in range(2):
                        v.tensor_copy(hball[:, g * TOWN + th * 512:
                                            g * TOWN + (th + 1) * 512],
                                      pb[4 + th][:, :])
                    v.drain()
                    b.inc(v.nop(), "s_dv")
                blk.vector(fx2)
                ff_dv = [cnt["s_dv"]]
            else:
                def fx2n(v, g=g, at=at_h):
                    v.wait_ge(sem["s_dv"], at)
                    b.inc(v.nop(), "s_dv")
                blk.vector(fx2n)
                ff_mm = [cnt["s_mm"]]
                ff_dv = [cnt["s_dv"]]
        b.barrier()
        b.free_tmp("x1b", "wupT", "wgtT")
        b.tap("x2", hball[:, :], [128, 64 * TOWN], BF16)

        gtl = b.sb_tmp("gtl", [128, 2 * 512], BF16)
        gout = b.sb_tmp("gout", [128, 4096], F32)
        dn_mm = [cnt["s_mm"]]
        dn_mm_hist = {0: None, 1: None}
        for hg in range(4):
            for g in range(64):
                buf = g % 2

                def dgl(sy, hg=hg, g=g, buf=buf):
                    if dn_mm_hist[buf] is not None:
                        sy.wait_ge(sem["s_mm"], dn_mm_hist[buf])
                    src = G_hbm.rearrange("(i dd) c -> dd i c", dd=128)[
                        2 * g:2 * g + 2, :, hg * 512:(hg + 1) * 512]
                    b.inc(sy.dma_start(out=gtl[:, buf * 512:(buf + 1) * 512],
                                       in_=src), ("s_hld", "s_h2")[buf], 16)
                blk.sync(dgl)
                g_nm = ("s_hld", "s_h2")[buf]
                at_g = cnt[g_nm]

                def dmm(t, hg=hg, g=g, buf=buf, nm=g_nm, at=at_g):
                    t.wait_ge(sem[nm], at)
                    if hg > 0 and g == 0:
                        t.wait_ge(sem["s_ac"], dn_ac[0])
                    for hc in range(4):
                        for th in range(2):
                            t.matmul(pb[hc * 2 + th][:, :],
                                     gtl[:, buf * 512 + hc * 128:
                                         buf * 512 + (hc + 1) * 128],
                                     hball[:, g * TOWN + th * 512:
                                           g * TOWN + (th + 1) * 512],
                                     start=(g == 0), stop=(g == 63),
                                     skip_group_check=True)
                    t.drain()
                    b.inc(t.nop(), "s_mm")
                blk.tensor(dmm)
                dn_mm = [cnt["s_mm"]]
                dn_mm_hist[buf] = cnt["s_mm"]

            def dsil(s, hg=hg, at=dn_mm[0]):
                s.wait_ge(sem["s_mm"], at)
                if hg > 0:
                    s.wait_ge(sem["s_st"], dn_st[0])
                for hc in range(4):
                    c = hg * 4 + hc
                    for th in range(2):
                        s.activation(out=gout[:, hc * 1024 + th * 512:
                                             hc * 1024 + (th + 1) * 512],
                                     in_=pb[hc * 2 + th][:, :], func=AF.Silu,
                                     scale=wsdp_c[:, c:c + 1])
                s.drain()
                b.inc(s.nop(), "s_ac")
            blk.scalar(dsil)
            dn_ac = [cnt["s_ac"]]

            def dsp(sy, hg=hg, at=dn_ac[0]):
                sy.wait_ge(sem["s_ac"], at)
                b.inc(sy.dma_start(out=g_hbm[:, hg * 4096:(hg + 1) * 4096],
                                   in_=gout[:, :]), "s_st", 16)
            blk.sync(dsp)
            dn_st = [cnt["s_st"]]
        b.barrier()
        b.free_tmp("hball", "gtl", "gout")
        if STOP_AFTER == "p10":
            _tail(b, y_out, evf); b.st.close(); return b

        sfm = b.sb_tmp("sfm", [128, 16 * 128], F32)
        gfm = b.sb_tmp("gfm", [128, 16 * 128], F32)
        stm = b.sb_tmp("stm", [128, D], F32)
        gtm = b.sb_tmp("gtm", [128, D], F32)
        xfin = b.sb_tmp("xfin", [128, D], F32)
        ofin = b.sb_tmp("ofin", [128, D], F32)
        fin_dv = [cnt["s_dv"]]
        fin_st = [cnt["s_st"]]
        for tt in range(NOWN):
            def frl(sy, tt=tt):
                sy.wait_ge(sem["s_dv"], fin_dv[0])
                sy.wait_ge(sem["s_st"], fin_st[0])
                b.inc(sy.dma_start(out=xfin[:, :],
                                   in_=x_in[tt * 128:(tt + 1) * 128, :]),
                      "s_hld", 16)
                b.inc(sy.dma_start(
                    out=sfm.ap().rearrange("p (c w) -> p c w", c=16),
                    in_=s_hbm.rearrange("p (c w) -> p c w", c=16)[
                        :, :, tt * 128:(tt + 1) * 128]), "s_hld", 16)
                b.inc(sy.dma_start(
                    out=gfm.ap().rearrange("p (c w) -> p c w", c=16),
                    in_=g_hbm.rearrange("p (c w) -> p c w", c=16)[
                        :, :, tt * 128:(tt + 1) * 128]), "s_hld", 16)
            blk.sync(frl)
            at_r = cnt["s_hld"]

            def ftr(t, tt=tt, at=at_r):
                t.wait_ge(sem["s_hld"], at)
                if tt > 0:
                    t.wait_ge(sem["s_dv"], fin_dv[0])
                for qq, src in ((0, sfm), (4, gfm)):
                    for c4 in range(4):
                        bank = pb[qq + c4]
                        for j in range(4):
                            c = c4 * 4 + j
                            t.matmul(bank[:, j * 128:(j + 1) * 128],
                                     src[:, c * 128:(c + 1) * 128],
                                     identf[:, :], start=True, stop=True)
                t.drain()
                b.inc(t.nop(), "s_mm")
            blk.tensor(ftr)
            at_t = cnt["s_mm"]

            def fev(v, tt=tt, at=at_t):
                v.wait_ge(sem["s_mm"], at)
                for c4 in range(4):
                    v.tensor_copy(stm[:, c4 * 512:(c4 + 1) * 512],
                                  pb[c4][:, :])
                    v.tensor_copy(gtm[:, c4 * 512:(c4 + 1) * 512],
                                  pb[4 + c4][:, :])
                v.drain()
                v.tensor_tensor(ofin[:, :], stm[:, :], gtm[:, :], ALU.mult)
                v.drain()
                v.tensor_tensor(ofin[:, :], ofin[:, :], xfin[:, :], ALU.add)
                v.drain()
                b.inc(v.nop(), "s_dv")
            blk.vector(fev)
            fin_dv = [cnt["s_dv"]]

            def fst(sy, tt=tt, at=fin_dv[0]):
                sy.wait_ge(sem["s_dv"], at)
                b.inc(sy.dma_start(out=y_out[tt * 128:(tt + 1) * 128, :],
                                   in_=ofin[:, :]), "s_st", 16)
            blk.sync(fst)
            fin_st = [cnt["s_st"]]

        def waitout(sy):
            sy.wait_ge(sem["s_st"], cnt["s_st"])
        blk.sync(waitout)

    b.st.close()
    return b


def _tail(b, y_out, some):
    b.barrier()

    def fin(v):
        v.memset(some[:, 0:D], 0.0)
        v.drain()
        b.inc(v.nop(), "s_dv")
    b.blk.vector(fin)
    at = b.cnt["s_dv"]

    def fin2(sy):
        sy.wait_ge(b.sem["s_dv"], at)
        for tt in range(TOWN // 128):
            b.inc(sy.dma_start(out=y_out[tt * 128:(tt + 1) * 128, :],
                               in_=some[:, 0:D]), "s_st", 16)
        sy.wait_ge(b.sem["s_st"], b.cnt["s_st"])
    b.blk.sync(fin2)


def make_in_map(inp, core):
    bidx, half = core // 2, core % 2
    x = np.ascontiguousarray(inp["L0_attn_x"])[bidx]
    xr = np.ascontiguousarray(
        np.concatenate([x[half * TOWN:], x[:half * TOWN]], axis=0))
    c = _host_consts(half)

    def perm(vec):
        v = np.asarray(vec, dtype=np.float32).reshape(64, 128)
        out = np.empty((128, 64), dtype=np.float32)
        for g in range(64):
            for dlt in range(2):
                out[dlt * 64:(dlt + 1) * 64, g] = v[:, 2 * g + dlt]
        return out

    m = dict(
        x=xr,
        w_qkv=inp["L0_attn_w_qkv"], b_qkv=inp["L0_attn_b_qkv"],
        pool_q=inp["L0_attn_pool_q"],
        w_lm_qkv=inp["L0_attn_w_lm_qkv"], b_lm_qkv=inp["L0_attn_b_lm_qkv"],
        w_lm_out=inp["L0_attn_w_lm_out"], b_lm_out=inp["L0_attn_b_lm_out"],
        w_l2t_q=inp["L0_attn_w_l2t_q"], b_l2t_q=inp["L0_attn_b_l2t_q"],
        w_l2t_out=inp["L0_attn_w_l2t_out"], b_l2t_out=inp["L0_attn_b_l2t_out"],
        w_lm_kv=inp["L0_attn_w_lm_kv"], b_lm_kv=inp["L0_attn_b_lm_kv"],
        w_out=inp["L0_attn_w_out"], b_out=inp["L0_attn_b_out"],
        w_gate=inp["L0_attn_w_gate"], b_gate=inp["L0_attn_b_gate"],
        su_ug=inp["L0_ffn_SU_ug"],
        ws_dp=inp["L0_ffn_Wscale_dp"],
        wu=inp["L0_ffn_W_up"], wg=inp["L0_ffn_W_gate"], wd=inp["L0_ffn_W_down"],
        su_dp_p=perm(inp["L0_ffn_SU_dp"]),
        ws_up_p=perm(inp["L0_ffn_Wscale_ug"][:INTER]),
        ws_gt_p=perm(inp["L0_ffn_Wscale_ug"][INTER:]),
        **c,
    )
    return {k: np.ascontiguousarray(v) for k, v in m.items()}


def kernel(**inp):
    prog = build_program()
    in_maps = [make_in_map(inp, core) for core in range(8)]
    res = run_bass_kernel_spmd(prog.nc, in_maps, core_ids=list(range(8)))
    out = np.empty((B, L, D), dtype=np.float32)
    for core in range(8):
        bidx, half = core // 2, core % 2
        y = np.asarray(res.results[core]["y"])
        out[bidx, half * TOWN:(half + 1) * TOWN, :] = y
    return out
